# version 03
# baseline (speedup 1.0000x reference)
import numpy as np
import ml_dtypes
import concourse.bass as bass
import concourse.mybir as mybir
from concourse.bass_utils import run_bass_kernel_spmd

F32 = mybir.dt.float32
BF16 = mybir.dt.bfloat16
AF = mybir.ActivationFunctionType
ALU = mybir.AluOpType
AX = mybir.AxisListType

D = 1024
H = 16
FC = 8
TP = 128
THJ = 16
TRS = 8
EPS_ST = 1e-5
EPS_RMS = 1e-6
SQS = float(2.0 ** -20)
SQS2 = float(2.0 ** -44)
SQB = float(2.0 ** 22 / 8.0)
SNB = float(2.0 ** 20 / 32.0)

ENGINES = ("sync", "tensor", "vector", "scalar", "gpsimd")


class Em:
    def __init__(self, sem_names):
        self.q = {k: [] for k in ENGINES}
        self.c = {k: 0 for k in sem_names}
        self.serialize = False

    def wait(self, eng, sem, val=None):
        if val is None:
            val = self.c[sem]
        if val > 0:
            self.q[eng].append(("w", sem, val))

    def op(self, eng, fn, inc=None):
        if self.serialize:
            for sname in self.c:
                self.wait(eng, sname)
        incs = ()
        if inc is not None:
            incs = ((inc, 1),)
            self.c[inc] += 1
        self.q[eng].append(("op", fn, incs))

    def dma(self, eng, out, in_, sem):
        if eng == "sync":
            self.wait("sync", sem)
        def fn(e, out=out, in_=in_):
            return e.dma_start(out=out, in_=in_)
        self.q[eng].append(("op", fn, ((sem, 16),)))
        self.c[sem] += 16


def build_nc(L=2048):
    NCH = L // TP
    NSUB = TP // THJ
    groups = [list(range(g, min(g + 2, NCH))) for g in range(0, NCH, 2)]
    nc = bass.Bass("TRN2", target_bir_lowering=False,
                   detect_race_conditions=False)

    dx = nc.dram_tensor("x", [L, D], F32, kind="ExternalInput").ap()
    dwproj = nc.dram_tensor("wproj", [128, 7 * 8192], F32, kind="ExternalInput").ap()
    dwo = nc.dram_tensor("wo", [128, 8192], F32, kind="ExternalInput").ap()
    dwffn = nc.dram_tensor("wffn", [128, 41024], F32, kind="ExternalInput").ap()
    dw2 = nc.dram_tensor("w2", [8, 128, 8192], F32, kind="ExternalInput").ap()
    dlnw = nc.dram_tensor("lnw_rep", [128, D], F32, kind="ExternalInput").ap()
    dgb = nc.dram_tensor("gb_rep", [128, 8], F32, kind="ExternalInput").ap()
    dcw = nc.dram_tensor("cw_f", [128, 32], F32, kind="ExternalInput").ap()
    dbdp = nc.dram_tensor("bdp", [128, 8], F32, kind="ExternalInput").ap()
    dsel = nc.dram_tensor("sel", [16, 128], F32, kind="ExternalInput").ap()
    dident = nc.dram_tensor("ident", [128, 128], F32, kind="ExternalInput").ap()
    dhalf = nc.dram_tensor("halfones", [128, 2], F32, kind="ExternalInput").ap()
    dout = nc.dram_tensor("out", [L, D], F32, kind="ExternalOutput").ap()
    dscr_s = nc.dram_tensor("scr_s", [L, D], F32).ap()
    dscr_xnb = nc.dram_tensor("scr_xnb", [NCH, 128, D], BF16).ap()
    dscr_cv = nc.dram_tensor("scr_cv", [3, D, TP], F32).ap()

    sem_names = ["dx", "dg", "tt", "vv", "aa"]

    from contextlib import ExitStack
    with ExitStack() as _st:
        wpool = _st.enter_context(nc.sbuf_tensor("wpool", [128, 65536], BF16))
        xn_tok = _st.enter_context(nc.sbuf_tensor("xn_tok", [128, D], F32))
        xn_fm = _st.enter_context(nc.sbuf_tensor("xn_fm", [128, FC * (TP + 3)], F32))
        kkfm = _st.enter_context(nc.sbuf_tensor("kkfm", [128, D], F32))
        cfm = _st.enter_context(nc.sbuf_tensor("cfm", [128, D], F32))
        vfm = _st.enter_context(nc.sbuf_tensor("vfm", [128, D], F32))
        winv = _st.enter_context(nc.sbuf_tensor("winv", [128, D], F32))
        kwb = _st.enter_context(nc.sbuf_tensor("kwb", [128, D], F32))
        tmpa = _st.enter_context(nc.sbuf_tensor("tmpa", [128, D], F32))
        rsbuf = _st.enter_context(nc.sbuf_tensor("rsbuf", [128, FC * TP], F32))
        lambuf = _st.enter_context(nc.sbuf_tensor("lambuf", [128, FC * TP], F32))
        zst = _st.enter_context(nc.sbuf_tensor("zst", [128, 512], F32))
        sm = _st.enter_context(nc.sbuf_tensor("sm", [128, 16], F32))
        sm2 = _st.enter_context(nc.sbuf_tensor("sm2", [128, 16], F32))
        sm3 = _st.enter_context(nc.sbuf_tensor("sm3", [128, 32], F32))
        sctmp = _st.enter_context(nc.sbuf_tensor("sctmp", [128, 64], F32))
        cwf = _st.enter_context(nc.sbuf_tensor("cwf", [128, 32], F32))
        bdps = _st.enter_context(nc.sbuf_tensor("bdps", [128, 8], F32))
        lnw = _st.enter_context(nc.sbuf_tensor("lnw", [128, D], BF16))
        gbr = _st.enter_context(nc.sbuf_tensor("gbr", [128, 8], F32))
        self32 = _st.enter_context(nc.sbuf_tensor("self32", [16, 128], F32))
        ident32 = _st.enter_context(nc.sbuf_tensor("ident32", [128, 128], F32))
        half2 = _st.enter_context(nc.sbuf_tensor("half2", [128, 2], F32))
        upool = _st.enter_context(nc.sbuf_tensor("upool", [128, 2 * 194 * THJ], F32))
        xnfmb = _st.enter_context(nc.sbuf_tensor("xnfmb", [128, D], BF16))
        up = upool[:]
        HJC = 194 * THJ
        hjbufs_ap = (up[0:16, 0:HJC], up[0:16, HJC:2 * HJC])
        hidq = up[:, 0:1024].bitcast(BF16)
        accq = up[:, 1024:3072]
        swb = up[:, 3072:3584].bitcast(BF16)
        satmp = up[:, 3584:4608]
        wdense = up[:, 4608:4608 + NCH * 8]
        tmpb = vfm
        gob = cfm[:, 0:512].bitcast(BF16)
        ebd = bdps
        pm = _st.enter_context(nc.psum_tensor("pm", [128, 512], F32))
        pm2 = _st.enter_context(nc.psum_tensor("pm2", [128, 512], F32))
        t1b = _st.enter_context(nc.psum_tensor("t1b", [128, 512], F32))
        prep0 = _st.enter_context(nc.psum_tensor("prep0", [128, 512], F32))
        prep1 = _st.enter_context(nc.psum_tensor("prep1", [128, 512], F32))
        pbig = _st.enter_context(nc.psum_tensor("pbig", [128, 1024], F32))
        paux = _st.enter_context(nc.psum_tensor("paux", [128, 512], F32))
        s_dx = _st.enter_context(nc.semaphore("s_dx"))
        s_dg = _st.enter_context(nc.semaphore("s_dg"))
        s_tt = _st.enter_context(nc.semaphore("s_tt"))
        s_vv = _st.enter_context(nc.semaphore("s_vv"))
        s_aa = _st.enter_context(nc.semaphore("s_aa"))
        block = _st.enter_context(nc.Block())
        sems = {"dx": s_dx, "dg": s_dg, "tt": s_tt, "vv": s_vv, "aa": s_aa}
        em = Em(sem_names)

        zv = zst[:].rearrange("p (f j) -> p f j", f=FC)
        rsb = rsbuf[:].rearrange("p (t f) -> p t f", f=FC)
        lmb = lambuf[:].rearrange("p (t f) -> p t f", f=FC)
        xnfm = xn_fm[:].rearrange("p (f t) -> p f t", f=FC)
        winvv = winv[:].rearrange("p (f t) -> p f t", f=FC)
        kwv = kwb[:].rearrange("p (f t) -> p f t", f=FC)

        rs_trk = sm[:, 0:8]
        dneg = sm[:, 8:16]
        p_sc = sm2[:, 0:8]
        q_sc = sm2[:, 8:16]
        mue = sm3[:, 0:8]
        sqv = sm3[:, 8:16]
        linv = sm3[:, 16:24]
        rs_tmp = sm3[:, 24:32]

        def bc8(ap2d, n=64):
            return ap2d.unsqueeze(2).broadcast_to([128, 8, n])

        def bc64(ap2d):
            return ap2d.unsqueeze(1).broadcast_to([128, 8, 64])

        for dst, src in [(cwf[:], dcw), (bdps[:], dbdp),
                         (gbr[:], dgb), (self32[:], dsel), (ident32[:], dident),
                         (half2[:], dhalf)]:
            em.dma("sync", dst, src, "dx")
        const_ld = em.c["dx"]
        em.dma("gpsimd", lnw[:], dlnw, "dg")
        for i in range(7):
            em.dma("gpsimd", wpool[:, i * 8192:(i + 1) * 8192],
                   dwproj[:, i * 8192:(i + 1) * 8192], "dg")
        em.dma("gpsimd", wpool[:, 7 * 8192:8 * 8192], dwo, "dg")
        attn_w_ld = em.c["dg"]

        em.wait("scalar", "dx", const_ld)
        em.op("scalar", lambda e: e.activation(bdps[:], bdps[:], AF.Exp,
                                               scale=-1.0), inc="aa")
        a_ebd = em.c["aa"]
        em.op("vector", lambda e: e.memset(zst[:], 0.0), inc="vv")
        em.op("vector", lambda e: e.memset(rs_trk, 0.0), inc="vv")
        em.wait("vector", "dx", const_ld)
        em.wait("vector", "dg", attn_w_ld)
        em.wait("vector", "aa", a_ebd)
        em.wait("tensor", "dx", const_ld)
        em.wait("tensor", "dg", attn_w_ld)

        wslice = {}
        for i, nm in enumerate(["r", "k", "g", "wt", "v", "a", "kk"]):
            wslice[nm] = wpool[:, i * 8192:(i + 1) * 8192]
        wslice["wo"] = wpool[:, 7 * 8192:8 * 8192]

        rep_free = {0: 0, 1: 0}

        for ci in range(NCH):
            em.wait("sync", "vv")
            em.dma("sync", tmpb[:], dx[ci * TP:(ci + 1) * TP, :], "dx")
            xld = em.c["dx"]
            em.wait("scalar", "vv")
            em.wait("scalar", "dx", xld)
            em.op("scalar", lambda e: e.activation(tmpa[:], tmpb[:], AF.Square,
                                                   accum_out=sm3[:, 0:1]), inc="aa")
            em.op("scalar", lambda e: e.activation(xn_tok[:], tmpb[:], AF.Identity,
                                                   accum_out=sm3[:, 1:2]), inc="aa")
            aln = em.c["aa"]
            em.wait("vector", "aa", aln)
            em.op("vector", lambda e: e.tensor_scalar(
                sm3[:, 2:3], sm3[:, 1:2], float(1.0 / D), None, op0=ALU.mult), inc="vv")
            em.op("vector", lambda e: e.tensor_scalar(
                sm3[:, 3:4], sm3[:, 0:1], float(1.0 / D), None, op0=ALU.mult), inc="vv")
            em.op("vector", lambda e: e.tensor_tensor(
                sm3[:, 4:5], sm3[:, 2:3], sm3[:, 2:3], op=ALU.mult), inc="vv")
            em.wait("vector", "vv")
            em.op("vector", lambda e: e.tensor_tensor(
                sm3[:, 5:6], sm3[:, 3:4], sm3[:, 4:5], op=ALU.subtract), inc="vv")
            em.wait("vector", "vv")
            em.op("vector", lambda e: e.tensor_scalar(
                sm3[:, 5:6], sm3[:, 5:6], 1e-5, None, op0=ALU.add), inc="vv")
            vvar = em.c["vv"]
            em.wait("scalar", "vv", vvar)
            em.op("scalar", lambda e: e.activation(sm3[:, 6:7], sm3[:, 5:6],
                                                   AF.Sqrt), inc="aa")
            asd = em.c["aa"]
            em.wait("vector", "aa", asd)
            em.op("vector", lambda e: e.reciprocal(sm3[:, 7:8], sm3[:, 6:7]), inc="vv")
            em.wait("vector", "vv")
            em.op("vector", lambda e: e.tensor_scalar(
                xn_tok[:], tmpb[:], sm3[:, 2:3], sm3[:, 7:8],
                op0=ALU.subtract, op1=ALU.mult), inc="vv")
            vxn = em.c["vv"]

            em.wait("tensor", "vv", vxn)
            for fc in range(FC):
                em.op("tensor", lambda e, fc=fc: e.transpose(
                    pbig[:, fc * 128:(fc + 1) * 128],
                    xn_tok[:, fc * 128:(fc + 1) * 128], ident32[:]),
                    inc=("tt" if fc == FC - 1 else None))
            ttr = em.c["tt"]
            em.wait("vector", "tt", ttr)
            if ci == 0:
                em.op("vector", lambda e: e.memset(xn_fm[:], 0.0), inc="vv")
            else:
                em.op("vector", lambda e: e.tensor_copy(
                    xnfm[:, :, 0:3], xnfm[:, :, TP:TP + 3]), inc="vv")
            for hb in range(2):
                em.op("vector", lambda e, hb=hb: e.tensor_copy(
                    xnfm[:, hb * 4:hb * 4 + 4, 3:TP + 3],
                    pbig[:, hb * 512:(hb + 1) * 512].rearrange(
                        "p (f t) -> p f t", f=4)), inc="vv")
            vxnfm = em.c["vv"]
            em.wait("scalar", "tt", ttr)
            em.wait("scalar", "dx")
            em.op("vector", lambda e: e.tensor_copy(
                xnfmb[:, 0:512], pbig[:, 0:512]), inc="aa")
            em.op("vector", lambda e: e.tensor_copy(
                xnfmb[:, 512:1024], pbig[:, 512:1024]), inc="aa")
            axnb = em.c["aa"]
            em.wait("sync", "aa", axnb)
            em.dma("sync", dscr_xnb[ci], xnfmb[:], "dx")

            cw3 = cwf[:].rearrange("p (f k) -> p f k", f=FC)
            em.op("vector", lambda e: e.tensor_tensor(
                cfm[:].rearrange("p (f t) -> p f t", f=FC),
                xnfm[:, :, 0:TP],
                cw3[:, :, 0:1].broadcast_to([128, FC, TP]), op=ALU.mult), inc="vv")
            for j in range(1, 4):
                em.op("vector", lambda e, j=j: e.tensor_tensor(
                    tmpa[:].rearrange("p (f t) -> p f t", f=FC),
                    xnfm[:, :, j:TP + j],
                    cw3[:, :, j:j + 1].broadcast_to([128, FC, TP]),
                    op=ALU.mult), inc="vv")
                em.op("vector", lambda e: e.tensor_tensor(
                    cfm[:], cfm[:], tmpa[:], op=ALU.add), inc="vv")
            vxc = em.c["vv"]
            em.wait("tensor", "vv", vxc)
            for fc in range(FC):
                em.op("tensor", lambda e, fc=fc: e.transpose(
                    pbig[:, fc * 128:(fc + 1) * 128],
                    cfm[:, fc * 128:(fc + 1) * 128], ident32[:]),
                    inc=("tt" if fc == FC - 1 else None))
            ttr2 = em.c["tt"]
            em.wait("vector", "tt", ttr2)
            for hb in range(2):
                em.op("vector", lambda e, hb=hb: e.scalar_tensor_tensor(
                    xn_tok[:, hb * 512:(hb + 1) * 512],
                    pbig[:, hb * 512:(hb + 1) * 512], 0.1,
                    xn_tok[:, hb * 512:(hb + 1) * 512],
                    op0=ALU.mult, op1=ALU.add), inc="vv")
            vxg = em.c["vv"]

            def proj(nm):
                ws = wslice[nm]
                for fc in range(FC):
                    for dc in range(FC):
                        lhsT = ws[:, dc * 1024 + fc * 128:dc * 1024 + (fc + 1) * 128]
                        rhs = xnfmb[:].rearrange("p (f t) -> p f t", f=FC)[:, dc, :]
                        em.op("tensor", lambda e, lhsT=lhsT, rhs=rhs, fc=fc,
                              st=(dc == 0), sp=(dc == FC - 1):
                              e.matmul(pbig[:, fc * 128:(fc + 1) * 128], lhsT,
                                       rhs, start=st, stop=sp),
                              inc=("tt" if dc == FC - 1 and fc == FC - 1 else None))
                return em.c["tt"]

            em.wait("tensor", "aa", axnb)
            em.wait("tensor", "vv", vxg)
            twt = proj("wt")
            em.wait("scalar", "vv")
            em.wait("scalar", "dx")
            em.wait("scalar", "tt", twt)
            em.op("scalar", lambda e: e.activation(
                tmpa[:, 0:512], pbig[:, 0:512], AF.Exp), inc="aa")
            em.op("scalar", lambda e: e.activation(
                tmpa[:, 512:1024], pbig[:, 512:1024], AF.Exp), inc="aa")
            em.op("scalar", lambda e: e.activation(tmpb[:], tmpa[:], AF.Exp,
                                                   scale=-1.0), inc="aa")
            awt = em.c["aa"]
            em.wait("vector", "aa", awt)
            em.op("vector", lambda e: e.tensor_tensor(
                tmpa[:].rearrange("p (f t) -> p f t", f=FC),
                tmpb[:].rearrange("p (f t) -> p f t", f=FC),
                bc8(ebd[:], TP), op=ALU.mult), inc="vv")
            em.op("vector", lambda e: e.tensor_scalar(
                tmpa[:], tmpa[:], 0.01, 0.99, op0=ALU.max, op1=ALU.min), inc="vv")
            em.op("vector", lambda e: e.reciprocal(winv[:], tmpa[:]), inc="vv")
            vwt = em.c["vv"]

            em.wait("tensor", "vv", vwt)
            tk = proj("k")
            em.wait("vector", "tt", tk)
            for hb in range(2):
                em.op("vector", lambda e, hb=hb: e.tensor_tensor(
                    kwb[:, hb * 512:(hb + 1) * 512],
                    pbig[:, hb * 512:(hb + 1) * 512],
                    winv[:, hb * 512:(hb + 1) * 512], op=ALU.mult), inc="vv")
            vk = em.c["vv"]

            em.wait("tensor", "vv", vk)
            tkk = proj("kk")
            em.wait("vector", "tt", tkk)
            for hb in range(2):
                em.op("vector", lambda e, hb=hb: e.tensor_copy(
                    kkfm[:, hb * 512:(hb + 1) * 512],
                    pbig[:, hb * 512:(hb + 1) * 512]), inc="vv")
            vkk = em.c["vv"]

            em.wait("tensor", "vv", vkk)
            ta = proj("a")
            em.wait("vector", "tt", ta)
            for hb in range(2):
                em.op("vector", lambda e, hb=hb: e.tensor_tensor(
                    cfm[:, hb * 512:(hb + 1) * 512],
                    kkfm[:, hb * 512:(hb + 1) * 512],
                    pbig[:, hb * 512:(hb + 1) * 512], op=ALU.mult), inc="vv")
            vc = em.c["vv"]

            em.wait("tensor", "vv", vc)
            tv = proj("v")
            em.wait("vector", "tt", tv)
            for hb in range(2):
                em.op("vector", lambda e, hb=hb: e.tensor_copy(
                    vfm[:, hb * 512:(hb + 1) * 512],
                    pbig[:, hb * 512:(hb + 1) * 512]), inc="vv")
            vv_ = em.c["vv"]

            em.wait("tensor", "vv", vv_)
            for t_i, src in enumerate((cfm, vfm)):
                for fc in range(FC):
                    em.op("tensor", lambda e, src=src, fc=fc, t_i=t_i: e.matmul(
                        paux[:, t_i * 16 + fc * 2:t_i * 16 + fc * 2 + 2],
                        src[:, fc * 128:(fc + 1) * 128], half2[:],
                        start=True, stop=True),
                        inc=("tt" if t_i == 1 and fc == FC - 1 else None))
            tsc = em.c["tt"]
            em.wait("vector", "tt", tsc)
            if ci == 0:
                em.op("vector", lambda e: e.memset(sctmp[:], 0.0), inc="vv")
            em.op("vector", lambda e: e.tensor_copy(sctmp[:, 0:16], paux[:, 0:16]),
                  inc="vv")
            em.op("vector", lambda e: e.tensor_copy(sctmp[:, 32:48], paux[:, 16:32]),
                  inc="vv")
            vsc = em.c["vv"]
            em.wait("tensor", "vv", vsc)
            em.op("tensor", lambda e: e.transpose(
                paux[0:64, 0:128], sctmp[:], ident32[:]), inc="tt")
            tsc2 = em.c["tt"]

            em.wait("sync", "vv", vv_)
            em.dma("sync", dscr_cv[0].rearrange("(fc p) t -> p fc t", p=128),
                   kkfm[:].rearrange("p (fc t) -> p fc t", fc=FC), "dx")
            em.dma("sync", dscr_cv[1].rearrange("(fc p) t -> p fc t", p=128),
                   cfm[:].rearrange("p (fc t) -> p fc t", fc=FC), "dx")
            em.dma("sync", dscr_cv[2].rearrange("(fc p) t -> p fc t", p=128),
                   vfm[:].rearrange("p (fc t) -> p fc t", fc=FC), "dx")
            cvout = em.c["dx"]
            rg = kkfm

            em.wait("tensor", "vv", vv_)
            tr = proj("r")
            em.wait("scalar", "tt", tr)
            em.wait("scalar", "dx", cvout)
            em.op("scalar", lambda e, rg=rg: e.activation(
                rg[:, 0:512], pbig[:, 0:512], AF.Sigmoid), inc="aa")
            em.op("scalar", lambda e, rg=rg: e.activation(
                rg[:, 512:1024], pbig[:, 512:1024], AF.Sigmoid), inc="aa")
            ar = em.c["aa"]

            em.wait("tensor", "aa", ar)
            tg = proj("g")
            em.wait("scalar", "tt", tg)
            em.wait("scalar", "vv")
            em.op("scalar", lambda e: e.activation(
                tmpb[:, 0:512], pbig[:, 0:512], AF.Sigmoid), inc="aa")
            em.op("scalar", lambda e: e.activation(
                tmpb[:, 512:1024], pbig[:, 512:1024], AF.Sigmoid), inc="aa")
            ag = em.c["aa"]
            em.wait("vector", "aa", ag)
            em.op("vector", lambda e, rg=rg: e.tensor_tensor(
                rg[:], rg[:], tmpb[:], op=ALU.mult), inc="vv")
            vrg = em.c["vv"]

            hjbufs = hjbufs_ap
            hj_dx = {}
            hj_vv = {}
            sel_done = {}

            def emit_hj_load(sub):
                hj = hjbufs[sub % 2]
                hjv = hj.rearrange("h (c t) -> h c t", c=194)
                em.wait("sync", "dx", cvout)
                if sub >= 2:
                    em.wait("sync", "tt", sel_done[sub - 2])
                for vi in range(3):
                    src = dscr_cv[vi].rearrange("(h j) t -> h j t", h=H)[
                        :, :, sub * THJ:(sub + 1) * THJ]
                    em.dma("sync", hjv[:, vi * 64:(vi + 1) * 64, :], src, "dx")
                hj_dx[sub] = em.c["dx"]
                em.wait("vector", "tt", tsc2)
                em.op("vector", lambda e, hjv=hjv, sub=sub: e.tensor_copy(
                    hjv[:, 192:193, :],
                    paux[0:16, sub * THJ:(sub + 1) * THJ].unsqueeze(1)), inc="vv")
                em.op("vector", lambda e, hjv=hjv, sub=sub: e.tensor_copy(
                    hjv[:, 193:194, :],
                    paux[32:48, sub * THJ:(sub + 1) * THJ].unsqueeze(1)), inc="vv")
                hj_vv[sub] = em.c["vv"]

            emit_hj_load(0)
            if NSUB > 1:
                emit_hj_load(1)

            for sub in range(NSUB):
                hj = hjbufs[sub % 2]
                hjv = hj.rearrange("h (c t) -> h c t", c=194)
                for ts in range(THJ):
                    tt_ = sub * THJ + ts
                    gt = ci * TP + tt_
                    par = gt % 2
                    prep = (prep0, prep1)[par]
                    after_rs = (gt % TRS == 0)

                    em.wait("tensor", "dx", hj_dx[sub])
                    em.wait("tensor", "vv", hj_vv[sub])
                    if rep_free[par] > 0:
                        em.wait("tensor", "vv", rep_free[par])
                    em.op("tensor", lambda e, prep=prep, hjv=hjv, ts=ts: e.matmul(
                        prep[:, 0:194], self32[:], hjv[:, :, ts],
                        start=True, stop=True), inc="tt")
                    trep = em.c["tt"]

                    wsl = winvv[:, :, tt_]
                    kwsl = kwv[:, :, tt_]

                    em.wait("vector", "tt", trep)
                    em.op("vector", lambda e, prep=prep: e.tensor_tensor(
                        pm[:].rearrange("p (f j) -> p f j", f=FC), zv,
                        bc64(prep[:, 0:64]), op=ALU.mult), inc="vv")
                    em.wait("vector", "vv")
                    em.op("vector", lambda e: e.tensor_reduce(
                        dneg, pm[:].rearrange("p (f j) -> p f j", f=FC),
                        axis=AX.X, op=ALU.add, negate=True), inc="vv")
                    em.wait("vector", "vv")
                    em.op("vector", lambda e, wsl=wsl: e.tensor_tensor(
                        p_sc, dneg, wsl, op=ALU.mult), inc="vv")
                    em.wait("vector", "vv")
                    em.op("vector", lambda e, prep=prep: e.tensor_tensor(
                        t1b[:].rearrange("p (f j) -> p f j", f=FC),
                        bc64(prep[:, 64:128]), bc8(p_sc), op=ALU.mult), inc="vv")
                    if after_rs:
                        qop = kwsl
                    else:
                        lprev = lmb[:, tt_ - 1, :]
                        em.op("vector", lambda e, lprev=lprev, kwsl=kwsl:
                              e.tensor_tensor(q_sc, lprev, kwsl, op=ALU.mult),
                              inc="vv")
                        em.wait("vector", "vv")
                        qop = q_sc
                    em.wait("vector", "vv")
                    em.op("vector", lambda e: e.tensor_tensor(
                        zst[:], zst[:], t1b[:], op=ALU.add), inc="vv")
                    em.op("vector", lambda e, prep=prep, qop=qop: e.tensor_tensor(
                        t1b[:].rearrange("p (f j) -> p f j", f=FC),
                        bc64(prep[:, 128:192]), bc8(qop), op=ALU.mult), inc="vv")
                    rs_prev = rs_trk if after_rs else rsb[:, tt_ - 1, :]
                    em.op("vector", lambda e, prep=prep, rs_prev=rs_prev:
                          e.scalar_tensor_tensor(rs_tmp, p_sc, prep[:, 192:193],
                                                 rs_prev, op0=ALU.mult,
                                                 op1=ALU.add), inc="vv")
                    em.wait("vector", "vv")
                    em.op("vector", lambda e, prep=prep, qop=qop, tt_=tt_:
                          e.scalar_tensor_tensor(rsb[:, tt_, :], qop,
                                                 prep[:, 193:194], rs_tmp,
                                                 op0=ALU.mult, op1=ALU.add),
                          inc="vv")
                    rep_free[par] = em.c["vv"]
                    em.wait("vector", "vv")
                    em.op("vector", lambda e: e.tensor_tensor(
                        zst[:], zst[:], t1b[:], op=ALU.add), inc="vv")
                    em.op("vector", lambda e: e.scalar_tensor_tensor(
                        pm2[:], zst[:], SQS2, zst[:], op0=ALU.mult, op1=ALU.mult),
                        inc="vv")
                    em.wait("vector", "vv")
                    em.op("vector", lambda e: e.tensor_reduce(
                        dneg, pm2[:].rearrange("p (f j) -> p f j", f=FC),
                        axis=AX.X, op=ALU.add), inc="vv")
                    vnsq = em.c["vv"]
                    if after_rs:
                        em.op("vector", lambda e, wsl=wsl: e.tensor_scalar(
                            mue, wsl, EPS_ST, None, op0=ALU.mult), inc="vv")
                    else:
                        lprev = lmb[:, tt_ - 1, :]
                        em.op("vector", lambda e, lprev=lprev, wsl=wsl:
                              e.scalar_tensor_tensor(mue, lprev, EPS_ST, wsl,
                                                     op0=ALU.mult, op1=ALU.mult),
                              inc="vv")
                    em.wait("scalar", "vv", vnsq)
                    em.op("scalar", lambda e: e.activation(sqv, dneg, AF.Sqrt),
                          inc="aa")
                    asv = em.c["aa"]
                    em.wait("vector", "vv")
                    em.wait("vector", "aa", asv)
                    em.op("vector", lambda e, tt_=tt_: e.scalar_tensor_tensor(
                        lmb[:, tt_, :], sqv, SQB, mue, op0=ALU.mult, op1=ALU.add),
                        inc="vv")
                    if gt % TRS == TRS - 1:
                        em.wait("vector", "vv")
                        em.op("vector", lambda e, tt_=tt_: e.reciprocal(
                            linv, lmb[:, tt_, :]), inc="vv")
                        em.wait("vector", "vv")
                        em.op("vector", lambda e: e.tensor_tensor(
                            zv, zv, bc8(linv), op=ALU.mult), inc="vv")
                        em.op("vector", lambda e, tt_=tt_: e.tensor_tensor(
                            rs_trk, rsb[:, tt_, :], linv, op=ALU.mult), inc="vv")
                sel_done[sub] = em.c["tt"]
                if sub + 2 < NSUB:
                    emit_hj_load(sub + 2)

            em.wait("vector", "vv")
            em.wait("vector", "dx", cvout)
            em.op("vector", lambda e: e.reciprocal(tmpa[:], lambuf[:]), inc="vv")
            em.op("vector", lambda e: e.tensor_tensor(
                tmpb[:], rsbuf[:], tmpa[:], op=ALU.mult), inc="vv")
            em.op("vector", lambda e, rg=rg: e.tensor_tensor(
                gob[:, 0:FC * TP].rearrange("p (f t) -> p f t", f=FC),
                tmpb[:].rearrange("p (t f) -> p f t", f=FC),
                rg[:].rearrange("p (f t) -> p f t", f=FC), op=ALU.mult), inc="vv")
            vgo = em.c["vv"]

            em.wait("tensor", "vv", vgo)
            gov = gob[:, 0:FC * TP].rearrange("p (f t) -> p f t", f=FC)
            for half in range(2):
                for fcp in range(FC):
                    em.op("tensor", lambda e, half=half, fcp=fcp,
                          st=(fcp == 0), sp=(fcp == FC - 1): e.matmul(
                              pbig[:, half * 512:(half + 1) * 512],
                              gov[:, fcp, :],
                              wslice["wo"][:, fcp * 1024 + half * 512:
                                           fcp * 1024 + (half + 1) * 512],
                              start=st, stop=sp),
                          inc=("tt" if half == 1 and fcp == FC - 1 else None))
            two = em.c["tt"]
            em.wait("vector", "tt", two)
            for hb in range(2):
                em.op("vector", lambda e, hb=hb: e.tensor_copy(
                    tmpb[:, hb * 512:(hb + 1) * 512],
                    pbig[:, hb * 512:(hb + 1) * 512]), inc="vv")
            vwoe = em.c["vv"]
            em.wait("scalar", "vv", vwoe)
            em.op("scalar", lambda e: e.activation(
                tmpa[:], tmpb[:], AF.Square, scale=SQS,
                accum_out=sm3[:, 0:1]), inc="aa")
            em.op("scalar", lambda e: e.activation(sm3[:, 1:2], sm3[:, 0:1],
                                                   AF.Sqrt), inc="aa")
            aso = em.c["aa"]
            em.wait("vector", "aa", aso)
            em.op("vector", lambda e: e.tensor_scalar(
                sm3[:, 2:3], sm3[:, 1:2], SNB, EPS_RMS,
                op0=ALU.mult, op1=ALU.add), inc="vv")
            em.wait("vector", "vv")
            em.op("vector", lambda e: e.reciprocal(sm3[:, 3:4], sm3[:, 2:3]),
                  inc="vv")
            em.wait("vector", "vv")
            em.op("vector", lambda e: e.scalar_tensor_tensor(
                tmpa[:], tmpb[:], sm3[:, 3:4], lnw[:], op0=ALU.mult,
                op1=ALU.mult), inc="vv")
            em.op("vector", lambda e: e.tensor_tensor(
                xn_tok[:], tmpa[:], xn_tok[:], op=ALU.add), inc="vv")
            vs = em.c["vv"]
            em.wait("sync", "vv", vs)
            em.dma("sync", dscr_s[ci * TP:(ci + 1) * TP, :], xn_tok[:], "dx")

        em.wait("gpsimd", "tt", em.c["tt"])
        em.wait("gpsimd", "aa", em.c["aa"])
        em.dma("gpsimd", wpool[:, 0:16384], dwffn[:, 0:16384], "dg")
        em.dma("gpsimd", wpool[:, 16384:32768], dwffn[:, 16384:32768], "dg")
        em.dma("gpsimd", wpool[:, 32768:40960], dwffn[:, 32768:40960], "dg")
        em.dma("gpsimd", wpool[:, 40960:41024], dwffn[:, 40960:41024], "dg")
        ffnw = em.c["dg"]
        em.wait("tensor", "dg", ffnw)
        wgate_sl = wpool[:, 40960:41024]

        lg = sm2[:, 0:8]
        ex = sm2[:, 8:16]
        mask1 = sctmp[:, 0:8]
        masked = sctmp[:, 8:16]
        keep = sctmp[:, 16:24]

        for grp in groups:
            g0 = grp[0]
            for ci in grp:
                lci = ci - g0
                em.wait("sync", "tt")
                em.dma("sync", xnfmb[:], dscr_xnb[ci], "dx")
                xbl = em.c["dx"]
                em.wait("tensor", "dx", xbl)
                xnv = xnfmb[:].rearrange("p (f t) -> p f t", f=FC)
                em.wait("tensor", "vv")
                for dc in range(FC):
                    em.op("tensor", lambda e, dc=dc, st=(dc == 0),
                          sp=(dc == FC - 1): e.matmul(
                              paux[:, 0:8], xnv[:, dc, :],
                              wgate_sl[:, dc * 8:(dc + 1) * 8], start=st, stop=sp),
                          inc=("tt" if dc == FC - 1 else None))
                tgt = em.c["tt"]
                em.wait("vector", "tt", tgt)
                em.op("vector", lambda e: e.tensor_tensor(
                    lg, paux[:, 0:8], gbr[:], op=ALU.add), inc="vv")
                em.wait("vector", "vv")
                em.op("vector", lambda e: e.tensor_reduce(
                    sm3[:, 0:1], lg, axis=AX.X, op=ALU.max, negate=True), inc="vv")
                vlg = em.c["vv"]
                em.wait("scalar", "vv", vlg)
                em.op("scalar", lambda e: e.activation(
                    ex, lg, AF.Exp, bias=sm3[:, 0:1]), inc="aa")
                age = em.c["aa"]
                em.wait("vector", "aa", age)
                em.op("vector", lambda e: e.tensor_reduce(
                    sm3[:, 1:2], ex, axis=AX.X, op=ALU.add), inc="vv")
                em.wait("vector", "vv")
                em.op("vector", lambda e: e.reciprocal(sm3[:, 2:3], sm3[:, 1:2]),
                      inc="vv")
                em.op("vector", lambda e: e.tensor_reduce(
                    sm3[:, 3:4], lg, axis=AX.X, op=ALU.max), inc="vv")
                em.wait("vector", "vv")
                em.op("vector", lambda e: e.tensor_single_scalar(
                    mask1, lg, sm3[:, 3:4], op=ALU.is_ge), inc="vv")
                em.wait("vector", "vv")
                em.op("vector", lambda e: e.scalar_tensor_tensor(
                    masked, mask1, -1e30, lg, op0=ALU.mult, op1=ALU.add), inc="vv")
                em.wait("vector", "vv")
                em.op("vector", lambda e: e.tensor_reduce(
                    sm3[:, 4:5], masked, axis=AX.X, op=ALU.max), inc="vv")
                em.wait("vector", "vv")
                em.op("vector", lambda e: e.tensor_single_scalar(
                    keep, lg, sm3[:, 4:5], op=ALU.is_ge), inc="vv")
                em.wait("vector", "vv")
                em.op("vector", lambda e: e.tensor_tensor(
                    keep, keep, ex, op=ALU.mult), inc="vv")
                em.wait("vector", "vv")
                em.op("vector", lambda e, ci=ci: e.tensor_scalar(
                    wdense[:, ci * 8:(ci + 1) * 8], keep, sm3[:, 2:3], None,
                    op0=ALU.mult), inc="vv")

                def mm_hidden(base_col, dst_bf):
                    for fc in range(FC):
                        for dc in range(FC):
                            em.op("tensor", lambda e, fc=fc, dc=dc,
                                  st=(dc == 0), sp=(dc == FC - 1),
                                  base_col=base_col: e.matmul(
                                      pbig[:, fc * 128:(fc + 1) * 128],
                                      wpool[:, base_col + dc * 2048 + fc * 128:
                                            base_col + dc * 2048 + (fc + 1) * 128],
                                      xnv[:, dc, :], start=st, stop=sp),
                                  inc=("tt" if dc == FC - 1 and fc == FC - 1
                                       else None))
                    t_a = em.c["tt"]
                    em.wait("scalar", "vv")
                    em.wait("scalar", "tt", t_a)
                    em.op("scalar", lambda e: e.activation(
                        satmp[:, 0:512], pbig[:, 0:512], AF.Silu), inc="aa")
                    em.op("scalar", lambda e: e.activation(
                        satmp[:, 512:1024], pbig[:, 512:1024], AF.Silu), inc="aa")
                    a_sa = em.c["aa"]
                    em.wait("tensor", "aa", a_sa)
                    for fc in range(FC):
                        for dc in range(FC):
                            em.op("tensor", lambda e, fc=fc, dc=dc,
                                  st=(dc == 0), sp=(dc == FC - 1),
                                  base_col=base_col: e.matmul(
                                      pbig[:, fc * 128:(fc + 1) * 128],
                                      wpool[:, base_col + dc * 2048 + 1024 + fc * 128:
                                            base_col + dc * 2048 + 1024 + (fc + 1) * 128],
                                      xnv[:, dc, :], start=st, stop=sp),
                                  inc=("tt" if dc == FC - 1 and fc == FC - 1
                                       else None))
                    t_b = em.c["tt"]
                    em.wait("vector", "tt", t_b)
                    for hb in range(2):
                        em.op("vector", lambda e, dst_bf=dst_bf, hb=hb:
                              e.tensor_tensor(
                                  dst_bf[:, hb * 512:(hb + 1) * 512],
                                  satmp[:, hb * 512:(hb + 1) * 512],
                                  pbig[:, hb * 512:(hb + 1) * 512],
                                  op=ALU.mult), inc="vv")
                    return em.c["vv"]

                v_sw = mm_hidden(0, swb)
                em.wait("tensor", "vv", v_sw)
                swv = swb.rearrange("p (f t) -> p f t", f=FC)
                for half in range(2):
                    for hc in range(FC):
                        em.op("tensor", lambda e, half=half, hc=hc,
                              st=(hc == 0), sp=(hc == FC - 1): e.matmul(
                                  pbig[:, half * 512:(half + 1) * 512],
                                  swv[:, hc, :],
                                  wpool[:, 32768 + hc * 1024 + half * 512:
                                        32768 + hc * 1024 + (half + 1) * 512],
                                  start=st, stop=sp),
                              inc=("tt" if half == 1 and hc == FC - 1 else None))
                tso = em.c["tt"]
                em.wait("vector", "tt", tso)
                for hb in range(2):
                    em.op("vector", lambda e, lci=lci, hb=hb: e.tensor_copy(
                        accq[:, lci * D + hb * 512:lci * D + (hb + 1) * 512],
                        pbig[:, hb * 512:(hb + 1) * 512]), inc="vv")
                em.wait("tensor", "vv", em.c["vv"])
                mm_hidden(16384, hidq[:, lci * D:(lci + 1) * D])
                em.wait("tensor", "vv", em.c["vv"])

            for e_i in range(8):
                dbslot = 49152 if e_i % 2 == 0 else 57344
                em.wait("gpsimd", "tt", em.c["tt"])
                em.dma("gpsimd", wpool[:, dbslot:dbslot + 8192], dw2[e_i], "dg")
                w2l = em.c["dg"]
                em.wait("tensor", "dg", w2l)
                for ci in grp:
                    lci = ci - g0
                    hv = hidq[:, lci * D:(lci + 1) * D].rearrange(
                        "p (f t) -> p f t", f=FC)
                    for half in range(2):
                        for hc in range(FC):
                            em.op("tensor", lambda e, half=half, hc=hc,
                                  st=(hc == 0), sp=(hc == FC - 1), hv=hv,
                                  dbslot=dbslot: e.matmul(
                                      pbig[:, half * 512:(half + 1) * 512],
                                      hv[:, hc, :],
                                      wpool[:, dbslot + hc * 1024 + half * 512:
                                            dbslot + hc * 1024 + (half + 1) * 512],
                                      start=st, stop=sp),
                                  inc=("tt" if half == 1 and hc == FC - 1
                                       else None))
                    tex = em.c["tt"]
                    em.wait("vector", "tt", tex)
                    for hb in range(2):
                        em.op("vector", lambda e, lci=lci, ci=ci, e_i=e_i, hb=hb:
                              e.scalar_tensor_tensor(
                                  accq[:, lci * D + hb * 512:lci * D + (hb + 1) * 512],
                                  pbig[:, hb * 512:(hb + 1) * 512],
                                  wdense[:, ci * 8 + e_i:ci * 8 + e_i + 1],
                                  accq[:, lci * D + hb * 512:lci * D + (hb + 1) * 512],
                                  op0=ALU.mult, op1=ALU.add), inc="vv")
                    em.wait("tensor", "vv", em.c["vv"])

            for ci in grp:
                lci = ci - g0
                em.wait("scalar", "vv")
                em.wait("scalar", "dx")
                em.op("scalar", lambda e, lci=lci: e.activation(
                    tmpa[:], accq[:, lci * D:(lci + 1) * D], AF.Silu), inc="aa")
                agf = em.c["aa"]
                em.wait("sync", "vv")
                em.dma("sync", tmpb[:], dscr_s[ci * TP:(ci + 1) * TP, :], "dx")
                sxl = em.c["dx"]
                em.wait("vector", "aa", agf)
                em.wait("vector", "dx", sxl)
                em.op("vector", lambda e: e.tensor_tensor(
                    tmpa[:], tmpa[:], tmpb[:], op=ALU.mult), inc="vv")
                vsg = em.c["vv"]
                em.wait("sync", "vv", vsg)
                em.dma("sync", tmpb[:], dx[ci * TP:(ci + 1) * TP, :], "dx")
                xxl = em.c["dx"]
                em.wait("vector", "dx", xxl)
                em.op("vector", lambda e: e.tensor_tensor(
                    tmpa[:], tmpa[:], tmpb[:], op=ALU.add), inc="vv")
                vout = em.c["vv"]
                em.wait("sync", "vv", vout)
                em.dma("sync", dout[ci * TP:(ci + 1) * TP, :], tmpa[:], "dx")

        em.wait("sync", "dx", em.c["dx"])
        em.wait("sync", "dg", em.c["dg"])

        def replay(eng_name):
            def run(handle):
                for item in em.q[eng_name]:
                    if item[0] == "w":
                        handle.wait_ge(sems[item[1]], item[2])
                    else:
                        instr = item[1](handle)
                        for s, n in item[2]:
                            instr.then_inc(sems[s], n)
            return run

        block.sync(replay("sync"))
        block.tensor(replay("tensor"))
        block.vector(replay("vector"))
        block.scalar(replay("scalar"))
        block.gpsimd(replay("gpsimd"))

    return nc


def _prep(inputs):
    f32 = np.float32
    fmap = np.array([(fp % 16) * 64 + fp // 16 for fp in range(D)])

    def lhsT_tiles(Wt):
        return np.ascontiguousarray(
            Wt.reshape(8, 128, Wt.shape[1]).transpose(1, 0, 2).reshape(128, -1))

    g = lambda k: np.asarray(inputs[k], f32)
    mats = []
    for nm, perm in [("Wr", True), ("Wk", True), ("Wg", True), ("Wt", True),
                     ("Wv", False), ("Wa", False), ("Wb", False)]:
        W = g("L0_attn_" + nm)
        if perm:
            W = W[fmap]
        mats.append(lhsT_tiles(np.ascontiguousarray(W.T)))
    wproj = np.concatenate(mats, axis=1)
    wo = lhsT_tiles(np.ascontiguousarray(g("L0_attn_Wo").T[fmap]))
    w_in = lhsT_tiles(g("L0_ffn_w_shared_in"))
    w1 = lhsT_tiles(g("L0_ffn_w1_shared"))
    w_out = lhsT_tiles(g("L0_ffn_w_shared_out"))
    wgate = lhsT_tiles(g("L0_ffn_w_gate"))
    wffn = np.concatenate([w_in, w1, w_out, wgate], axis=1)
    w2 = np.stack([lhsT_tiles(g("L0_ffn_w2")[e]) for e in range(8)])
    lnw_rep = np.ascontiguousarray(
        np.broadcast_to(g("L0_attn_ln_weight")[None, :], (128, D)))
    gb_rep = np.ascontiguousarray(
        np.broadcast_to(g("L0_ffn_gate_bias")[None, :], (128, 8)))
    cw = g("L0_attn_conv_w")[:, 0, :]
    cw_f = np.ascontiguousarray(
        cw.reshape(8, 128, 4).transpose(1, 0, 2).reshape(128, 32))
    bdp = np.ascontiguousarray(g("L0_attn_base_decay")[fmap].reshape(8, 128).T)
    sel = np.zeros((16, 128), f32)
    for p in range(128):
        sel[p % 16, p] = 1.0
    ident = np.eye(128, dtype=f32)
    half = np.zeros((128, 2), f32)
    half[0:64, 0] = 1.0
    half[64:128, 1] = 1.0
    return dict(wproj=wproj, wo=wo, wffn=wffn, w2=w2, lnw_rep=lnw_rep,
                gb_rep=gb_rep, cw_f=cw_f, bdp=bdp, sel=sel, ident=ident,
                halfones=half)


def kernel(**inputs):
    x = np.asarray(inputs["L0_attn_x"], np.float32)
    B, L, _ = x.shape
    shared = _prep(inputs)
    nc = build_nc(L)
    in_maps = []
    for b in range(B):
        m = dict(shared)
        m["x"] = np.ascontiguousarray(x[b])
        in_maps.append(m)
    res = run_bass_kernel_spmd(nc, in_maps, core_ids=list(range(B)))
    return np.stack([res.results[b]["out"] for b in range(B)], axis=0)


# version 04
# speedup vs baseline: 1.0351x; 1.0351x over previous
import numpy as np
import ml_dtypes
import concourse.bass as bass
import concourse.mybir as mybir
from concourse.bass_utils import run_bass_kernel_spmd

F32 = mybir.dt.float32
BF16 = mybir.dt.bfloat16
AF = mybir.ActivationFunctionType
ALU = mybir.AluOpType
AX = mybir.AxisListType

D = 1024
H = 16
FC = 8
TP = 128
THJ = 16
TRS = 8
EPS_ST = 1e-5
EPS_RMS = 1e-6
SQS = float(2.0 ** -20)
SQS2 = float(2.0 ** -44)
SQB = float(2.0 ** 22 / 8.0)
SNB = float(2.0 ** 20 / 32.0)

ENGINES = ("sync", "tensor", "vector", "scalar", "gpsimd")


class Em:
    def __init__(self, sem_names):
        self.q = {k: [] for k in ENGINES}
        self.c = {k: 0 for k in sem_names}
        self.serialize = False

    def wait(self, eng, sem, val=None):
        if val is None:
            val = self.c[sem]
        if val > 0:
            self.q[eng].append(("w", sem, val))

    def op(self, eng, fn, inc=None):
        if self.serialize:
            for sname in self.c:
                self.wait(eng, sname)
        incs = ()
        if inc is not None:
            incs = ((inc, 1),)
            self.c[inc] += 1
        self.q[eng].append(("op", fn, incs))

    def dma(self, eng, out, in_, sem):
        if eng == "sync":
            self.wait("sync", sem)
        def fn(e, out=out, in_=in_):
            return e.dma_start(out=out, in_=in_)
        self.q[eng].append(("op", fn, ((sem, 16),)))
        self.c[sem] += 16


def build_nc(L=2048):
    NCH = L // TP
    NSUB = TP // THJ
    groups = [list(range(g, min(g + 2, NCH))) for g in range(0, NCH, 2)]
    nc = bass.Bass("TRN2", target_bir_lowering=False,
                   detect_race_conditions=False)

    dx = nc.dram_tensor("x", [L, D], F32, kind="ExternalInput").ap()
    dwproj = nc.dram_tensor("wproj", [128, 7 * 8192], F32, kind="ExternalInput").ap()
    dwo = nc.dram_tensor("wo", [128, 8192], F32, kind="ExternalInput").ap()
    dwffn = nc.dram_tensor("wffn", [128, 41024], F32, kind="ExternalInput").ap()
    dw2 = nc.dram_tensor("w2", [8, 128, 8192], F32, kind="ExternalInput").ap()
    dlnw = nc.dram_tensor("lnw_rep", [128, D], F32, kind="ExternalInput").ap()
    dgb = nc.dram_tensor("gb_rep", [128, 8], F32, kind="ExternalInput").ap()
    dcw = nc.dram_tensor("cw_f", [128, 32], F32, kind="ExternalInput").ap()
    dbdp = nc.dram_tensor("bdp", [128, 8], F32, kind="ExternalInput").ap()
    dsel = nc.dram_tensor("sel", [16, 128], F32, kind="ExternalInput").ap()
    dident = nc.dram_tensor("ident", [128, 128], F32, kind="ExternalInput").ap()
    dhalf = nc.dram_tensor("halfones", [128, 2], F32, kind="ExternalInput").ap()
    dout = nc.dram_tensor("out", [L, D], F32, kind="ExternalOutput").ap()
    dscr_s = nc.dram_tensor("scr_s", [L, D], F32).ap()
    dscr_xnb = nc.dram_tensor("scr_xnb", [NCH, 128, D], BF16).ap()
    dscr_cv = nc.dram_tensor("scr_cv", [3, D, TP], F32).ap()

    sem_names = ["dx", "dg", "tt", "vv", "aa"]

    from contextlib import ExitStack
    with ExitStack() as _st:
        wpool = _st.enter_context(nc.sbuf_tensor("wpool", [128, 65536], BF16))
        xn_tok = _st.enter_context(nc.sbuf_tensor("xn_tok", [128, D], F32))
        xn_fm = _st.enter_context(nc.sbuf_tensor("xn_fm", [128, FC * (TP + 3)], F32))
        kkfm = _st.enter_context(nc.sbuf_tensor("kkfm", [128, D], F32))
        cfm = _st.enter_context(nc.sbuf_tensor("cfm", [128, D], F32))
        vfm = _st.enter_context(nc.sbuf_tensor("vfm", [128, D], F32))
        winv = _st.enter_context(nc.sbuf_tensor("winv", [128, D], F32))
        kwb = _st.enter_context(nc.sbuf_tensor("kwb", [128, D], F32))
        tmpa = _st.enter_context(nc.sbuf_tensor("tmpa", [128, D], F32))
        rsbuf = _st.enter_context(nc.sbuf_tensor("rsbuf", [128, FC * TP], F32))
        lambuf = _st.enter_context(nc.sbuf_tensor("lambuf", [128, FC * TP], F32))
        zst = _st.enter_context(nc.sbuf_tensor("zst", [128, 512], F32))
        sm = _st.enter_context(nc.sbuf_tensor("sm", [128, 16], F32))
        sm2 = _st.enter_context(nc.sbuf_tensor("sm2", [128, 16], F32))
        sm3 = _st.enter_context(nc.sbuf_tensor("sm3", [128, 32], F32))
        sm4 = _st.enter_context(nc.sbuf_tensor("sm4", [128, 16], F32))
        sctmp = _st.enter_context(nc.sbuf_tensor("sctmp", [128, 64], F32))
        cwf = _st.enter_context(nc.sbuf_tensor("cwf", [128, 32], F32))
        bdps = _st.enter_context(nc.sbuf_tensor("bdps", [128, 8], F32))
        lnw = _st.enter_context(nc.sbuf_tensor("lnw", [128, D], BF16))
        gbr = _st.enter_context(nc.sbuf_tensor("gbr", [128, 8], F32))
        self32 = _st.enter_context(nc.sbuf_tensor("self32", [16, 128], F32))
        ident32 = _st.enter_context(nc.sbuf_tensor("ident32", [128, 128], F32))
        half2 = _st.enter_context(nc.sbuf_tensor("half2", [128, 2], F32))
        upool = _st.enter_context(nc.sbuf_tensor("upool", [128, 2 * 194 * THJ], F32))
        xnfmb = _st.enter_context(nc.sbuf_tensor("xnfmb", [128, D], BF16))
        up = upool[:]
        HJC = 194 * THJ
        hjbufs_ap = (up[0:16, 0:HJC], up[0:16, HJC:2 * HJC])
        hidq = up[:, 0:1024].bitcast(BF16)
        accq = up[:, 1024:3072]
        swb = up[:, 3072:3584].bitcast(BF16)
        satmp = up[:, 3584:4608]
        wdense = up[:, 4608:4608 + NCH * 8]
        tmpb = vfm
        gob = cfm[:, 0:512].bitcast(BF16)
        ebd = bdps
        pm = _st.enter_context(nc.psum_tensor("pm", [128, 512], F32))
        pm2 = _st.enter_context(nc.psum_tensor("pm2", [128, 512], F32))
        t1b = _st.enter_context(nc.psum_tensor("t1b", [128, 512], F32))
        prep0 = _st.enter_context(nc.psum_tensor("prep0", [128, 512], F32))
        prep1 = _st.enter_context(nc.psum_tensor("prep1", [128, 512], F32))
        pbig = _st.enter_context(nc.psum_tensor("pbig", [128, 1024], F32))
        paux = _st.enter_context(nc.psum_tensor("paux", [128, 512], F32))
        s_dx = _st.enter_context(nc.semaphore("s_dx"))
        s_dg = _st.enter_context(nc.semaphore("s_dg"))
        s_tt = _st.enter_context(nc.semaphore("s_tt"))
        s_vv = _st.enter_context(nc.semaphore("s_vv"))
        s_aa = _st.enter_context(nc.semaphore("s_aa"))
        block = _st.enter_context(nc.Block())
        sems = {"dx": s_dx, "dg": s_dg, "tt": s_tt, "vv": s_vv, "aa": s_aa}
        em = Em(sem_names)

        zv = zst[:].rearrange("p (f j) -> p f j", f=FC)
        rsb = rsbuf[:].rearrange("p (t f) -> p t f", f=FC)
        lmb = lambuf[:].rearrange("p (t f) -> p t f", f=FC)
        xnfm = xn_fm[:].rearrange("p (f t) -> p f t", f=FC)
        winvv = winv[:].rearrange("p (f t) -> p f t", f=FC)
        kwv = kwb[:].rearrange("p (f t) -> p f t", f=FC)

        rs_trk = sm[:, 0:8]
        dnegs = (sm[:, 8:16], sm4[:, 0:8])
        sqvs = (sm3[:, 8:16], sm4[:, 8:16])
        p_sc = sm2[:, 0:8]
        q_sc = sm2[:, 8:16]
        mue = sm3[:, 0:8]
        sqv = sm3[:, 8:16]
        linv = sm3[:, 16:24]
        rs_tmp = sm3[:, 24:32]

        def bc8(ap2d, n=64):
            return ap2d.unsqueeze(2).broadcast_to([128, 8, n])

        def bc64(ap2d):
            return ap2d.unsqueeze(1).broadcast_to([128, 8, 64])

        for dst, src in [(cwf[:], dcw), (bdps[:], dbdp),
                         (gbr[:], dgb), (self32[:], dsel), (ident32[:], dident),
                         (half2[:], dhalf)]:
            em.dma("sync", dst, src, "dx")
        const_ld = em.c["dx"]
        em.dma("gpsimd", lnw[:], dlnw, "dg")
        for i in range(7):
            em.dma("gpsimd", wpool[:, i * 8192:(i + 1) * 8192],
                   dwproj[:, i * 8192:(i + 1) * 8192], "dg")
        em.dma("gpsimd", wpool[:, 7 * 8192:8 * 8192], dwo, "dg")
        attn_w_ld = em.c["dg"]

        em.wait("scalar", "dx", const_ld)
        em.op("scalar", lambda e: e.activation(bdps[:], bdps[:], AF.Exp,
                                               scale=-1.0), inc="aa")
        a_ebd = em.c["aa"]
        em.op("vector", lambda e: e.memset(zst[:], 0.0), inc="vv")
        em.op("vector", lambda e: e.memset(rs_trk, 0.0), inc="vv")
        em.wait("vector", "dx", const_ld)
        em.wait("vector", "dg", attn_w_ld)
        em.wait("vector", "aa", a_ebd)
        em.wait("tensor", "dx", const_ld)
        em.wait("tensor", "dg", attn_w_ld)

        wslice = {}
        for i, nm in enumerate(["r", "k", "g", "wt", "v", "a", "kk"]):
            wslice[nm] = wpool[:, i * 8192:(i + 1) * 8192]
        wslice["wo"] = wpool[:, 7 * 8192:8 * 8192]

        rep_free = {0: 0, 1: 0}
        pend_s6 = [None]

        for ci in range(NCH):
            em.wait("sync", "vv")
            em.dma("sync", tmpb[:], dx[ci * TP:(ci + 1) * TP, :], "dx")
            xld = em.c["dx"]
            em.wait("scalar", "vv")
            em.wait("scalar", "dx", xld)
            em.op("scalar", lambda e: e.activation(tmpa[:], tmpb[:], AF.Square,
                                                   accum_out=sm3[:, 0:1]), inc="aa")
            em.op("scalar", lambda e: e.activation(xn_tok[:], tmpb[:], AF.Identity,
                                                   accum_out=sm3[:, 1:2]), inc="aa")
            aln = em.c["aa"]
            em.wait("vector", "aa", aln)
            em.op("vector", lambda e: e.tensor_scalar(
                sm3[:, 2:3], sm3[:, 1:2], float(1.0 / D), None, op0=ALU.mult), inc="vv")
            em.op("vector", lambda e: e.tensor_scalar(
                sm3[:, 3:4], sm3[:, 0:1], float(1.0 / D), None, op0=ALU.mult), inc="vv")
            em.op("vector", lambda e: e.tensor_tensor(
                sm3[:, 4:5], sm3[:, 2:3], sm3[:, 2:3], op=ALU.mult), inc="vv")
            em.wait("vector", "vv")
            em.op("vector", lambda e: e.tensor_tensor(
                sm3[:, 5:6], sm3[:, 3:4], sm3[:, 4:5], op=ALU.subtract), inc="vv")
            em.wait("vector", "vv")
            em.op("vector", lambda e: e.tensor_scalar(
                sm3[:, 5:6], sm3[:, 5:6], 1e-5, None, op0=ALU.add), inc="vv")
            vvar = em.c["vv"]
            em.wait("scalar", "vv", vvar)
            em.op("scalar", lambda e: e.activation(sm3[:, 6:7], sm3[:, 5:6],
                                                   AF.Sqrt), inc="aa")
            asd = em.c["aa"]
            em.wait("vector", "aa", asd)
            em.op("vector", lambda e: e.reciprocal(sm3[:, 7:8], sm3[:, 6:7]), inc="vv")
            em.wait("vector", "vv")
            em.op("vector", lambda e: e.tensor_scalar(
                xn_tok[:], tmpb[:], sm3[:, 2:3], sm3[:, 7:8],
                op0=ALU.subtract, op1=ALU.mult), inc="vv")
            vxn = em.c["vv"]

            em.wait("tensor", "vv", vxn)
            for fc in range(FC):
                em.op("tensor", lambda e, fc=fc: e.transpose(
                    pbig[:, fc * 128:(fc + 1) * 128],
                    xn_tok[:, fc * 128:(fc + 1) * 128], ident32[:]),
                    inc=("tt" if fc == FC - 1 else None))
            ttr = em.c["tt"]
            em.wait("vector", "tt", ttr)
            if ci == 0:
                em.op("vector", lambda e: e.memset(xn_fm[:], 0.0), inc="vv")
            else:
                em.op("vector", lambda e: e.tensor_copy(
                    xnfm[:, :, 0:3], xnfm[:, :, TP:TP + 3]), inc="vv")
            for hb in range(2):
                em.op("vector", lambda e, hb=hb: e.tensor_copy(
                    xnfm[:, hb * 4:hb * 4 + 4, 3:TP + 3],
                    pbig[:, hb * 512:(hb + 1) * 512].rearrange(
                        "p (f t) -> p f t", f=4)), inc="vv")
            vxnfm = em.c["vv"]
            em.wait("scalar", "tt", ttr)
            em.wait("scalar", "dx")
            em.op("vector", lambda e: e.tensor_copy(
                xnfmb[:, 0:512], pbig[:, 0:512]), inc="aa")
            em.op("vector", lambda e: e.tensor_copy(
                xnfmb[:, 512:1024], pbig[:, 512:1024]), inc="aa")
            axnb = em.c["aa"]
            em.wait("sync", "aa", axnb)
            em.dma("sync", dscr_xnb[ci], xnfmb[:], "dx")

            cw3 = cwf[:].rearrange("p (f k) -> p f k", f=FC)
            em.op("vector", lambda e: e.tensor_tensor(
                cfm[:].rearrange("p (f t) -> p f t", f=FC),
                xnfm[:, :, 0:TP],
                cw3[:, :, 0:1].broadcast_to([128, FC, TP]), op=ALU.mult), inc="vv")
            for j in range(1, 4):
                em.op("vector", lambda e, j=j: e.tensor_tensor(
                    tmpa[:].rearrange("p (f t) -> p f t", f=FC),
                    xnfm[:, :, j:TP + j],
                    cw3[:, :, j:j + 1].broadcast_to([128, FC, TP]),
                    op=ALU.mult), inc="vv")
                em.op("vector", lambda e: e.tensor_tensor(
                    cfm[:], cfm[:], tmpa[:], op=ALU.add), inc="vv")
            vxc = em.c["vv"]
            em.wait("tensor", "vv", vxc)
            for fc in range(FC):
                em.op("tensor", lambda e, fc=fc: e.transpose(
                    pbig[:, fc * 128:(fc + 1) * 128],
                    cfm[:, fc * 128:(fc + 1) * 128], ident32[:]),
                    inc=("tt" if fc == FC - 1 else None))
            ttr2 = em.c["tt"]
            em.wait("vector", "tt", ttr2)
            for hb in range(2):
                em.op("vector", lambda e, hb=hb: e.scalar_tensor_tensor(
                    xn_tok[:, hb * 512:(hb + 1) * 512],
                    pbig[:, hb * 512:(hb + 1) * 512], 0.1,
                    xn_tok[:, hb * 512:(hb + 1) * 512],
                    op0=ALU.mult, op1=ALU.add), inc="vv")
            vxg = em.c["vv"]

            def proj(nm):
                ws = wslice[nm]
                for fc in range(FC):
                    for dc in range(FC):
                        lhsT = ws[:, dc * 1024 + fc * 128:dc * 1024 + (fc + 1) * 128]
                        rhs = xnfmb[:].rearrange("p (f t) -> p f t", f=FC)[:, dc, :]
                        em.op("tensor", lambda e, lhsT=lhsT, rhs=rhs, fc=fc,
                              st=(dc == 0), sp=(dc == FC - 1):
                              e.matmul(pbig[:, fc * 128:(fc + 1) * 128], lhsT,
                                       rhs, start=st, stop=sp),
                              inc=("tt" if dc == FC - 1 and fc == FC - 1 else None))
                return em.c["tt"]

            em.wait("tensor", "aa", axnb)
            em.wait("tensor", "vv", vxg)
            twt = proj("wt")
            em.wait("scalar", "vv")
            em.wait("scalar", "dx")
            em.wait("scalar", "tt", twt)
            em.op("scalar", lambda e: e.activation(
                tmpa[:, 0:512], pbig[:, 0:512], AF.Exp), inc="aa")
            em.op("scalar", lambda e: e.activation(
                tmpa[:, 512:1024], pbig[:, 512:1024], AF.Exp), inc="aa")
            em.op("scalar", lambda e: e.activation(tmpb[:], tmpa[:], AF.Exp,
                                                   scale=-1.0), inc="aa")
            awt = em.c["aa"]
            em.wait("vector", "aa", awt)
            em.op("vector", lambda e: e.tensor_tensor(
                tmpa[:].rearrange("p (f t) -> p f t", f=FC),
                tmpb[:].rearrange("p (f t) -> p f t", f=FC),
                bc8(ebd[:], TP), op=ALU.mult), inc="vv")
            em.op("vector", lambda e: e.tensor_scalar(
                tmpa[:], tmpa[:], 0.01, 0.99, op0=ALU.max, op1=ALU.min), inc="vv")
            em.op("vector", lambda e: e.reciprocal(winv[:], tmpa[:]), inc="vv")
            vwt = em.c["vv"]

            em.wait("tensor", "vv", vwt)
            tk = proj("k")
            em.wait("vector", "tt", tk)
            for hb in range(2):
                em.op("vector", lambda e, hb=hb: e.tensor_tensor(
                    kwb[:, hb * 512:(hb + 1) * 512],
                    pbig[:, hb * 512:(hb + 1) * 512],
                    winv[:, hb * 512:(hb + 1) * 512], op=ALU.mult), inc="vv")
            vk = em.c["vv"]

            em.wait("tensor", "vv", vk)
            tkk = proj("kk")
            em.wait("vector", "tt", tkk)
            for hb in range(2):
                em.op("vector", lambda e, hb=hb: e.tensor_copy(
                    kkfm[:, hb * 512:(hb + 1) * 512],
                    pbig[:, hb * 512:(hb + 1) * 512]), inc="vv")
            vkk = em.c["vv"]

            em.wait("tensor", "vv", vkk)
            ta = proj("a")
            em.wait("vector", "tt", ta)
            for hb in range(2):
                em.op("vector", lambda e, hb=hb: e.tensor_tensor(
                    cfm[:, hb * 512:(hb + 1) * 512],
                    kkfm[:, hb * 512:(hb + 1) * 512],
                    pbig[:, hb * 512:(hb + 1) * 512], op=ALU.mult), inc="vv")
            vc = em.c["vv"]

            em.wait("tensor", "vv", vc)
            tv = proj("v")
            em.wait("vector", "tt", tv)
            for hb in range(2):
                em.op("vector", lambda e, hb=hb: e.tensor_copy(
                    vfm[:, hb * 512:(hb + 1) * 512],
                    pbig[:, hb * 512:(hb + 1) * 512]), inc="vv")
            vv_ = em.c["vv"]

            em.wait("tensor", "vv", vv_)
            for t_i, src in enumerate((cfm, vfm)):
                for fc in range(FC):
                    em.op("tensor", lambda e, src=src, fc=fc, t_i=t_i: e.matmul(
                        paux[:, t_i * 16 + fc * 2:t_i * 16 + fc * 2 + 2],
                        src[:, fc * 128:(fc + 1) * 128], half2[:],
                        start=True, stop=True),
                        inc=("tt" if t_i == 1 and fc == FC - 1 else None))
            tsc = em.c["tt"]
            em.wait("vector", "tt", tsc)
            if ci == 0:
                em.op("vector", lambda e: e.memset(sctmp[:], 0.0), inc="vv")
            em.op("vector", lambda e: e.tensor_copy(sctmp[:, 0:16], paux[:, 0:16]),
                  inc="vv")
            em.op("vector", lambda e: e.tensor_copy(sctmp[:, 32:48], paux[:, 16:32]),
                  inc="vv")
            vsc = em.c["vv"]
            em.wait("tensor", "vv", vsc)
            em.op("tensor", lambda e: e.transpose(
                paux[0:64, 0:128], sctmp[:], ident32[:]), inc="tt")
            tsc2 = em.c["tt"]

            em.wait("sync", "vv", vv_)
            em.dma("sync", dscr_cv[0].rearrange("(fc p) t -> p fc t", p=128),
                   kkfm[:].rearrange("p (fc t) -> p fc t", fc=FC), "dx")
            em.dma("sync", dscr_cv[1].rearrange("(fc p) t -> p fc t", p=128),
                   cfm[:].rearrange("p (fc t) -> p fc t", fc=FC), "dx")
            em.dma("sync", dscr_cv[2].rearrange("(fc p) t -> p fc t", p=128),
                   vfm[:].rearrange("p (fc t) -> p fc t", fc=FC), "dx")
            cvout = em.c["dx"]
            rg = kkfm

            em.wait("tensor", "vv", vv_)
            tr = proj("r")
            em.wait("scalar", "tt", tr)
            em.wait("scalar", "dx", cvout)
            em.op("scalar", lambda e, rg=rg: e.activation(
                rg[:, 0:512], pbig[:, 0:512], AF.Sigmoid), inc="aa")
            em.op("scalar", lambda e, rg=rg: e.activation(
                rg[:, 512:1024], pbig[:, 512:1024], AF.Sigmoid), inc="aa")
            ar = em.c["aa"]

            em.wait("tensor", "aa", ar)
            tg = proj("g")
            em.wait("scalar", "tt", tg)
            em.wait("scalar", "vv")
            em.op("scalar", lambda e: e.activation(
                tmpb[:, 0:512], pbig[:, 0:512], AF.Sigmoid), inc="aa")
            em.op("scalar", lambda e: e.activation(
                tmpb[:, 512:1024], pbig[:, 512:1024], AF.Sigmoid), inc="aa")
            ag = em.c["aa"]
            em.wait("vector", "aa", ag)
            em.op("vector", lambda e, rg=rg: e.tensor_tensor(
                rg[:], rg[:], tmpb[:], op=ALU.mult), inc="vv")
            vrg = em.c["vv"]

            hjbufs = hjbufs_ap
            hj_dx = {}
            hj_vv = {}
            sel_done = {}

            def emit_hj_load(sub):
                hj = hjbufs[sub % 2]
                hjv = hj.rearrange("h (c t) -> h c t", c=194)
                em.wait("sync", "dx", cvout)
                if sub >= 2:
                    em.wait("sync", "tt", sel_done[sub - 2])
                for vi in range(3):
                    src = dscr_cv[vi].rearrange("(h j) t -> h j t", h=H)[
                        :, :, sub * THJ:(sub + 1) * THJ]
                    em.dma("sync", hjv[:, vi * 64:(vi + 1) * 64, :], src, "dx")
                hj_dx[sub] = em.c["dx"]
                em.wait("vector", "tt", tsc2)
                em.op("vector", lambda e, hjv=hjv, sub=sub: e.tensor_copy(
                    hjv[:, 192:193, :],
                    paux[0:16, sub * THJ:(sub + 1) * THJ].unsqueeze(1)), inc="vv")
                em.op("vector", lambda e, hjv=hjv, sub=sub: e.tensor_copy(
                    hjv[:, 193:194, :],
                    paux[32:48, sub * THJ:(sub + 1) * THJ].unsqueeze(1)), inc="vv")
                hj_vv[sub] = em.c["vv"]

            emit_hj_load(0)
            if NSUB > 1:
                emit_hj_load(1)

            for sub in range(NSUB):
                hj = hjbufs[sub % 2]
                hjv = hj.rearrange("h (c t) -> h c t", c=194)
                for ts in range(THJ):
                    tt_ = sub * THJ + ts
                    gt = ci * TP + tt_
                    par = gt % 2
                    prep = (prep0, prep1)[par]
                    dneg = dnegs[par]
                    sqv = sqvs[par]
                    after_rs = (gt % TRS == 0)

                    em.wait("tensor", "dx", hj_dx[sub])
                    em.wait("tensor", "vv", hj_vv[sub])
                    if rep_free[par] > 0:
                        em.wait("tensor", "vv", rep_free[par])
                    em.op("tensor", lambda e, prep=prep, hjv=hjv, ts=ts: e.matmul(
                        prep[:, 0:194], self32[:], hjv[:, :, ts],
                        start=True, stop=True), inc="tt")
                    trep = em.c["tt"]

                    wsl = winvv[:, :, tt_]
                    kwsl = kwv[:, :, tt_]

                    em.wait("vector", "tt", trep)
                    em.op("vector", lambda e, prep=prep: e.tensor_tensor(
                        pm[:].rearrange("p (f j) -> p f j", f=FC), zv,
                        bc64(prep[:, 0:64]), op=ALU.mult), inc="vv")
                    em.wait("vector", "vv")
                    em.op("vector", lambda e, dneg=dneg: e.tensor_reduce(
                        dneg, pm[:].rearrange("p (f j) -> p f j", f=FC),
                        axis=AX.X, op=ALU.add, negate=True), inc="vv")
                    em.wait("vector", "vv")
                    em.op("vector", lambda e, wsl=wsl, dneg=dneg: e.tensor_tensor(
                        p_sc, dneg, wsl, op=ALU.mult), inc="vv")
                    em.wait("vector", "vv")
                    em.op("vector", lambda e, prep=prep: e.tensor_tensor(
                        t1b[:].rearrange("p (f j) -> p f j", f=FC),
                        bc64(prep[:, 64:128]), bc8(p_sc), op=ALU.mult), inc="vv")
                    if pend_s6[0] is not None:
                        pend_s6[0]()
                        pend_s6[0] = None
                    if after_rs:
                        qop = kwsl
                    else:
                        lprev = lmb[:, tt_ - 1, :]
                        em.op("vector", lambda e, lprev=lprev, kwsl=kwsl:
                              e.tensor_tensor(q_sc, lprev, kwsl, op=ALU.mult),
                              inc="vv")
                        em.wait("vector", "vv")
                        qop = q_sc
                    em.wait("vector", "vv")
                    em.op("vector", lambda e: e.tensor_tensor(
                        zst[:], zst[:], t1b[:], op=ALU.add), inc="vv")
                    em.op("vector", lambda e, prep=prep, qop=qop: e.tensor_tensor(
                        t1b[:].rearrange("p (f j) -> p f j", f=FC),
                        bc64(prep[:, 128:192]), bc8(qop), op=ALU.mult), inc="vv")
                    rs_prev = rs_trk if after_rs else rsb[:, tt_ - 1, :]
                    em.op("vector", lambda e, prep=prep, rs_prev=rs_prev:
                          e.scalar_tensor_tensor(rs_tmp, p_sc, prep[:, 192:193],
                                                 rs_prev, op0=ALU.mult,
                                                 op1=ALU.add), inc="vv")
                    em.wait("vector", "vv")
                    em.op("vector", lambda e, prep=prep, qop=qop, tt_=tt_:
                          e.scalar_tensor_tensor(rsb[:, tt_, :], qop,
                                                 prep[:, 193:194], rs_tmp,
                                                 op0=ALU.mult, op1=ALU.add),
                          inc="vv")
                    rep_free[par] = em.c["vv"]
                    em.wait("vector", "vv")
                    em.op("vector", lambda e: e.tensor_tensor(
                        zst[:], zst[:], t1b[:], op=ALU.add), inc="vv")
                    em.op("vector", lambda e: e.scalar_tensor_tensor(
                        pm2[:], zst[:], SQS2, zst[:], op0=ALU.mult, op1=ALU.mult),
                        inc="vv")
                    em.wait("vector", "vv")
                    em.op("vector", lambda e, dneg=dneg: e.tensor_reduce(
                        dneg, pm2[:].rearrange("p (f j) -> p f j", f=FC),
                        axis=AX.X, op=ALU.add), inc="vv")
                    vnsq = em.c["vv"]
                    if after_rs:
                        em.op("vector", lambda e, wsl=wsl: e.tensor_scalar(
                            mue, wsl, EPS_ST, None, op0=ALU.mult), inc="vv")
                    else:
                        lprev = lmb[:, tt_ - 1, :]
                        em.op("vector", lambda e, lprev=lprev, wsl=wsl:
                              e.scalar_tensor_tensor(mue, lprev, EPS_ST, wsl,
                                                     op0=ALU.mult, op1=ALU.mult),
                              inc="vv")
                    em.wait("scalar", "vv", vnsq)
                    em.op("scalar", lambda e, sqv=sqv, dneg=dneg: e.activation(
                        sqv, dneg, AF.Sqrt), inc="aa")
                    asv = em.c["aa"]

                    def make_s6(tt_=tt_, asv=asv, sqv=sqv):
                        def emit():
                            em.wait("vector", "aa", asv)
                            em.op("vector", lambda e: e.scalar_tensor_tensor(
                                lmb[:, tt_, :], sqv, SQB, mue,
                                op0=ALU.mult, op1=ALU.add), inc="vv")
                            em.wait("vector", "vv")
                        return emit

                    if gt % TRS == TRS - 1:
                        make_s6()()
                        em.op("vector", lambda e, tt_=tt_: e.reciprocal(
                            linv, lmb[:, tt_, :]), inc="vv")
                        em.wait("vector", "vv")
                        em.op("vector", lambda e: e.tensor_tensor(
                            zv, zv, bc8(linv), op=ALU.mult), inc="vv")
                        em.op("vector", lambda e, tt_=tt_: e.tensor_tensor(
                            rs_trk, rsb[:, tt_, :], linv, op=ALU.mult), inc="vv")
                    else:
                        pend_s6[0] = make_s6()
                sel_done[sub] = em.c["tt"]
                if sub + 2 < NSUB:
                    emit_hj_load(sub + 2)

            if pend_s6[0] is not None:
                pend_s6[0]()
                pend_s6[0] = None
            em.wait("vector", "vv")
            em.wait("vector", "dx", cvout)
            em.op("vector", lambda e: e.reciprocal(tmpa[:], lambuf[:]), inc="vv")
            em.op("vector", lambda e: e.tensor_tensor(
                tmpb[:], rsbuf[:], tmpa[:], op=ALU.mult), inc="vv")
            em.op("vector", lambda e, rg=rg: e.tensor_tensor(
                gob[:, 0:FC * TP].rearrange("p (f t) -> p f t", f=FC),
                tmpb[:].rearrange("p (t f) -> p f t", f=FC),
                rg[:].rearrange("p (f t) -> p f t", f=FC), op=ALU.mult), inc="vv")
            vgo = em.c["vv"]

            em.wait("tensor", "vv", vgo)
            gov = gob[:, 0:FC * TP].rearrange("p (f t) -> p f t", f=FC)
            for half in range(2):
                for fcp in range(FC):
                    em.op("tensor", lambda e, half=half, fcp=fcp,
                          st=(fcp == 0), sp=(fcp == FC - 1): e.matmul(
                              pbig[:, half * 512:(half + 1) * 512],
                              gov[:, fcp, :],
                              wslice["wo"][:, fcp * 1024 + half * 512:
                                           fcp * 1024 + (half + 1) * 512],
                              start=st, stop=sp),
                          inc=("tt" if half == 1 and fcp == FC - 1 else None))
            two = em.c["tt"]
            em.wait("vector", "tt", two)
            for hb in range(2):
                em.op("vector", lambda e, hb=hb: e.tensor_copy(
                    tmpb[:, hb * 512:(hb + 1) * 512],
                    pbig[:, hb * 512:(hb + 1) * 512]), inc="vv")
            vwoe = em.c["vv"]
            em.wait("scalar", "vv", vwoe)
            em.op("scalar", lambda e: e.activation(
                tmpa[:], tmpb[:], AF.Square, scale=SQS,
                accum_out=sm3[:, 0:1]), inc="aa")
            em.op("scalar", lambda e: e.activation(sm3[:, 1:2], sm3[:, 0:1],
                                                   AF.Sqrt), inc="aa")
            aso = em.c["aa"]
            em.wait("vector", "aa", aso)
            em.op("vector", lambda e: e.tensor_scalar(
                sm3[:, 2:3], sm3[:, 1:2], SNB, EPS_RMS,
                op0=ALU.mult, op1=ALU.add), inc="vv")
            em.wait("vector", "vv")
            em.op("vector", lambda e: e.reciprocal(sm3[:, 3:4], sm3[:, 2:3]),
                  inc="vv")
            em.wait("vector", "vv")
            em.op("vector", lambda e: e.scalar_tensor_tensor(
                tmpa[:], tmpb[:], sm3[:, 3:4], lnw[:], op0=ALU.mult,
                op1=ALU.mult), inc="vv")
            em.op("vector", lambda e: e.tensor_tensor(
                xn_tok[:], tmpa[:], xn_tok[:], op=ALU.add), inc="vv")
            vs = em.c["vv"]
            em.wait("sync", "vv", vs)
            em.dma("sync", dscr_s[ci * TP:(ci + 1) * TP, :], xn_tok[:], "dx")

        em.wait("gpsimd", "tt", em.c["tt"])
        em.wait("gpsimd", "aa", em.c["aa"])
        em.dma("gpsimd", wpool[:, 0:16384], dwffn[:, 0:16384], "dg")
        em.dma("gpsimd", wpool[:, 16384:32768], dwffn[:, 16384:32768], "dg")
        em.dma("gpsimd", wpool[:, 32768:40960], dwffn[:, 32768:40960], "dg")
        em.dma("gpsimd", wpool[:, 40960:41024], dwffn[:, 40960:41024], "dg")
        ffnw = em.c["dg"]
        em.wait("tensor", "dg", ffnw)
        wgate_sl = wpool[:, 40960:41024]

        lg = sm2[:, 0:8]
        ex = sm2[:, 8:16]
        mask1 = sctmp[:, 0:8]
        masked = sctmp[:, 8:16]
        keep = sctmp[:, 16:24]

        for grp in groups:
            g0 = grp[0]
            for ci in grp:
                lci = ci - g0
                em.wait("sync", "tt")
                em.dma("sync", xnfmb[:], dscr_xnb[ci], "dx")
                xbl = em.c["dx"]
                em.wait("tensor", "dx", xbl)
                xnv = xnfmb[:].rearrange("p (f t) -> p f t", f=FC)
                em.wait("tensor", "vv")
                for dc in range(FC):
                    em.op("tensor", lambda e, dc=dc, st=(dc == 0),
                          sp=(dc == FC - 1): e.matmul(
                              paux[:, 0:8], xnv[:, dc, :],
                              wgate_sl[:, dc * 8:(dc + 1) * 8], start=st, stop=sp),
                          inc=("tt" if dc == FC - 1 else None))
                tgt = em.c["tt"]
                em.wait("vector", "tt", tgt)
                em.op("vector", lambda e: e.tensor_tensor(
                    lg, paux[:, 0:8], gbr[:], op=ALU.add), inc="vv")
                em.wait("vector", "vv")
                em.op("vector", lambda e: e.tensor_reduce(
                    sm3[:, 0:1], lg, axis=AX.X, op=ALU.max, negate=True), inc="vv")
                vlg = em.c["vv"]
                em.wait("scalar", "vv", vlg)
                em.op("scalar", lambda e: e.activation(
                    ex, lg, AF.Exp, bias=sm3[:, 0:1]), inc="aa")
                age = em.c["aa"]
                em.wait("vector", "aa", age)
                em.op("vector", lambda e: e.tensor_reduce(
                    sm3[:, 1:2], ex, axis=AX.X, op=ALU.add), inc="vv")
                em.wait("vector", "vv")
                em.op("vector", lambda e: e.reciprocal(sm3[:, 2:3], sm3[:, 1:2]),
                      inc="vv")
                em.op("vector", lambda e: e.tensor_reduce(
                    sm3[:, 3:4], lg, axis=AX.X, op=ALU.max), inc="vv")
                em.wait("vector", "vv")
                em.op("vector", lambda e: e.tensor_single_scalar(
                    mask1, lg, sm3[:, 3:4], op=ALU.is_ge), inc="vv")
                em.wait("vector", "vv")
                em.op("vector", lambda e: e.scalar_tensor_tensor(
                    masked, mask1, -1e30, lg, op0=ALU.mult, op1=ALU.add), inc="vv")
                em.wait("vector", "vv")
                em.op("vector", lambda e: e.tensor_reduce(
                    sm3[:, 4:5], masked, axis=AX.X, op=ALU.max), inc="vv")
                em.wait("vector", "vv")
                em.op("vector", lambda e: e.tensor_single_scalar(
                    keep, lg, sm3[:, 4:5], op=ALU.is_ge), inc="vv")
                em.wait("vector", "vv")
                em.op("vector", lambda e: e.tensor_tensor(
                    keep, keep, ex, op=ALU.mult), inc="vv")
                em.wait("vector", "vv")
                em.op("vector", lambda e, ci=ci: e.tensor_scalar(
                    wdense[:, ci * 8:(ci + 1) * 8], keep, sm3[:, 2:3], None,
                    op0=ALU.mult), inc="vv")

                def mm_hidden(base_col, dst_bf):
                    for fc in range(FC):
                        for dc in range(FC):
                            em.op("tensor", lambda e, fc=fc, dc=dc,
                                  st=(dc == 0), sp=(dc == FC - 1),
                                  base_col=base_col: e.matmul(
                                      pbig[:, fc * 128:(fc + 1) * 128],
                                      wpool[:, base_col + dc * 2048 + fc * 128:
                                            base_col + dc * 2048 + (fc + 1) * 128],
                                      xnv[:, dc, :], start=st, stop=sp),
                                  inc=("tt" if dc == FC - 1 and fc == FC - 1
                                       else None))
                    t_a = em.c["tt"]
                    em.wait("scalar", "vv")
                    em.wait("scalar", "tt", t_a)
                    em.op("scalar", lambda e: e.activation(
                        satmp[:, 0:512], pbig[:, 0:512], AF.Silu), inc="aa")
                    em.op("scalar", lambda e: e.activation(
                        satmp[:, 512:1024], pbig[:, 512:1024], AF.Silu), inc="aa")
                    a_sa = em.c["aa"]
                    em.wait("tensor", "aa", a_sa)
                    for fc in range(FC):
                        for dc in range(FC):
                            em.op("tensor", lambda e, fc=fc, dc=dc,
                                  st=(dc == 0), sp=(dc == FC - 1),
                                  base_col=base_col: e.matmul(
                                      pbig[:, fc * 128:(fc + 1) * 128],
                                      wpool[:, base_col + dc * 2048 + 1024 + fc * 128:
                                            base_col + dc * 2048 + 1024 + (fc + 1) * 128],
                                      xnv[:, dc, :], start=st, stop=sp),
                                  inc=("tt" if dc == FC - 1 and fc == FC - 1
                                       else None))
                    t_b = em.c["tt"]
                    em.wait("vector", "tt", t_b)
                    for hb in range(2):
                        em.op("vector", lambda e, dst_bf=dst_bf, hb=hb:
                              e.tensor_tensor(
                                  dst_bf[:, hb * 512:(hb + 1) * 512],
                                  satmp[:, hb * 512:(hb + 1) * 512],
                                  pbig[:, hb * 512:(hb + 1) * 512],
                                  op=ALU.mult), inc="vv")
                    return em.c["vv"]

                v_sw = mm_hidden(0, swb)
                em.wait("tensor", "vv", v_sw)
                swv = swb.rearrange("p (f t) -> p f t", f=FC)
                for half in range(2):
                    for hc in range(FC):
                        em.op("tensor", lambda e, half=half, hc=hc,
                              st=(hc == 0), sp=(hc == FC - 1): e.matmul(
                                  pbig[:, half * 512:(half + 1) * 512],
                                  swv[:, hc, :],
                                  wpool[:, 32768 + hc * 1024 + half * 512:
                                        32768 + hc * 1024 + (half + 1) * 512],
                                  start=st, stop=sp),
                              inc=("tt" if half == 1 and hc == FC - 1 else None))
                tso = em.c["tt"]
                em.wait("vector", "tt", tso)
                for hb in range(2):
                    em.op("vector", lambda e, lci=lci, hb=hb: e.tensor_copy(
                        accq[:, lci * D + hb * 512:lci * D + (hb + 1) * 512],
                        pbig[:, hb * 512:(hb + 1) * 512]), inc="vv")
                em.wait("tensor", "vv", em.c["vv"])
                mm_hidden(16384, hidq[:, lci * D:(lci + 1) * D])
                em.wait("tensor", "vv", em.c["vv"])

            for e_i in range(8):
                dbslot = 49152 if e_i % 2 == 0 else 57344
                em.wait("gpsimd", "tt", em.c["tt"])
                em.dma("gpsimd", wpool[:, dbslot:dbslot + 8192], dw2[e_i], "dg")
                w2l = em.c["dg"]
                em.wait("tensor", "dg", w2l)
                for ci in grp:
                    lci = ci - g0
                    hv = hidq[:, lci * D:(lci + 1) * D].rearrange(
                        "p (f t) -> p f t", f=FC)
                    for half in range(2):
                        for hc in range(FC):
                            em.op("tensor", lambda e, half=half, hc=hc,
                                  st=(hc == 0), sp=(hc == FC - 1), hv=hv,
                                  dbslot=dbslot: e.matmul(
                                      pbig[:, half * 512:(half + 1) * 512],
                                      hv[:, hc, :],
                                      wpool[:, dbslot + hc * 1024 + half * 512:
                                            dbslot + hc * 1024 + (half + 1) * 512],
                                      start=st, stop=sp),
                                  inc=("tt" if half == 1 and hc == FC - 1
                                       else None))
                    tex = em.c["tt"]
                    em.wait("vector", "tt", tex)
                    for hb in range(2):
                        em.op("vector", lambda e, lci=lci, ci=ci, e_i=e_i, hb=hb:
                              e.scalar_tensor_tensor(
                                  accq[:, lci * D + hb * 512:lci * D + (hb + 1) * 512],
                                  pbig[:, hb * 512:(hb + 1) * 512],
                                  wdense[:, ci * 8 + e_i:ci * 8 + e_i + 1],
                                  accq[:, lci * D + hb * 512:lci * D + (hb + 1) * 512],
                                  op0=ALU.mult, op1=ALU.add), inc="vv")
                    em.wait("tensor", "vv", em.c["vv"])

            for ci in grp:
                lci = ci - g0
                em.wait("scalar", "vv")
                em.wait("scalar", "dx")
                em.op("scalar", lambda e, lci=lci: e.activation(
                    tmpa[:], accq[:, lci * D:(lci + 1) * D], AF.Silu), inc="aa")
                agf = em.c["aa"]
                em.wait("sync", "vv")
                em.dma("sync", tmpb[:], dscr_s[ci * TP:(ci + 1) * TP, :], "dx")
                sxl = em.c["dx"]
                em.wait("vector", "aa", agf)
                em.wait("vector", "dx", sxl)
                em.op("vector", lambda e: e.tensor_tensor(
                    tmpa[:], tmpa[:], tmpb[:], op=ALU.mult), inc="vv")
                vsg = em.c["vv"]
                em.wait("sync", "vv", vsg)
                em.dma("sync", tmpb[:], dx[ci * TP:(ci + 1) * TP, :], "dx")
                xxl = em.c["dx"]
                em.wait("vector", "dx", xxl)
                em.op("vector", lambda e: e.tensor_tensor(
                    tmpa[:], tmpa[:], tmpb[:], op=ALU.add), inc="vv")
                vout = em.c["vv"]
                em.wait("sync", "vv", vout)
                em.dma("sync", dout[ci * TP:(ci + 1) * TP, :], tmpa[:], "dx")

        em.wait("sync", "dx", em.c["dx"])
        em.wait("sync", "dg", em.c["dg"])

        def replay(eng_name):
            def run(handle):
                for item in em.q[eng_name]:
                    if item[0] == "w":
                        handle.wait_ge(sems[item[1]], item[2])
                    else:
                        instr = item[1](handle)
                        for s, n in item[2]:
                            instr.then_inc(sems[s], n)
            return run

        block.sync(replay("sync"))
        block.tensor(replay("tensor"))
        block.vector(replay("vector"))
        block.scalar(replay("scalar"))
        block.gpsimd(replay("gpsimd"))

    return nc


def _prep(inputs):
    f32 = np.float32
    fmap = np.array([(fp % 16) * 64 + fp // 16 for fp in range(D)])

    def lhsT_tiles(Wt):
        return np.ascontiguousarray(
            Wt.reshape(8, 128, Wt.shape[1]).transpose(1, 0, 2).reshape(128, -1))

    g = lambda k: np.asarray(inputs[k], f32)
    mats = []
    for nm, perm in [("Wr", True), ("Wk", True), ("Wg", True), ("Wt", True),
                     ("Wv", False), ("Wa", False), ("Wb", False)]:
        W = g("L0_attn_" + nm)
        if perm:
            W = W[fmap]
        mats.append(lhsT_tiles(np.ascontiguousarray(W.T)))
    wproj = np.concatenate(mats, axis=1)
    wo = lhsT_tiles(np.ascontiguousarray(g("L0_attn_Wo").T[fmap]))
    w_in = lhsT_tiles(g("L0_ffn_w_shared_in"))
    w1 = lhsT_tiles(g("L0_ffn_w1_shared"))
    w_out = lhsT_tiles(g("L0_ffn_w_shared_out"))
    wgate = lhsT_tiles(g("L0_ffn_w_gate"))
    wffn = np.concatenate([w_in, w1, w_out, wgate], axis=1)
    w2 = np.stack([lhsT_tiles(g("L0_ffn_w2")[e]) for e in range(8)])
    lnw_rep = np.ascontiguousarray(
        np.broadcast_to(g("L0_attn_ln_weight")[None, :], (128, D)))
    gb_rep = np.ascontiguousarray(
        np.broadcast_to(g("L0_ffn_gate_bias")[None, :], (128, 8)))
    cw = g("L0_attn_conv_w")[:, 0, :]
    cw_f = np.ascontiguousarray(
        cw.reshape(8, 128, 4).transpose(1, 0, 2).reshape(128, 32))
    bdp = np.ascontiguousarray(g("L0_attn_base_decay")[fmap].reshape(8, 128).T)
    sel = np.zeros((16, 128), f32)
    for p in range(128):
        sel[p % 16, p] = 1.0
    ident = np.eye(128, dtype=f32)
    half = np.zeros((128, 2), f32)
    half[0:64, 0] = 1.0
    half[64:128, 1] = 1.0
    return dict(wproj=wproj, wo=wo, wffn=wffn, w2=w2, lnw_rep=lnw_rep,
                gb_rep=gb_rep, cw_f=cw_f, bdp=bdp, sel=sel, ident=ident,
                halfones=half)


def kernel(**inputs):
    x = np.asarray(inputs["L0_attn_x"], np.float32)
    B, L, _ = x.shape
    shared = _prep(inputs)
    nc = build_nc(L)
    in_maps = []
    for b in range(B):
        m = dict(shared)
        m["x"] = np.ascontiguousarray(x[b])
        in_maps.append(m)
    res = run_bass_kernel_spmd(nc, in_maps, core_ids=list(range(B)))
    return np.stack([res.results[b]["out"] for b in range(B)], axis=0)


# version 05
# speedup vs baseline: 1.2712x; 1.2281x over previous
import numpy as np
import ml_dtypes
import concourse.bass as bass
import concourse.mybir as mybir
from concourse.bass_utils import run_bass_kernel_spmd

F32 = mybir.dt.float32
BF16 = mybir.dt.bfloat16
AF = mybir.ActivationFunctionType
ALU = mybir.AluOpType
AX = mybir.AxisListType

D = 1024
H = 16
FC = 8
TP = 128
THJ = 16
TRS = 8
EPS_ST = 1e-5
EPS_RMS = 1e-6
SQS = float(2.0 ** -20)
SQS2 = float(2.0 ** -44)
SQB = float(2.0 ** 22 / 8.0)
SNB = float(2.0 ** 20 / 32.0)

ENGINES = ("sync", "tensor", "vector", "scalar", "gpsimd")


class Em:
    def __init__(self, sem_names):
        self.q = {k: [] for k in ENGINES}
        self.c = {k: 0 for k in sem_names}
        self.serialize = False

    def wait(self, eng, sem, val=None):
        if val is None:
            val = self.c[sem]
        if val > 0:
            self.q[eng].append(("w", sem, val))

    def op(self, eng, fn, inc=None):
        if self.serialize:
            for sname in self.c:
                self.wait(eng, sname)
        incs = ()
        if inc is not None:
            incs = ((inc, 1),)
            self.c[inc] += 1
        self.q[eng].append(("op", fn, incs))

    def dma(self, eng, out, in_, sem):
        if eng == "sync":
            self.wait("sync", sem)
        def fn(e, out=out, in_=in_):
            return e.dma_start(out=out, in_=in_)
        self.q[eng].append(("op", fn, ((sem, 16),)))
        self.c[sem] += 16


def build_nc(L=2048):
    NCH = L // TP
    NSUB = TP // THJ
    groups = [list(range(g, min(g + 2, NCH))) for g in range(0, NCH, 2)]
    nc = bass.Bass("TRN2", target_bir_lowering=False,
                   detect_race_conditions=False)

    dx = nc.dram_tensor("x", [L, D], F32, kind="ExternalInput").ap()
    dwproj = nc.dram_tensor("wproj", [128, 7 * 8192], F32, kind="ExternalInput").ap()
    dwo = nc.dram_tensor("wo", [128, 8192], F32, kind="ExternalInput").ap()
    dwffn = nc.dram_tensor("wffn", [128, 41024], F32, kind="ExternalInput").ap()
    dw2 = nc.dram_tensor("w2", [8, 128, 8192], F32, kind="ExternalInput").ap()
    dlnw = nc.dram_tensor("lnw_rep", [128, D], F32, kind="ExternalInput").ap()
    dgb = nc.dram_tensor("gb_rep", [128, 8], F32, kind="ExternalInput").ap()
    dcw = nc.dram_tensor("cw_f", [128, 32], F32, kind="ExternalInput").ap()
    dbdp = nc.dram_tensor("bdp", [128, 8], F32, kind="ExternalInput").ap()
    dsel = nc.dram_tensor("sel", [16, 128], F32, kind="ExternalInput").ap()
    dident = nc.dram_tensor("ident", [128, 128], F32, kind="ExternalInput").ap()
    dhalf = nc.dram_tensor("halfones", [128, 2], F32, kind="ExternalInput").ap()
    dout = nc.dram_tensor("out", [L, D], F32, kind="ExternalOutput").ap()
    dscr_s = nc.dram_tensor("scr_s", [L, D], F32).ap()
    dscr_xnb = nc.dram_tensor("scr_xnb", [NCH, 128, D], BF16).ap()
    dscr_cv = nc.dram_tensor("scr_cv", [3, D, TP], F32).ap()

    sem_names = ["dx", "dg", "tt", "vv", "aa"]

    from contextlib import ExitStack
    with ExitStack() as _st:
        wpool = _st.enter_context(nc.sbuf_tensor("wpool", [128, 65536], BF16))
        xn_tok = _st.enter_context(nc.sbuf_tensor("xn_tok", [128, D], F32))
        xn_fm = _st.enter_context(nc.sbuf_tensor("xn_fm", [128, FC * (TP + 3)], F32))
        kkfm = _st.enter_context(nc.sbuf_tensor("kkfm", [128, D], F32))
        cfm = _st.enter_context(nc.sbuf_tensor("cfm", [128, D], F32))
        vfm = _st.enter_context(nc.sbuf_tensor("vfm", [128, D], F32))
        winv = _st.enter_context(nc.sbuf_tensor("winv", [128, D], F32))
        kwb = _st.enter_context(nc.sbuf_tensor("kwb", [128, D], F32))
        tmpa = _st.enter_context(nc.sbuf_tensor("tmpa", [128, D], F32))
        rsbuf = _st.enter_context(nc.sbuf_tensor("rsbuf", [128, FC * TP], F32))
        lambuf = _st.enter_context(nc.sbuf_tensor("lambuf", [128, FC * TP], F32))
        zst = _st.enter_context(nc.sbuf_tensor("zst", [128, 512], F32))
        sm = _st.enter_context(nc.sbuf_tensor("sm", [128, 16], F32))
        sm2 = _st.enter_context(nc.sbuf_tensor("sm2", [128, 16], F32))
        sm3 = _st.enter_context(nc.sbuf_tensor("sm3", [128, 32], F32))
        sm4 = _st.enter_context(nc.sbuf_tensor("sm4", [128, 16], F32))
        sctmp = _st.enter_context(nc.sbuf_tensor("sctmp", [128, 64], F32))
        cwf = _st.enter_context(nc.sbuf_tensor("cwf", [128, 32], F32))
        bdps = _st.enter_context(nc.sbuf_tensor("bdps", [128, 8], F32))
        lnw = _st.enter_context(nc.sbuf_tensor("lnw", [128, D], BF16))
        gbr = _st.enter_context(nc.sbuf_tensor("gbr", [128, 8], F32))
        self32 = _st.enter_context(nc.sbuf_tensor("self32", [16, 128], F32))
        ident32 = _st.enter_context(nc.sbuf_tensor("ident32", [128, 128], F32))
        half2 = _st.enter_context(nc.sbuf_tensor("half2", [128, 2], F32))
        upool = _st.enter_context(nc.sbuf_tensor("upool", [128, 2 * 194 * THJ], F32))
        xnfmb = _st.enter_context(nc.sbuf_tensor("xnfmb", [128, D], BF16))
        up = upool[:]
        HJC = 194 * THJ
        hjbufs_ap = (up[0:16, 0:HJC], up[0:16, HJC:2 * HJC])
        hidq = up[:, 0:1024].bitcast(BF16)
        accq = up[:, 1024:3072]
        swb = up[:, 3072:3584].bitcast(BF16)
        satmp = up[:, 3584:4608]
        wdense = up[:, 4608:4608 + NCH * 8]
        tmpb = vfm
        gob = cfm[:, 0:512].bitcast(BF16)
        ebd = bdps
        pm = _st.enter_context(nc.psum_tensor("pm", [128, 512], F32))
        pm2 = _st.enter_context(nc.psum_tensor("pm2", [128, 512], F32))
        t1b = _st.enter_context(nc.psum_tensor("t1b", [128, 512], F32))
        prep0 = _st.enter_context(nc.psum_tensor("prep0", [128, 512], F32))
        prep1 = _st.enter_context(nc.psum_tensor("prep1", [128, 512], F32))
        pbig = _st.enter_context(nc.psum_tensor("pbig", [128, 1024], F32))
        paux = _st.enter_context(nc.psum_tensor("paux", [128, 512], F32))
        s_dx = _st.enter_context(nc.semaphore("s_dx"))
        s_dg = _st.enter_context(nc.semaphore("s_dg"))
        s_tt = _st.enter_context(nc.semaphore("s_tt"))
        s_vv = _st.enter_context(nc.semaphore("s_vv"))
        s_aa = _st.enter_context(nc.semaphore("s_aa"))
        block = _st.enter_context(nc.Block())
        sems = {"dx": s_dx, "dg": s_dg, "tt": s_tt, "vv": s_vv, "aa": s_aa}
        em = Em(sem_names)

        zv = zst[:].rearrange("p (f j) -> p f j", f=FC)
        rsb = rsbuf[:].rearrange("p (t f) -> p t f", f=FC)
        lmb = lambuf[:].rearrange("p (t f) -> p t f", f=FC)
        xnfm = xn_fm[:].rearrange("p (f t) -> p f t", f=FC)
        winvv = winv[:].rearrange("p (f t) -> p f t", f=FC)
        kwv = kwb[:].rearrange("p (f t) -> p f t", f=FC)

        rs_trk = sm[:, 0:8]
        dnegs = (sm[:, 8:16], sm4[:, 0:8])
        sqvs = (sm3[:, 8:16], sm4[:, 8:16])
        p_sc = sm2[:, 0:8]
        q_sc = sm2[:, 8:16]
        mue = sm3[:, 0:8]
        sqv = sm3[:, 8:16]
        linv = sm3[:, 16:24]
        rs_tmp = sm3[:, 24:32]

        def bc8(ap2d, n=64):
            return ap2d.unsqueeze(2).broadcast_to([128, 8, n])

        def bc64(ap2d):
            return ap2d.unsqueeze(1).broadcast_to([128, 8, 64])

        for dst, src in [(cwf[:], dcw), (bdps[:], dbdp),
                         (gbr[:], dgb), (self32[:], dsel), (ident32[:], dident),
                         (half2[:], dhalf)]:
            em.dma("sync", dst, src, "dx")
        const_ld = em.c["dx"]
        em.dma("gpsimd", lnw[:], dlnw, "dg")
        for i in range(7):
            em.dma("gpsimd", wpool[:, i * 8192:(i + 1) * 8192],
                   dwproj[:, i * 8192:(i + 1) * 8192], "dg")
        em.dma("gpsimd", wpool[:, 7 * 8192:8 * 8192], dwo, "dg")
        attn_w_ld = em.c["dg"]

        em.wait("scalar", "dx", const_ld)
        em.op("scalar", lambda e: e.activation(bdps[:], bdps[:], AF.Exp,
                                               scale=-1.0), inc="aa")
        a_ebd = em.c["aa"]
        em.op("vector", lambda e: e.memset(zst[:], 0.0), inc="vv")
        em.op("vector", lambda e: e.memset(rs_trk, 0.0), inc="vv")
        em.wait("vector", "dx", const_ld)
        em.wait("vector", "dg", attn_w_ld)
        em.wait("vector", "aa", a_ebd)
        em.wait("tensor", "dx", const_ld)
        em.wait("tensor", "dg", attn_w_ld)

        wslice = {}
        for i, nm in enumerate(["r", "k", "g", "wt", "v", "a", "kk"]):
            wslice[nm] = wpool[:, i * 8192:(i + 1) * 8192]
        wslice["wo"] = wpool[:, 7 * 8192:8 * 8192]

        rep_free = {0: 0, 1: 0}
        pend_s6 = [None]

        for ci in range(NCH):
            em.wait("sync", "vv")
            em.dma("sync", tmpb[:], dx[ci * TP:(ci + 1) * TP, :], "dx")
            xld = em.c["dx"]
            em.wait("scalar", "vv")
            em.wait("scalar", "dx", xld)
            em.op("scalar", lambda e: e.activation(tmpa[:], tmpb[:], AF.Square,
                                                   accum_out=sm3[:, 0:1]), inc="aa")
            em.op("scalar", lambda e: e.activation(xn_tok[:], tmpb[:], AF.Identity,
                                                   accum_out=sm3[:, 1:2]), inc="aa")
            aln = em.c["aa"]
            em.wait("vector", "aa", aln)
            em.op("vector", lambda e: e.tensor_scalar(
                sm3[:, 2:3], sm3[:, 1:2], float(1.0 / D), None, op0=ALU.mult), inc="vv")
            em.op("vector", lambda e: e.tensor_scalar(
                sm3[:, 3:4], sm3[:, 0:1], float(1.0 / D), None, op0=ALU.mult), inc="vv")
            em.op("vector", lambda e: e.tensor_tensor(
                sm3[:, 4:5], sm3[:, 2:3], sm3[:, 2:3], op=ALU.mult), inc="vv")
            em.wait("vector", "vv")
            em.op("vector", lambda e: e.tensor_tensor(
                sm3[:, 5:6], sm3[:, 3:4], sm3[:, 4:5], op=ALU.subtract), inc="vv")
            em.wait("vector", "vv")
            em.op("vector", lambda e: e.tensor_scalar(
                sm3[:, 5:6], sm3[:, 5:6], 1e-5, None, op0=ALU.add), inc="vv")
            vvar = em.c["vv"]
            em.wait("scalar", "vv", vvar)
            em.op("scalar", lambda e: e.activation(sm3[:, 6:7], sm3[:, 5:6],
                                                   AF.Sqrt), inc="aa")
            asd = em.c["aa"]
            em.wait("vector", "aa", asd)
            em.op("vector", lambda e: e.reciprocal(sm3[:, 7:8], sm3[:, 6:7]), inc="vv")
            em.wait("vector", "vv")
            em.op("vector", lambda e: e.tensor_scalar(
                xn_tok[:], tmpb[:], sm3[:, 2:3], sm3[:, 7:8],
                op0=ALU.subtract, op1=ALU.mult), inc="vv")
            vxn = em.c["vv"]

            em.wait("tensor", "vv", vxn)
            for fc in range(FC):
                em.op("tensor", lambda e, fc=fc: e.transpose(
                    pbig[:, fc * 128:(fc + 1) * 128],
                    xn_tok[:, fc * 128:(fc + 1) * 128], ident32[:]),
                    inc=("tt" if fc == FC - 1 else None))
            ttr = em.c["tt"]
            em.wait("vector", "tt", ttr)
            if ci == 0:
                em.op("vector", lambda e: e.memset(xn_fm[:], 0.0), inc="vv")
            else:
                em.op("vector", lambda e: e.tensor_copy(
                    xnfm[:, :, 0:3], xnfm[:, :, TP:TP + 3]), inc="vv")
            for hb in range(2):
                em.op("vector", lambda e, hb=hb: e.tensor_copy(
                    xnfm[:, hb * 4:hb * 4 + 4, 3:TP + 3],
                    pbig[:, hb * 512:(hb + 1) * 512].rearrange(
                        "p (f t) -> p f t", f=4)), inc="vv")
            vxnfm = em.c["vv"]
            em.wait("scalar", "tt", ttr)
            em.wait("scalar", "dx")
            em.op("vector", lambda e: e.tensor_copy(
                xnfmb[:, 0:512], pbig[:, 0:512]), inc="aa")
            em.op("vector", lambda e: e.tensor_copy(
                xnfmb[:, 512:1024], pbig[:, 512:1024]), inc="aa")
            axnb = em.c["aa"]
            em.wait("sync", "aa", axnb)
            em.dma("sync", dscr_xnb[ci], xnfmb[:], "dx")

            cw3 = cwf[:].rearrange("p (f k) -> p f k", f=FC)
            em.op("vector", lambda e: e.tensor_tensor(
                cfm[:].rearrange("p (f t) -> p f t", f=FC),
                xnfm[:, :, 0:TP],
                cw3[:, :, 0:1].broadcast_to([128, FC, TP]), op=ALU.mult), inc="vv")
            for j in range(1, 4):
                em.op("vector", lambda e, j=j: e.tensor_tensor(
                    tmpa[:].rearrange("p (f t) -> p f t", f=FC),
                    xnfm[:, :, j:TP + j],
                    cw3[:, :, j:j + 1].broadcast_to([128, FC, TP]),
                    op=ALU.mult), inc="vv")
                em.op("vector", lambda e: e.tensor_tensor(
                    cfm[:], cfm[:], tmpa[:], op=ALU.add), inc="vv")
            vxc = em.c["vv"]
            em.wait("tensor", "vv", vxc)
            for fc in range(FC):
                em.op("tensor", lambda e, fc=fc: e.transpose(
                    pbig[:, fc * 128:(fc + 1) * 128],
                    cfm[:, fc * 128:(fc + 1) * 128], ident32[:]),
                    inc=("tt" if fc == FC - 1 else None))
            ttr2 = em.c["tt"]
            em.wait("vector", "tt", ttr2)
            for hb in range(2):
                em.op("vector", lambda e, hb=hb: e.scalar_tensor_tensor(
                    xn_tok[:, hb * 512:(hb + 1) * 512],
                    pbig[:, hb * 512:(hb + 1) * 512], 0.1,
                    xn_tok[:, hb * 512:(hb + 1) * 512],
                    op0=ALU.mult, op1=ALU.add), inc="vv")
            vxg = em.c["vv"]

            def proj(nm):
                ws = wslice[nm]
                for fc in range(FC):
                    for dc in range(FC):
                        lhsT = ws[:, dc * 1024 + fc * 128:dc * 1024 + (fc + 1) * 128]
                        rhs = xnfmb[:].rearrange("p (f t) -> p f t", f=FC)[:, dc, :]
                        em.op("tensor", lambda e, lhsT=lhsT, rhs=rhs, fc=fc,
                              st=(dc == 0), sp=(dc == FC - 1):
                              e.matmul(pbig[:, fc * 128:(fc + 1) * 128], lhsT,
                                       rhs, start=st, stop=sp),
                              inc=("tt" if dc == FC - 1 and fc == FC - 1 else None))
                return em.c["tt"]

            em.wait("tensor", "aa", axnb)
            em.wait("tensor", "vv", vxg)
            twt = proj("wt")
            em.wait("scalar", "vv")
            em.wait("scalar", "dx")
            em.wait("scalar", "tt", twt)
            em.op("scalar", lambda e: e.activation(
                tmpa[:, 0:512], pbig[:, 0:512], AF.Exp), inc="aa")
            em.op("scalar", lambda e: e.activation(
                tmpa[:, 512:1024], pbig[:, 512:1024], AF.Exp), inc="aa")
            em.op("scalar", lambda e: e.activation(tmpb[:], tmpa[:], AF.Exp,
                                                   scale=-1.0), inc="aa")
            awt = em.c["aa"]
            em.wait("vector", "aa", awt)
            em.op("vector", lambda e: e.tensor_tensor(
                tmpa[:].rearrange("p (f t) -> p f t", f=FC),
                tmpb[:].rearrange("p (f t) -> p f t", f=FC),
                bc8(ebd[:], TP), op=ALU.mult), inc="vv")
            em.op("vector", lambda e: e.tensor_scalar(
                tmpa[:], tmpa[:], 0.01, 0.99, op0=ALU.max, op1=ALU.min), inc="vv")
            em.op("vector", lambda e: e.reciprocal(winv[:], tmpa[:]), inc="vv")
            vwt = em.c["vv"]

            em.wait("tensor", "vv", vwt)
            tk = proj("k")
            em.wait("vector", "tt", tk)
            for hb in range(2):
                em.op("vector", lambda e, hb=hb: e.tensor_tensor(
                    kwb[:, hb * 512:(hb + 1) * 512],
                    pbig[:, hb * 512:(hb + 1) * 512],
                    winv[:, hb * 512:(hb + 1) * 512], op=ALU.mult), inc="vv")
            vk = em.c["vv"]

            em.wait("tensor", "vv", vk)
            tkk = proj("kk")
            em.wait("vector", "tt", tkk)
            for hb in range(2):
                em.op("vector", lambda e, hb=hb: e.tensor_copy(
                    kkfm[:, hb * 512:(hb + 1) * 512],
                    pbig[:, hb * 512:(hb + 1) * 512]), inc="vv")
            vkk = em.c["vv"]

            em.wait("tensor", "vv", vkk)
            ta = proj("a")
            em.wait("vector", "tt", ta)
            for hb in range(2):
                em.op("vector", lambda e, hb=hb: e.tensor_tensor(
                    cfm[:, hb * 512:(hb + 1) * 512],
                    kkfm[:, hb * 512:(hb + 1) * 512],
                    pbig[:, hb * 512:(hb + 1) * 512], op=ALU.mult), inc="vv")
            vc = em.c["vv"]

            em.wait("tensor", "vv", vc)
            tv = proj("v")
            em.wait("vector", "tt", tv)
            for hb in range(2):
                em.op("vector", lambda e, hb=hb: e.tensor_copy(
                    vfm[:, hb * 512:(hb + 1) * 512],
                    pbig[:, hb * 512:(hb + 1) * 512]), inc="vv")
            vv_ = em.c["vv"]

            em.wait("tensor", "vv", vv_)
            for t_i, src in enumerate((cfm, vfm)):
                for fc in range(FC):
                    em.op("tensor", lambda e, src=src, fc=fc, t_i=t_i: e.matmul(
                        paux[:, t_i * 16 + fc * 2:t_i * 16 + fc * 2 + 2],
                        src[:, fc * 128:(fc + 1) * 128], half2[:],
                        start=True, stop=True),
                        inc=("tt" if t_i == 1 and fc == FC - 1 else None))
            tsc = em.c["tt"]
            em.wait("vector", "tt", tsc)
            if ci == 0:
                em.op("vector", lambda e: e.memset(sctmp[:], 0.0), inc="vv")
            em.op("vector", lambda e: e.tensor_copy(sctmp[:, 0:16], paux[:, 0:16]),
                  inc="vv")
            em.op("vector", lambda e: e.tensor_copy(sctmp[:, 32:48], paux[:, 16:32]),
                  inc="vv")
            vsc = em.c["vv"]
            em.wait("tensor", "vv", vsc)
            em.op("tensor", lambda e: e.transpose(
                paux[0:64, 0:128], sctmp[:], ident32[:]), inc="tt")
            tsc2 = em.c["tt"]

            em.wait("sync", "vv", vv_)
            em.dma("sync", dscr_cv[0].rearrange("(fc p) t -> p fc t", p=128),
                   kkfm[:].rearrange("p (fc t) -> p fc t", fc=FC), "dx")
            em.dma("sync", dscr_cv[1].rearrange("(fc p) t -> p fc t", p=128),
                   cfm[:].rearrange("p (fc t) -> p fc t", fc=FC), "dx")
            em.dma("sync", dscr_cv[2].rearrange("(fc p) t -> p fc t", p=128),
                   vfm[:].rearrange("p (fc t) -> p fc t", fc=FC), "dx")
            cvout = em.c["dx"]
            rg = kkfm

            em.wait("tensor", "vv", vv_)
            tr = proj("r")
            em.wait("scalar", "tt", tr)
            em.wait("scalar", "dx", cvout)
            em.op("scalar", lambda e, rg=rg: e.activation(
                rg[:, 0:512], pbig[:, 0:512], AF.Sigmoid), inc="aa")
            em.op("scalar", lambda e, rg=rg: e.activation(
                rg[:, 512:1024], pbig[:, 512:1024], AF.Sigmoid), inc="aa")
            ar = em.c["aa"]

            em.wait("tensor", "aa", ar)
            tg = proj("g")
            em.wait("scalar", "tt", tg)
            em.wait("scalar", "vv")
            em.op("scalar", lambda e: e.activation(
                tmpb[:, 0:512], pbig[:, 0:512], AF.Sigmoid), inc="aa")
            em.op("scalar", lambda e: e.activation(
                tmpb[:, 512:1024], pbig[:, 512:1024], AF.Sigmoid), inc="aa")
            ag = em.c["aa"]
            em.wait("vector", "aa", ag)
            em.op("vector", lambda e, rg=rg: e.tensor_tensor(
                rg[:], rg[:], tmpb[:], op=ALU.mult), inc="vv")
            vrg = em.c["vv"]

            hjbufs = hjbufs_ap
            hj_dx = {}
            hj_vv = {}
            sel_done = {}

            def emit_hj_load(sub):
                hj = hjbufs[sub % 2]
                hjv = hj.rearrange("h (c t) -> h c t", c=194)
                em.wait("sync", "dx", cvout)
                if sub >= 2:
                    em.wait("sync", "tt", sel_done[sub - 2])
                for vi in range(3):
                    src = dscr_cv[vi].rearrange("(h j) t -> h j t", h=H)[
                        :, :, sub * THJ:(sub + 1) * THJ]
                    em.dma("sync", hjv[:, vi * 64:(vi + 1) * 64, :], src, "dx")
                hj_dx[sub] = em.c["dx"]
                em.wait("vector", "tt", tsc2)
                em.op("vector", lambda e, hjv=hjv, sub=sub: e.tensor_copy(
                    hjv[:, 192:193, :],
                    paux[0:16, sub * THJ:(sub + 1) * THJ].unsqueeze(1)), inc="vv")
                em.op("vector", lambda e, hjv=hjv, sub=sub: e.tensor_copy(
                    hjv[:, 193:194, :],
                    paux[32:48, sub * THJ:(sub + 1) * THJ].unsqueeze(1)), inc="vv")
                hj_vv[sub] = em.c["vv"]

            emit_hj_load(0)
            if NSUB > 1:
                emit_hj_load(1)

            for sub in range(NSUB):
                hj = hjbufs[sub % 2]
                hjv = hj.rearrange("h (c t) -> h c t", c=194)
                for ts in range(THJ):
                    tt_ = sub * THJ + ts
                    gt = ci * TP + tt_
                    par = gt % 2
                    prep = (prep0, prep1)[par]
                    dneg = dnegs[par]
                    sqv = sqvs[par]
                    after_rs = (gt % TRS == 0)

                    em.wait("tensor", "dx", hj_dx[sub])
                    em.wait("tensor", "vv", hj_vv[sub])
                    if rep_free[par] > 0:
                        em.wait("tensor", "vv", rep_free[par])
                    em.op("tensor", lambda e, prep=prep, hjv=hjv, ts=ts: e.matmul(
                        prep[:, 0:194], self32[:], hjv[:, :, ts],
                        start=True, stop=True), inc="tt")
                    trep = em.c["tt"]

                    wsl = winvv[:, :, tt_]
                    kwsl = kwv[:, :, tt_]

                    em.wait("vector", "tt", trep)
                    em.op("vector", lambda e, prep=prep: e.tensor_tensor(
                        pm[:].rearrange("p (f j) -> p f j", f=FC), zv,
                        bc64(prep[:, 0:64]), op=ALU.mult), inc="vv")
                    em.op("vector", lambda e, dneg=dneg: e.tensor_reduce(
                        dneg, pm[:].rearrange("p (f j) -> p f j", f=FC),
                        axis=AX.X, op=ALU.add, negate=True), inc="vv")
                    if pend_s6[0] is not None:
                        pend_s6[0]()
                        pend_s6[0] = None
                    else:
                        em.wait("vector", "vv")
                    em.op("vector", lambda e, wsl=wsl, dneg=dneg: e.tensor_tensor(
                        p_sc, dneg, wsl, op=ALU.mult), inc="vv")
                    if after_rs:
                        qop = kwsl
                        em.wait("vector", "vv")
                    else:
                        lprev = lmb[:, tt_ - 1, :]
                        em.op("vector", lambda e, lprev=lprev, kwsl=kwsl:
                              e.tensor_tensor(q_sc, lprev, kwsl, op=ALU.mult),
                              inc="vv")
                        qop = q_sc
                    em.op("vector", lambda e, prep=prep: e.tensor_tensor(
                        t1b[:].rearrange("p (f j) -> p f j", f=FC),
                        bc64(prep[:, 64:128]), bc8(p_sc), op=ALU.mult), inc="vv")
                    em.op("vector", lambda e: e.tensor_tensor(
                        zst[:], zst[:], t1b[:], op=ALU.add), inc="vv")
                    em.op("vector", lambda e, prep=prep, qop=qop: e.tensor_tensor(
                        t1b[:].rearrange("p (f j) -> p f j", f=FC),
                        bc64(prep[:, 128:192]), bc8(qop), op=ALU.mult), inc="vv")
                    rs_prev = rs_trk if after_rs else rsb[:, tt_ - 1, :]
                    em.op("vector", lambda e, prep=prep, rs_prev=rs_prev:
                          e.scalar_tensor_tensor(rs_tmp, p_sc, prep[:, 192:193],
                                                 rs_prev, op0=ALU.mult,
                                                 op1=ALU.add), inc="vv")
                    em.op("vector", lambda e: e.tensor_tensor(
                        zst[:], zst[:], t1b[:], op=ALU.add), inc="vv")
                    em.op("vector", lambda e, prep=prep, qop=qop, tt_=tt_:
                          e.scalar_tensor_tensor(rsb[:, tt_, :], qop,
                                                 prep[:, 193:194], rs_tmp,
                                                 op0=ALU.mult, op1=ALU.add),
                          inc="vv")
                    rep_free[par] = em.c["vv"]
                    em.op("vector", lambda e: e.scalar_tensor_tensor(
                        pm2[:], zst[:], SQS2, zst[:], op0=ALU.mult, op1=ALU.mult),
                        inc="vv")
                    em.op("vector", lambda e, dneg=dneg: e.tensor_reduce(
                        dneg, pm2[:].rearrange("p (f j) -> p f j", f=FC),
                        axis=AX.X, op=ALU.add), inc="vv")
                    vnsq = em.c["vv"]
                    if after_rs:
                        em.op("vector", lambda e, wsl=wsl: e.tensor_scalar(
                            mue, wsl, EPS_ST, None, op0=ALU.mult), inc="vv")
                    else:
                        lprev = lmb[:, tt_ - 1, :]
                        em.op("vector", lambda e, lprev=lprev, wsl=wsl:
                              e.scalar_tensor_tensor(mue, lprev, EPS_ST, wsl,
                                                     op0=ALU.mult, op1=ALU.mult),
                              inc="vv")
                    em.wait("scalar", "vv", vnsq)
                    em.op("scalar", lambda e, sqv=sqv, dneg=dneg: e.activation(
                        sqv, dneg, AF.Sqrt), inc="aa")
                    asv = em.c["aa"]

                    def make_s6(tt_=tt_, asv=asv, sqv=sqv):
                        def emit():
                            em.wait("vector", "aa", asv)
                            em.op("vector", lambda e, tt_=tt_, sqv=sqv:
                                  e.scalar_tensor_tensor(
                                      lmb[:, tt_, :], sqv, SQB, mue,
                                      op0=ALU.mult, op1=ALU.add), inc="vv")
                        return emit

                    if gt % TRS == TRS - 1:
                        make_s6()()
                        em.wait("vector", "vv")
                        em.op("vector", lambda e, tt_=tt_: e.reciprocal(
                            linv, lmb[:, tt_, :]), inc="vv")
                        em.wait("vector", "vv")
                        em.op("vector", lambda e: e.tensor_tensor(
                            zv, zv, bc8(linv), op=ALU.mult), inc="vv")
                        em.op("vector", lambda e, tt_=tt_: e.tensor_tensor(
                            rs_trk, rsb[:, tt_, :], linv, op=ALU.mult), inc="vv")
                    else:
                        pend_s6[0] = make_s6()
                sel_done[sub] = em.c["tt"]
                if sub + 2 < NSUB:
                    emit_hj_load(sub + 2)

            if pend_s6[0] is not None:
                pend_s6[0]()
                pend_s6[0] = None
            em.wait("vector", "vv")
            em.wait("vector", "dx", cvout)
            em.op("vector", lambda e: e.reciprocal(tmpa[:], lambuf[:]), inc="vv")
            em.op("vector", lambda e: e.tensor_tensor(
                tmpb[:], rsbuf[:], tmpa[:], op=ALU.mult), inc="vv")
            em.op("vector", lambda e, rg=rg: e.tensor_tensor(
                gob[:, 0:FC * TP].rearrange("p (f t) -> p f t", f=FC),
                tmpb[:].rearrange("p (t f) -> p f t", f=FC),
                rg[:].rearrange("p (f t) -> p f t", f=FC), op=ALU.mult), inc="vv")
            vgo = em.c["vv"]

            em.wait("tensor", "vv", vgo)
            gov = gob[:, 0:FC * TP].rearrange("p (f t) -> p f t", f=FC)
            for half in range(2):
                for fcp in range(FC):
                    em.op("tensor", lambda e, half=half, fcp=fcp,
                          st=(fcp == 0), sp=(fcp == FC - 1): e.matmul(
                              pbig[:, half * 512:(half + 1) * 512],
                              gov[:, fcp, :],
                              wslice["wo"][:, fcp * 1024 + half * 512:
                                           fcp * 1024 + (half + 1) * 512],
                              start=st, stop=sp),
                          inc=("tt" if half == 1 and fcp == FC - 1 else None))
            two = em.c["tt"]
            em.wait("vector", "tt", two)
            for hb in range(2):
                em.op("vector", lambda e, hb=hb: e.tensor_copy(
                    tmpb[:, hb * 512:(hb + 1) * 512],
                    pbig[:, hb * 512:(hb + 1) * 512]), inc="vv")
            vwoe = em.c["vv"]
            em.wait("scalar", "vv", vwoe)
            em.op("scalar", lambda e: e.activation(
                tmpa[:], tmpb[:], AF.Square, scale=SQS,
                accum_out=sm3[:, 0:1]), inc="aa")
            em.op("scalar", lambda e: e.activation(sm3[:, 1:2], sm3[:, 0:1],
                                                   AF.Sqrt), inc="aa")
            aso = em.c["aa"]
            em.wait("vector", "aa", aso)
            em.op("vector", lambda e: e.tensor_scalar(
                sm3[:, 2:3], sm3[:, 1:2], SNB, EPS_RMS,
                op0=ALU.mult, op1=ALU.add), inc="vv")
            em.wait("vector", "vv")
            em.op("vector", lambda e: e.reciprocal(sm3[:, 3:4], sm3[:, 2:3]),
                  inc="vv")
            em.wait("vector", "vv")
            em.op("vector", lambda e: e.scalar_tensor_tensor(
                tmpa[:], tmpb[:], sm3[:, 3:4], lnw[:], op0=ALU.mult,
                op1=ALU.mult), inc="vv")
            em.op("vector", lambda e: e.tensor_tensor(
                xn_tok[:], tmpa[:], xn_tok[:], op=ALU.add), inc="vv")
            vs = em.c["vv"]
            em.wait("sync", "vv", vs)
            em.dma("sync", dscr_s[ci * TP:(ci + 1) * TP, :], xn_tok[:], "dx")

        em.wait("gpsimd", "tt", em.c["tt"])
        em.wait("gpsimd", "aa", em.c["aa"])
        em.dma("gpsimd", wpool[:, 0:16384], dwffn[:, 0:16384], "dg")
        em.dma("gpsimd", wpool[:, 16384:32768], dwffn[:, 16384:32768], "dg")
        em.dma("gpsimd", wpool[:, 32768:40960], dwffn[:, 32768:40960], "dg")
        em.dma("gpsimd", wpool[:, 40960:41024], dwffn[:, 40960:41024], "dg")
        ffnw = em.c["dg"]
        em.wait("tensor", "dg", ffnw)
        wgate_sl = wpool[:, 40960:41024]

        lg = sm2[:, 0:8]
        ex = sm2[:, 8:16]
        mask1 = sctmp[:, 0:8]
        masked = sctmp[:, 8:16]
        keep = sctmp[:, 16:24]

        for grp in groups:
            g0 = grp[0]
            for ci in grp:
                lci = ci - g0
                em.wait("sync", "tt")
                em.dma("sync", xnfmb[:], dscr_xnb[ci], "dx")
                xbl = em.c["dx"]
                em.wait("tensor", "dx", xbl)
                xnv = xnfmb[:].rearrange("p (f t) -> p f t", f=FC)
                em.wait("tensor", "vv")
                for dc in range(FC):
                    em.op("tensor", lambda e, dc=dc, st=(dc == 0),
                          sp=(dc == FC - 1): e.matmul(
                              paux[:, 0:8], xnv[:, dc, :],
                              wgate_sl[:, dc * 8:(dc + 1) * 8], start=st, stop=sp),
                          inc=("tt" if dc == FC - 1 else None))
                tgt = em.c["tt"]
                em.wait("vector", "tt", tgt)
                em.op("vector", lambda e: e.tensor_tensor(
                    lg, paux[:, 0:8], gbr[:], op=ALU.add), inc="vv")
                em.wait("vector", "vv")
                em.op("vector", lambda e: e.tensor_reduce(
                    sm3[:, 0:1], lg, axis=AX.X, op=ALU.max, negate=True), inc="vv")
                vlg = em.c["vv"]
                em.wait("scalar", "vv", vlg)
                em.op("scalar", lambda e: e.activation(
                    ex, lg, AF.Exp, bias=sm3[:, 0:1]), inc="aa")
                age = em.c["aa"]
                em.wait("vector", "aa", age)
                em.op("vector", lambda e: e.tensor_reduce(
                    sm3[:, 1:2], ex, axis=AX.X, op=ALU.add), inc="vv")
                em.wait("vector", "vv")
                em.op("vector", lambda e: e.reciprocal(sm3[:, 2:3], sm3[:, 1:2]),
                      inc="vv")
                em.op("vector", lambda e: e.tensor_reduce(
                    sm3[:, 3:4], lg, axis=AX.X, op=ALU.max), inc="vv")
                em.wait("vector", "vv")
                em.op("vector", lambda e: e.tensor_single_scalar(
                    mask1, lg, sm3[:, 3:4], op=ALU.is_ge), inc="vv")
                em.wait("vector", "vv")
                em.op("vector", lambda e: e.scalar_tensor_tensor(
                    masked, mask1, -1e30, lg, op0=ALU.mult, op1=ALU.add), inc="vv")
                em.wait("vector", "vv")
                em.op("vector", lambda e: e.tensor_reduce(
                    sm3[:, 4:5], masked, axis=AX.X, op=ALU.max), inc="vv")
                em.wait("vector", "vv")
                em.op("vector", lambda e: e.tensor_single_scalar(
                    keep, lg, sm3[:, 4:5], op=ALU.is_ge), inc="vv")
                em.wait("vector", "vv")
                em.op("vector", lambda e: e.tensor_tensor(
                    keep, keep, ex, op=ALU.mult), inc="vv")
                em.wait("vector", "vv")
                em.op("vector", lambda e, ci=ci: e.tensor_scalar(
                    wdense[:, ci * 8:(ci + 1) * 8], keep, sm3[:, 2:3], None,
                    op0=ALU.mult), inc="vv")

                def mm_hidden(base_col, dst_bf):
                    for fc in range(FC):
                        for dc in range(FC):
                            em.op("tensor", lambda e, fc=fc, dc=dc,
                                  st=(dc == 0), sp=(dc == FC - 1),
                                  base_col=base_col: e.matmul(
                                      pbig[:, fc * 128:(fc + 1) * 128],
                                      wpool[:, base_col + dc * 2048 + fc * 128:
                                            base_col + dc * 2048 + (fc + 1) * 128],
                                      xnv[:, dc, :], start=st, stop=sp),
                                  inc=("tt" if dc == FC - 1 and fc == FC - 1
                                       else None))
                    t_a = em.c["tt"]
                    em.wait("scalar", "vv")
                    em.wait("scalar", "tt", t_a)
                    em.op("scalar", lambda e: e.activation(
                        satmp[:, 0:512], pbig[:, 0:512], AF.Silu), inc="aa")
                    em.op("scalar", lambda e: e.activation(
                        satmp[:, 512:1024], pbig[:, 512:1024], AF.Silu), inc="aa")
                    a_sa = em.c["aa"]
                    em.wait("tensor", "aa", a_sa)
                    for fc in range(FC):
                        for dc in range(FC):
                            em.op("tensor", lambda e, fc=fc, dc=dc,
                                  st=(dc == 0), sp=(dc == FC - 1),
                                  base_col=base_col: e.matmul(
                                      pbig[:, fc * 128:(fc + 1) * 128],
                                      wpool[:, base_col + dc * 2048 + 1024 + fc * 128:
                                            base_col + dc * 2048 + 1024 + (fc + 1) * 128],
                                      xnv[:, dc, :], start=st, stop=sp),
                                  inc=("tt" if dc == FC - 1 and fc == FC - 1
                                       else None))
                    t_b = em.c["tt"]
                    em.wait("vector", "tt", t_b)
                    for hb in range(2):
                        em.op("vector", lambda e, dst_bf=dst_bf, hb=hb:
                              e.tensor_tensor(
                                  dst_bf[:, hb * 512:(hb + 1) * 512],
                                  satmp[:, hb * 512:(hb + 1) * 512],
                                  pbig[:, hb * 512:(hb + 1) * 512],
                                  op=ALU.mult), inc="vv")
                    return em.c["vv"]

                v_sw = mm_hidden(0, swb)
                em.wait("tensor", "vv", v_sw)
                swv = swb.rearrange("p (f t) -> p f t", f=FC)
                for half in range(2):
                    for hc in range(FC):
                        em.op("tensor", lambda e, half=half, hc=hc,
                              st=(hc == 0), sp=(hc == FC - 1): e.matmul(
                                  pbig[:, half * 512:(half + 1) * 512],
                                  swv[:, hc, :],
                                  wpool[:, 32768 + hc * 1024 + half * 512:
                                        32768 + hc * 1024 + (half + 1) * 512],
                                  start=st, stop=sp),
                              inc=("tt" if half == 1 and hc == FC - 1 else None))
                tso = em.c["tt"]
                em.wait("vector", "tt", tso)
                for hb in range(2):
                    em.op("vector", lambda e, lci=lci, hb=hb: e.tensor_copy(
                        accq[:, lci * D + hb * 512:lci * D + (hb + 1) * 512],
                        pbig[:, hb * 512:(hb + 1) * 512]), inc="vv")
                em.wait("tensor", "vv", em.c["vv"])
                mm_hidden(16384, hidq[:, lci * D:(lci + 1) * D])
                em.wait("tensor", "vv", em.c["vv"])

            for e_i in range(8):
                dbslot = 49152 if e_i % 2 == 0 else 57344
                em.wait("gpsimd", "tt", em.c["tt"])
                em.dma("gpsimd", wpool[:, dbslot:dbslot + 8192], dw2[e_i], "dg")
                w2l = em.c["dg"]
                em.wait("tensor", "dg", w2l)
                for ci in grp:
                    lci = ci - g0
                    hv = hidq[:, lci * D:(lci + 1) * D].rearrange(
                        "p (f t) -> p f t", f=FC)
                    for half in range(2):
                        for hc in range(FC):
                            em.op("tensor", lambda e, half=half, hc=hc,
                                  st=(hc == 0), sp=(hc == FC - 1), hv=hv,
                                  dbslot=dbslot: e.matmul(
                                      pbig[:, half * 512:(half + 1) * 512],
                                      hv[:, hc, :],
                                      wpool[:, dbslot + hc * 1024 + half * 512:
                                            dbslot + hc * 1024 + (half + 1) * 512],
                                      start=st, stop=sp),
                                  inc=("tt" if half == 1 and hc == FC - 1
                                       else None))
                    tex = em.c["tt"]
                    em.wait("vector", "tt", tex)
                    for hb in range(2):
                        em.op("vector", lambda e, lci=lci, ci=ci, e_i=e_i, hb=hb:
                              e.scalar_tensor_tensor(
                                  accq[:, lci * D + hb * 512:lci * D + (hb + 1) * 512],
                                  pbig[:, hb * 512:(hb + 1) * 512],
                                  wdense[:, ci * 8 + e_i:ci * 8 + e_i + 1],
                                  accq[:, lci * D + hb * 512:lci * D + (hb + 1) * 512],
                                  op0=ALU.mult, op1=ALU.add), inc="vv")
                    em.wait("tensor", "vv", em.c["vv"])

            for ci in grp:
                lci = ci - g0
                em.wait("scalar", "vv")
                em.wait("scalar", "dx")
                em.op("scalar", lambda e, lci=lci: e.activation(
                    tmpa[:], accq[:, lci * D:(lci + 1) * D], AF.Silu), inc="aa")
                agf = em.c["aa"]
                em.wait("sync", "vv")
                em.dma("sync", tmpb[:], dscr_s[ci * TP:(ci + 1) * TP, :], "dx")
                sxl = em.c["dx"]
                em.wait("vector", "aa", agf)
                em.wait("vector", "dx", sxl)
                em.op("vector", lambda e: e.tensor_tensor(
                    tmpa[:], tmpa[:], tmpb[:], op=ALU.mult), inc="vv")
                vsg = em.c["vv"]
                em.wait("sync", "vv", vsg)
                em.dma("sync", tmpb[:], dx[ci * TP:(ci + 1) * TP, :], "dx")
                xxl = em.c["dx"]
                em.wait("vector", "dx", xxl)
                em.op("vector", lambda e: e.tensor_tensor(
                    tmpa[:], tmpa[:], tmpb[:], op=ALU.add), inc="vv")
                vout = em.c["vv"]
                em.wait("sync", "vv", vout)
                em.dma("sync", dout[ci * TP:(ci + 1) * TP, :], tmpa[:], "dx")

        em.wait("sync", "dx", em.c["dx"])
        em.wait("sync", "dg", em.c["dg"])

        def replay(eng_name):
            def run(handle):
                for item in em.q[eng_name]:
                    if item[0] == "w":
                        handle.wait_ge(sems[item[1]], item[2])
                    else:
                        instr = item[1](handle)
                        for s, n in item[2]:
                            instr.then_inc(sems[s], n)
            return run

        block.sync(replay("sync"))
        block.tensor(replay("tensor"))
        block.vector(replay("vector"))
        block.scalar(replay("scalar"))
        block.gpsimd(replay("gpsimd"))

    return nc


def _prep(inputs):
    f32 = np.float32
    fmap = np.array([(fp % 16) * 64 + fp // 16 for fp in range(D)])

    def lhsT_tiles(Wt):
        return np.ascontiguousarray(
            Wt.reshape(8, 128, Wt.shape[1]).transpose(1, 0, 2).reshape(128, -1))

    g = lambda k: np.asarray(inputs[k], f32)
    mats = []
    for nm, perm in [("Wr", True), ("Wk", True), ("Wg", True), ("Wt", True),
                     ("Wv", False), ("Wa", False), ("Wb", False)]:
        W = g("L0_attn_" + nm)
        if perm:
            W = W[fmap]
        mats.append(lhsT_tiles(np.ascontiguousarray(W.T)))
    wproj = np.concatenate(mats, axis=1)
    wo = lhsT_tiles(np.ascontiguousarray(g("L0_attn_Wo").T[fmap]))
    w_in = lhsT_tiles(g("L0_ffn_w_shared_in"))
    w1 = lhsT_tiles(g("L0_ffn_w1_shared"))
    w_out = lhsT_tiles(g("L0_ffn_w_shared_out"))
    wgate = lhsT_tiles(g("L0_ffn_w_gate"))
    wffn = np.concatenate([w_in, w1, w_out, wgate], axis=1)
    w2 = np.stack([lhsT_tiles(g("L0_ffn_w2")[e]) for e in range(8)])
    lnw_rep = np.ascontiguousarray(
        np.broadcast_to(g("L0_attn_ln_weight")[None, :], (128, D)))
    gb_rep = np.ascontiguousarray(
        np.broadcast_to(g("L0_ffn_gate_bias")[None, :], (128, 8)))
    cw = g("L0_attn_conv_w")[:, 0, :]
    cw_f = np.ascontiguousarray(
        cw.reshape(8, 128, 4).transpose(1, 0, 2).reshape(128, 32))
    bdp = np.ascontiguousarray(g("L0_attn_base_decay")[fmap].reshape(8, 128).T)
    sel = np.zeros((16, 128), f32)
    for p in range(128):
        sel[p % 16, p] = 1.0
    ident = np.eye(128, dtype=f32)
    half = np.zeros((128, 2), f32)
    half[0:64, 0] = 1.0
    half[64:128, 1] = 1.0
    return dict(wproj=wproj, wo=wo, wffn=wffn, w2=w2, lnw_rep=lnw_rep,
                gb_rep=gb_rep, cw_f=cw_f, bdp=bdp, sel=sel, ident=ident,
                halfones=half)


def kernel(**inputs):
    x = np.asarray(inputs["L0_attn_x"], np.float32)
    B, L, _ = x.shape
    shared = _prep(inputs)
    nc = build_nc(L)
    in_maps = []
    for b in range(B):
        m = dict(shared)
        m["x"] = np.ascontiguousarray(x[b])
        in_maps.append(m)
    res = run_bass_kernel_spmd(nc, in_maps, core_ids=list(range(B)))
    return np.stack([res.results[b]["out"] for b in range(B)], axis=0)


# version 06
# speedup vs baseline: 1.3373x; 1.0520x over previous
import numpy as np
import ml_dtypes
import concourse.bass as bass
import concourse.mybir as mybir
from concourse.bass_utils import run_bass_kernel_spmd

F32 = mybir.dt.float32
BF16 = mybir.dt.bfloat16
AF = mybir.ActivationFunctionType
ALU = mybir.AluOpType
AX = mybir.AxisListType

D = 1024
H = 16
FC = 8
TP = 128
THJ = 16
TRS = 8
EPS_ST = 1e-5
EPS_RMS = 1e-6
SQS = float(2.0 ** -20)
SQS2 = float(2.0 ** -44)
SQB = float(2.0 ** 22 / 8.0)
SNB = float(2.0 ** 20 / 32.0)

ENGINES = ("sync", "tensor", "vector", "scalar", "gpsimd")


class Em:
    def __init__(self, sem_names):
        self.q = {k: [] for k in ENGINES}
        self.c = {k: 0 for k in sem_names}
        self.serialize = False

    def wait(self, eng, sem, val=None):
        if val is None:
            val = self.c[sem]
        if val > 0:
            self.q[eng].append(("w", sem, val))

    def op(self, eng, fn, inc=None):
        if self.serialize:
            for sname in self.c:
                self.wait(eng, sname)
        incs = ()
        if inc is not None:
            incs = ((inc, 1),)
            self.c[inc] += 1
        self.q[eng].append(("op", fn, incs))

    def dma(self, eng, out, in_, sem):
        if eng == "sync":
            self.wait("sync", sem)
        def fn(e, out=out, in_=in_):
            return e.dma_start(out=out, in_=in_)
        self.q[eng].append(("op", fn, ((sem, 16),)))
        self.c[sem] += 16


def build_nc(L=2048):
    NCH = L // TP
    NSUB = TP // THJ
    groups = [list(range(g, min(g + 2, NCH))) for g in range(0, NCH, 2)]
    nc = bass.Bass("TRN2", target_bir_lowering=False,
                   detect_race_conditions=False)

    dx = nc.dram_tensor("x", [L, D], F32, kind="ExternalInput").ap()
    dwproj = nc.dram_tensor("wproj", [128, 7 * 8192], F32, kind="ExternalInput").ap()
    dwo = nc.dram_tensor("wo", [128, 8192], F32, kind="ExternalInput").ap()
    dwffn = nc.dram_tensor("wffn", [128, 41024], F32, kind="ExternalInput").ap()
    dw2 = nc.dram_tensor("w2", [8, 128, 8192], F32, kind="ExternalInput").ap()
    dlnw = nc.dram_tensor("lnw_rep", [128, D], F32, kind="ExternalInput").ap()
    dgb = nc.dram_tensor("gb_rep", [128, 8], F32, kind="ExternalInput").ap()
    dcw = nc.dram_tensor("cw_f", [128, 32], F32, kind="ExternalInput").ap()
    dbdp = nc.dram_tensor("bdp", [128, 8], F32, kind="ExternalInput").ap()
    dsel = nc.dram_tensor("sel", [16, 128], F32, kind="ExternalInput").ap()
    dident = nc.dram_tensor("ident", [128, 128], F32, kind="ExternalInput").ap()
    dhalf = nc.dram_tensor("halfones", [128, 2], F32, kind="ExternalInput").ap()
    dout = nc.dram_tensor("out", [L, D], F32, kind="ExternalOutput").ap()
    dscr_s = nc.dram_tensor("scr_s", [L, D], F32).ap()
    dscr_xnb = nc.dram_tensor("scr_xnb", [NCH, 128, D], BF16).ap()
    dscr_cv = nc.dram_tensor("scr_cv", [3, D, TP], F32).ap()

    sem_names = ["dx", "dg", "tt", "vv", "aa"]

    from contextlib import ExitStack
    with ExitStack() as _st:
        wpool = _st.enter_context(nc.sbuf_tensor("wpool", [128, 65536], BF16))
        xn_tok = _st.enter_context(nc.sbuf_tensor("xn_tok", [128, D], F32))
        xn_fm = _st.enter_context(nc.sbuf_tensor("xn_fm", [128, FC * (TP + 3)], F32))
        kkfm = _st.enter_context(nc.sbuf_tensor("kkfm", [128, D], F32))
        cfm = _st.enter_context(nc.sbuf_tensor("cfm", [128, D], F32))
        vfm = _st.enter_context(nc.sbuf_tensor("vfm", [128, D], F32))
        winv = _st.enter_context(nc.sbuf_tensor("winv", [128, D], F32))
        kwb = _st.enter_context(nc.sbuf_tensor("kwb", [128, D], F32))
        tmpa = _st.enter_context(nc.sbuf_tensor("tmpa", [128, D], F32))
        rsbuf = _st.enter_context(nc.sbuf_tensor("rsbuf", [128, FC * TP], F32))
        lambuf = _st.enter_context(nc.sbuf_tensor("lambuf", [128, FC * TP], F32))
        zst = _st.enter_context(nc.sbuf_tensor("zst", [128, 512], F32))
        sm = _st.enter_context(nc.sbuf_tensor("sm", [128, 16], F32))
        sm2 = _st.enter_context(nc.sbuf_tensor("sm2", [128, 16], F32))
        sm3 = _st.enter_context(nc.sbuf_tensor("sm3", [128, 32], F32))
        sm4 = _st.enter_context(nc.sbuf_tensor("sm4", [128, 16], F32))
        sctmp = _st.enter_context(nc.sbuf_tensor("sctmp", [128, 64], F32))
        cwf = _st.enter_context(nc.sbuf_tensor("cwf", [128, 32], F32))
        bdps = _st.enter_context(nc.sbuf_tensor("bdps", [128, 8], F32))
        lnw = _st.enter_context(nc.sbuf_tensor("lnw", [128, D], BF16))
        gbr = _st.enter_context(nc.sbuf_tensor("gbr", [128, 8], F32))
        self32 = _st.enter_context(nc.sbuf_tensor("self32", [16, 128], F32))
        ident32 = _st.enter_context(nc.sbuf_tensor("ident32", [128, 128], F32))
        half2 = _st.enter_context(nc.sbuf_tensor("half2", [128, 2], F32))
        upool = _st.enter_context(nc.sbuf_tensor("upool", [128, 2 * 194 * THJ], F32))
        xnfmb = _st.enter_context(nc.sbuf_tensor("xnfmb", [128, D], BF16))
        up = upool[:]
        HJC = 194 * THJ
        hjbufs_ap = (up[0:16, 0:HJC], up[0:16, HJC:2 * HJC])
        hidq = up[:, 0:1024].bitcast(BF16)
        accq = up[:, 1024:3072]
        swb = up[:, 3072:3584].bitcast(BF16)
        satmp = up[:, 3584:4608]
        wdense = up[:, 4608:4608 + NCH * 8]
        tmpb = vfm
        gob = cfm[:, 0:512].bitcast(BF16)
        ebd = bdps
        pm = _st.enter_context(nc.psum_tensor("pm", [128, 512], F32))
        pm2 = _st.enter_context(nc.psum_tensor("pm2", [128, 512], F32))
        t1b = _st.enter_context(nc.psum_tensor("t1b", [128, 512], F32))
        prep0 = _st.enter_context(nc.psum_tensor("prep0", [128, 512], F32))
        prep1 = _st.enter_context(nc.psum_tensor("prep1", [128, 512], F32))
        pbig = _st.enter_context(nc.psum_tensor("pbig", [128, 1024], F32))
        paux = _st.enter_context(nc.psum_tensor("paux", [128, 512], F32))
        s_dx = _st.enter_context(nc.semaphore("s_dx"))
        s_dg = _st.enter_context(nc.semaphore("s_dg"))
        s_tt = _st.enter_context(nc.semaphore("s_tt"))
        s_vv = _st.enter_context(nc.semaphore("s_vv"))
        s_aa = _st.enter_context(nc.semaphore("s_aa"))
        block = _st.enter_context(nc.Block())
        sems = {"dx": s_dx, "dg": s_dg, "tt": s_tt, "vv": s_vv, "aa": s_aa}
        em = Em(sem_names)

        zv = zst[:].rearrange("p (f j) -> p f j", f=FC)
        rsb = rsbuf[:].rearrange("p (t f) -> p t f", f=FC)
        lmb = lambuf[:].rearrange("p (t f) -> p t f", f=FC)
        xnfm = xn_fm[:].rearrange("p (f t) -> p f t", f=FC)
        winvv = winv[:].rearrange("p (f t) -> p f t", f=FC)
        kwv = kwb[:].rearrange("p (f t) -> p f t", f=FC)

        rs_trk = sm[:, 0:8]
        dnegs = (sm[:, 8:16], sm4[:, 0:8])
        sqvs = (sm3[:, 8:16], sm4[:, 8:16])
        p_sc = sm2[:, 0:8]
        q_sc = sm2[:, 8:16]
        mue = sm3[:, 0:8]
        sqv = sm3[:, 8:16]
        linv = sm3[:, 16:24]
        rs_tmp = sm3[:, 24:32]

        def bc8(ap2d, n=64):
            return ap2d.unsqueeze(2).broadcast_to([128, 8, n])

        def bc64(ap2d):
            return ap2d.unsqueeze(1).broadcast_to([128, 8, 64])

        for dst, src in [(cwf[:], dcw), (bdps[:], dbdp),
                         (gbr[:], dgb), (self32[:], dsel), (ident32[:], dident),
                         (half2[:], dhalf)]:
            em.dma("sync", dst, src, "dx")
        const_ld = em.c["dx"]
        em.dma("gpsimd", lnw[:], dlnw, "dg")
        for i in range(7):
            em.dma("gpsimd", wpool[:, i * 8192:(i + 1) * 8192],
                   dwproj[:, i * 8192:(i + 1) * 8192], "dg")
        em.dma("gpsimd", wpool[:, 7 * 8192:8 * 8192], dwo, "dg")
        attn_w_ld = em.c["dg"]

        em.wait("scalar", "dx", const_ld)
        em.op("scalar", lambda e: e.activation(bdps[:], bdps[:], AF.Exp,
                                               scale=-1.0), inc="aa")
        a_ebd = em.c["aa"]
        em.op("vector", lambda e: e.memset(zst[:], 0.0), inc="vv")
        em.op("vector", lambda e: e.memset(rs_trk, 0.0), inc="vv")
        em.wait("vector", "dx", const_ld)
        em.wait("vector", "dg", attn_w_ld)
        em.wait("vector", "aa", a_ebd)
        em.wait("tensor", "dx", const_ld)
        em.wait("tensor", "dg", attn_w_ld)

        wslice = {}
        for i, nm in enumerate(["r", "k", "g", "wt", "v", "a", "kk"]):
            wslice[nm] = wpool[:, i * 8192:(i + 1) * 8192]
        wslice["wo"] = wpool[:, 7 * 8192:8 * 8192]

        rep_free = {0: 0, 1: 0}
        pend_s6 = [None]

        for ci in range(NCH):
            em.wait("sync", "vv")
            em.dma("sync", tmpb[:], dx[ci * TP:(ci + 1) * TP, :], "dx")
            xld = em.c["dx"]
            em.wait("scalar", "vv")
            em.wait("scalar", "dx", xld)
            em.op("scalar", lambda e: e.activation(tmpa[:], tmpb[:], AF.Square,
                                                   accum_out=sm3[:, 0:1]), inc="aa")
            em.op("scalar", lambda e: e.activation(xn_tok[:], tmpb[:], AF.Identity,
                                                   accum_out=sm3[:, 1:2]), inc="aa")
            aln = em.c["aa"]
            em.wait("vector", "aa", aln)
            em.op("vector", lambda e: e.tensor_scalar(
                sm3[:, 2:3], sm3[:, 1:2], float(1.0 / D), None, op0=ALU.mult), inc="vv")
            em.op("vector", lambda e: e.tensor_scalar(
                sm3[:, 3:4], sm3[:, 0:1], float(1.0 / D), None, op0=ALU.mult), inc="vv")
            em.op("vector", lambda e: e.tensor_tensor(
                sm3[:, 4:5], sm3[:, 2:3], sm3[:, 2:3], op=ALU.mult), inc="vv")
            em.wait("vector", "vv")
            em.op("vector", lambda e: e.tensor_tensor(
                sm3[:, 5:6], sm3[:, 3:4], sm3[:, 4:5], op=ALU.subtract), inc="vv")
            em.wait("vector", "vv")
            em.op("vector", lambda e: e.tensor_scalar(
                sm3[:, 5:6], sm3[:, 5:6], 1e-5, None, op0=ALU.add), inc="vv")
            vvar = em.c["vv"]
            em.wait("scalar", "vv", vvar)
            em.op("scalar", lambda e: e.activation(sm3[:, 6:7], sm3[:, 5:6],
                                                   AF.Sqrt), inc="aa")
            asd = em.c["aa"]
            em.wait("vector", "aa", asd)
            em.op("vector", lambda e: e.reciprocal(sm3[:, 7:8], sm3[:, 6:7]), inc="vv")
            em.wait("vector", "vv")
            em.op("vector", lambda e: e.tensor_scalar(
                xn_tok[:], tmpb[:], sm3[:, 2:3], sm3[:, 7:8],
                op0=ALU.subtract, op1=ALU.mult), inc="vv")
            vxn = em.c["vv"]

            em.wait("tensor", "vv", vxn)
            for fc in range(FC):
                em.op("tensor", lambda e, fc=fc: e.transpose(
                    pbig[:, fc * 128:(fc + 1) * 128],
                    xn_tok[:, fc * 128:(fc + 1) * 128], ident32[:]),
                    inc=("tt" if fc == FC - 1 else None))
            ttr = em.c["tt"]
            em.wait("vector", "tt", ttr)
            if ci == 0:
                em.op("vector", lambda e: e.memset(xn_fm[:], 0.0), inc="vv")
            else:
                em.op("vector", lambda e: e.tensor_copy(
                    xnfm[:, :, 0:3], xnfm[:, :, TP:TP + 3]), inc="vv")
            for hb in range(2):
                em.op("vector", lambda e, hb=hb: e.tensor_copy(
                    xnfm[:, hb * 4:hb * 4 + 4, 3:TP + 3],
                    pbig[:, hb * 512:(hb + 1) * 512].rearrange(
                        "p (f t) -> p f t", f=4)), inc="vv")
            vxnfm = em.c["vv"]
            em.wait("scalar", "tt", ttr)
            em.wait("scalar", "dx")
            em.op("vector", lambda e: e.tensor_copy(
                xnfmb[:, 0:512], pbig[:, 0:512]), inc="aa")
            em.op("vector", lambda e: e.tensor_copy(
                xnfmb[:, 512:1024], pbig[:, 512:1024]), inc="aa")
            axnb = em.c["aa"]
            em.wait("sync", "aa", axnb)
            em.dma("sync", dscr_xnb[ci], xnfmb[:], "dx")

            cw3 = cwf[:].rearrange("p (f k) -> p f k", f=FC)
            em.op("vector", lambda e: e.tensor_tensor(
                cfm[:].rearrange("p (f t) -> p f t", f=FC),
                xnfm[:, :, 0:TP],
                cw3[:, :, 0:1].broadcast_to([128, FC, TP]), op=ALU.mult), inc="vv")
            for j in range(1, 4):
                em.op("vector", lambda e, j=j: e.tensor_tensor(
                    tmpa[:].rearrange("p (f t) -> p f t", f=FC),
                    xnfm[:, :, j:TP + j],
                    cw3[:, :, j:j + 1].broadcast_to([128, FC, TP]),
                    op=ALU.mult), inc="vv")
                em.op("vector", lambda e: e.tensor_tensor(
                    cfm[:], cfm[:], tmpa[:], op=ALU.add), inc="vv")
            vxc = em.c["vv"]
            em.wait("tensor", "vv", vxc)
            for fc in range(FC):
                em.op("tensor", lambda e, fc=fc: e.transpose(
                    pbig[:, fc * 128:(fc + 1) * 128],
                    cfm[:, fc * 128:(fc + 1) * 128], ident32[:]),
                    inc=("tt" if fc == FC - 1 else None))
            ttr2 = em.c["tt"]
            em.wait("vector", "tt", ttr2)
            for hb in range(2):
                em.op("vector", lambda e, hb=hb: e.scalar_tensor_tensor(
                    xn_tok[:, hb * 512:(hb + 1) * 512],
                    pbig[:, hb * 512:(hb + 1) * 512], 0.1,
                    xn_tok[:, hb * 512:(hb + 1) * 512],
                    op0=ALU.mult, op1=ALU.add), inc="vv")
            vxg = em.c["vv"]

            def proj(nm):
                ws = wslice[nm]
                for fc in range(FC):
                    for dc in range(FC):
                        lhsT = ws[:, dc * 1024 + fc * 128:dc * 1024 + (fc + 1) * 128]
                        rhs = xnfmb[:].rearrange("p (f t) -> p f t", f=FC)[:, dc, :]
                        em.op("tensor", lambda e, lhsT=lhsT, rhs=rhs, fc=fc,
                              st=(dc == 0), sp=(dc == FC - 1):
                              e.matmul(pbig[:, fc * 128:(fc + 1) * 128], lhsT,
                                       rhs, start=st, stop=sp),
                              inc=("tt" if dc == FC - 1 and fc == FC - 1 else None))
                return em.c["tt"]

            em.wait("tensor", "aa", axnb)
            em.wait("tensor", "vv", vxg)
            twt = proj("wt")
            em.wait("scalar", "vv")
            em.wait("scalar", "dx")
            em.wait("scalar", "tt", twt)
            em.op("scalar", lambda e: e.activation(
                tmpa[:, 0:512], pbig[:, 0:512], AF.Exp), inc="aa")
            em.op("scalar", lambda e: e.activation(
                tmpa[:, 512:1024], pbig[:, 512:1024], AF.Exp), inc="aa")
            em.op("scalar", lambda e: e.activation(tmpb[:], tmpa[:], AF.Exp,
                                                   scale=-1.0), inc="aa")
            awt = em.c["aa"]
            em.wait("vector", "aa", awt)
            em.op("vector", lambda e: e.tensor_tensor(
                tmpa[:].rearrange("p (f t) -> p f t", f=FC),
                tmpb[:].rearrange("p (f t) -> p f t", f=FC),
                bc8(ebd[:], TP), op=ALU.mult), inc="vv")
            em.op("vector", lambda e: e.tensor_scalar(
                tmpa[:], tmpa[:], 0.01, 0.99, op0=ALU.max, op1=ALU.min), inc="vv")
            em.op("vector", lambda e: e.reciprocal(winv[:], tmpa[:]), inc="vv")
            vwt = em.c["vv"]

            em.wait("tensor", "vv", vwt)
            tk = proj("k")
            em.wait("vector", "tt", tk)
            for hb in range(2):
                em.op("vector", lambda e, hb=hb: e.tensor_tensor(
                    kwb[:, hb * 512:(hb + 1) * 512],
                    pbig[:, hb * 512:(hb + 1) * 512],
                    winv[:, hb * 512:(hb + 1) * 512], op=ALU.mult), inc="vv")
            vk = em.c["vv"]

            em.wait("tensor", "vv", vk)
            tkk = proj("kk")
            em.wait("vector", "tt", tkk)
            for hb in range(2):
                em.op("vector", lambda e, hb=hb: e.tensor_copy(
                    kkfm[:, hb * 512:(hb + 1) * 512],
                    pbig[:, hb * 512:(hb + 1) * 512]), inc="vv")
            vkk = em.c["vv"]

            em.wait("tensor", "vv", vkk)
            ta = proj("a")
            em.wait("vector", "tt", ta)
            for hb in range(2):
                em.op("vector", lambda e, hb=hb: e.tensor_tensor(
                    cfm[:, hb * 512:(hb + 1) * 512],
                    kkfm[:, hb * 512:(hb + 1) * 512],
                    pbig[:, hb * 512:(hb + 1) * 512], op=ALU.mult), inc="vv")
            vc = em.c["vv"]

            em.wait("tensor", "vv", vc)
            tv = proj("v")
            em.wait("vector", "tt", tv)
            for hb in range(2):
                em.op("vector", lambda e, hb=hb: e.tensor_copy(
                    vfm[:, hb * 512:(hb + 1) * 512],
                    pbig[:, hb * 512:(hb + 1) * 512]), inc="vv")
            vv_ = em.c["vv"]

            em.wait("tensor", "vv", vv_)
            for t_i, src in enumerate((cfm, vfm)):
                for fc in range(FC):
                    em.op("tensor", lambda e, src=src, fc=fc, t_i=t_i: e.matmul(
                        paux[:, t_i * 16 + fc * 2:t_i * 16 + fc * 2 + 2],
                        src[:, fc * 128:(fc + 1) * 128], half2[:],
                        start=True, stop=True),
                        inc=("tt" if t_i == 1 and fc == FC - 1 else None))
            tsc = em.c["tt"]
            em.wait("vector", "tt", tsc)
            if ci == 0:
                em.op("vector", lambda e: e.memset(sctmp[:], 0.0), inc="vv")
            em.op("vector", lambda e: e.tensor_copy(sctmp[:, 0:16], paux[:, 0:16]),
                  inc="vv")
            em.op("vector", lambda e: e.tensor_copy(sctmp[:, 32:48], paux[:, 16:32]),
                  inc="vv")
            vsc = em.c["vv"]
            em.wait("tensor", "vv", vsc)
            em.op("tensor", lambda e: e.transpose(
                paux[0:64, 0:128], sctmp[:], ident32[:]), inc="tt")
            tsc2 = em.c["tt"]

            em.wait("sync", "vv", vv_)
            em.dma("sync", dscr_cv[0].rearrange("(fc p) t -> p fc t", p=128),
                   kkfm[:].rearrange("p (fc t) -> p fc t", fc=FC), "dx")
            em.dma("sync", dscr_cv[1].rearrange("(fc p) t -> p fc t", p=128),
                   cfm[:].rearrange("p (fc t) -> p fc t", fc=FC), "dx")
            em.dma("sync", dscr_cv[2].rearrange("(fc p) t -> p fc t", p=128),
                   vfm[:].rearrange("p (fc t) -> p fc t", fc=FC), "dx")
            cvout = em.c["dx"]
            rg = kkfm

            em.wait("tensor", "vv", vv_)
            tr = proj("r")
            em.wait("scalar", "tt", tr)
            em.wait("scalar", "dx", cvout)
            em.op("scalar", lambda e, rg=rg: e.activation(
                rg[:, 0:512], pbig[:, 0:512], AF.Sigmoid), inc="aa")
            em.op("scalar", lambda e, rg=rg: e.activation(
                rg[:, 512:1024], pbig[:, 512:1024], AF.Sigmoid), inc="aa")
            ar = em.c["aa"]

            em.wait("tensor", "aa", ar)
            tg = proj("g")
            em.wait("scalar", "tt", tg)
            em.wait("scalar", "vv")
            em.op("scalar", lambda e: e.activation(
                tmpb[:, 0:512], pbig[:, 0:512], AF.Sigmoid), inc="aa")
            em.op("scalar", lambda e: e.activation(
                tmpb[:, 512:1024], pbig[:, 512:1024], AF.Sigmoid), inc="aa")
            ag = em.c["aa"]
            em.wait("vector", "aa", ag)
            em.op("vector", lambda e, rg=rg: e.tensor_tensor(
                rg[:], rg[:], tmpb[:], op=ALU.mult), inc="vv")
            vrg = em.c["vv"]

            hjbufs = hjbufs_ap
            hj_dx = {}
            hj_vv = {}
            sel_done = {}

            def emit_hj_load(sub):
                hj = hjbufs[sub % 2]
                hjv = hj.rearrange("h (c t) -> h c t", c=194)
                em.wait("sync", "dx", cvout)
                if sub >= 2:
                    em.wait("sync", "tt", sel_done[sub - 2])
                for vi in range(3):
                    src = dscr_cv[vi].rearrange("(h j) t -> h j t", h=H)[
                        :, :, sub * THJ:(sub + 1) * THJ]
                    em.dma("sync", hjv[:, vi * 64:(vi + 1) * 64, :], src, "dx")
                hj_dx[sub] = em.c["dx"]
                em.wait("vector", "tt", tsc2)
                em.op("vector", lambda e, hjv=hjv, sub=sub: e.tensor_copy(
                    hjv[:, 192:193, :],
                    paux[0:16, sub * THJ:(sub + 1) * THJ].unsqueeze(1)), inc="vv")
                em.op("vector", lambda e, hjv=hjv, sub=sub: e.tensor_copy(
                    hjv[:, 193:194, :],
                    paux[32:48, sub * THJ:(sub + 1) * THJ].unsqueeze(1)), inc="vv")
                hj_vv[sub] = em.c["vv"]

            emit_hj_load(0)
            if NSUB > 1:
                emit_hj_load(1)

            for sub in range(NSUB):
                hj = hjbufs[sub % 2]
                hjv = hj.rearrange("h (c t) -> h c t", c=194)
                for ts in range(THJ):
                    tt_ = sub * THJ + ts
                    gt = ci * TP + tt_
                    par = gt % 2
                    prep = (prep0, prep1)[par]
                    dneg = dnegs[par]
                    sqv = sqvs[par]
                    after_rs = (gt % TRS == 0)

                    em.wait("tensor", "dx", hj_dx[sub])
                    em.wait("tensor", "vv", hj_vv[sub])
                    if rep_free[par] > 0:
                        em.wait("tensor", "vv", rep_free[par])
                    em.op("tensor", lambda e, prep=prep, hjv=hjv, ts=ts: e.matmul(
                        prep[:, 0:194], self32[:], hjv[:, :, ts],
                        start=True, stop=True), inc="tt")
                    trep = em.c["tt"]

                    wsl = winvv[:, :, tt_]
                    kwsl = kwv[:, :, tt_]

                    em.wait("vector", "tt", trep)
                    em.op("vector", lambda e, prep=prep: e.tensor_tensor(
                        pm[:].rearrange("p (f j) -> p f j", f=FC), zv,
                        bc64(prep[:, 0:64]), op=ALU.mult), inc="vv")
                    em.op("vector", lambda e, dneg=dneg: e.tensor_reduce(
                        dneg, pm[:].rearrange("p (f j) -> p f j", f=FC),
                        axis=AX.X, op=ALU.add, negate=True), inc="vv")
                    if pend_s6[0] is not None:
                        pend_s6[0]()
                        pend_s6[0] = None
                    else:
                        em.wait("vector", "vv")
                    em.op("vector", lambda e, wsl=wsl, dneg=dneg: e.tensor_tensor(
                        p_sc, dneg, wsl, op=ALU.mult), inc="vv")
                    if after_rs:
                        qop = kwsl
                        em.wait("vector", "vv")
                    else:
                        lprev = lmb[:, tt_ - 1, :]
                        em.op("vector", lambda e, lprev=lprev, kwsl=kwsl:
                              e.tensor_tensor(q_sc, lprev, kwsl, op=ALU.mult),
                              inc="vv")
                        qop = q_sc
                    em.op("vector", lambda e, prep=prep: e.tensor_tensor(
                        t1b[:].rearrange("p (f j) -> p f j", f=FC),
                        bc64(prep[:, 64:128]), bc8(p_sc), op=ALU.mult), inc="vv")
                    em.op("vector", lambda e: e.tensor_tensor(
                        zst[:], zst[:], t1b[:], op=ALU.add), inc="vv")
                    em.op("vector", lambda e, prep=prep, qop=qop: e.tensor_tensor(
                        t1b[:].rearrange("p (f j) -> p f j", f=FC),
                        bc64(prep[:, 128:192]), bc8(qop), op=ALU.mult), inc="vv")
                    rs_prev = rs_trk if after_rs else rsb[:, tt_ - 1, :]
                    em.op("vector", lambda e, prep=prep, rs_prev=rs_prev:
                          e.scalar_tensor_tensor(rs_tmp, p_sc, prep[:, 192:193],
                                                 rs_prev, op0=ALU.mult,
                                                 op1=ALU.add), inc="vv")
                    em.op("vector", lambda e: e.tensor_tensor(
                        zst[:], zst[:], t1b[:], op=ALU.add), inc="vv")
                    em.op("vector", lambda e, prep=prep, qop=qop, tt_=tt_:
                          e.scalar_tensor_tensor(rsb[:, tt_, :], qop,
                                                 prep[:, 193:194], rs_tmp,
                                                 op0=ALU.mult, op1=ALU.add),
                          inc="vv")
                    rep_free[par] = em.c["vv"]
                    em.op("vector", lambda e: e.scalar_tensor_tensor(
                        pm2[:], zst[:], SQS2, zst[:], op0=ALU.mult, op1=ALU.mult),
                        inc="vv")
                    em.op("vector", lambda e, dneg=dneg: e.tensor_reduce(
                        dneg, pm2[:].rearrange("p (f j) -> p f j", f=FC),
                        axis=AX.X, op=ALU.add), inc="vv")
                    vnsq = em.c["vv"]
                    if after_rs:
                        em.op("vector", lambda e, wsl=wsl: e.tensor_scalar(
                            mue, wsl, EPS_ST, None, op0=ALU.mult), inc="vv")
                    else:
                        lprev = lmb[:, tt_ - 1, :]
                        em.op("vector", lambda e, lprev=lprev, wsl=wsl:
                              e.scalar_tensor_tensor(mue, lprev, EPS_ST, wsl,
                                                     op0=ALU.mult, op1=ALU.mult),
                              inc="vv")
                    em.wait("scalar", "vv", vnsq)
                    em.op("scalar", lambda e, sqv=sqv, dneg=dneg: e.activation(
                        sqv, dneg, AF.Sqrt), inc="aa")
                    asv = em.c["aa"]

                    def make_s6(tt_=tt_, asv=asv, sqv=sqv):
                        def emit():
                            em.wait("vector", "aa", asv)
                            em.op("vector", lambda e, tt_=tt_, sqv=sqv:
                                  e.scalar_tensor_tensor(
                                      lmb[:, tt_, :], sqv, SQB, mue,
                                      op0=ALU.mult, op1=ALU.add), inc="vv")
                        return emit

                    if gt % TRS == TRS - 1:
                        make_s6()()
                        em.wait("vector", "vv")
                        em.op("vector", lambda e, tt_=tt_: e.reciprocal(
                            linv, lmb[:, tt_, :]), inc="vv")
                        em.wait("vector", "vv")
                        em.op("vector", lambda e: e.tensor_tensor(
                            zv, zv, bc8(linv), op=ALU.mult), inc="vv")
                        em.op("vector", lambda e, tt_=tt_: e.tensor_tensor(
                            rs_trk, rsb[:, tt_, :], linv, op=ALU.mult), inc="vv")
                    else:
                        pend_s6[0] = make_s6()
                sel_done[sub] = em.c["tt"]
                if sub + 2 < NSUB:
                    emit_hj_load(sub + 2)

            if pend_s6[0] is not None:
                pend_s6[0]()
                pend_s6[0] = None
            em.wait("vector", "vv")
            em.wait("vector", "dx", cvout)
            em.op("vector", lambda e: e.reciprocal(tmpa[:], lambuf[:]), inc="vv")
            em.op("vector", lambda e: e.tensor_tensor(
                tmpb[:], rsbuf[:], tmpa[:], op=ALU.mult), inc="vv")
            em.op("vector", lambda e, rg=rg: e.tensor_tensor(
                gob[:, 0:FC * TP].rearrange("p (f t) -> p f t", f=FC),
                tmpb[:].rearrange("p (t f) -> p f t", f=FC),
                rg[:].rearrange("p (f t) -> p f t", f=FC), op=ALU.mult), inc="vv")
            vgo = em.c["vv"]

            em.wait("tensor", "vv", vgo)
            gov = gob[:, 0:FC * TP].rearrange("p (f t) -> p f t", f=FC)
            for half in range(2):
                for fcp in range(FC):
                    em.op("tensor", lambda e, half=half, fcp=fcp,
                          st=(fcp == 0), sp=(fcp == FC - 1): e.matmul(
                              pbig[:, half * 512:(half + 1) * 512],
                              gov[:, fcp, :],
                              wslice["wo"][:, fcp * 1024 + half * 512:
                                           fcp * 1024 + (half + 1) * 512],
                              start=st, stop=sp),
                          inc=("tt" if half == 1 and fcp == FC - 1 else None))
            two = em.c["tt"]
            em.wait("vector", "tt", two)
            for hb in range(2):
                em.op("vector", lambda e, hb=hb: e.tensor_copy(
                    tmpb[:, hb * 512:(hb + 1) * 512],
                    pbig[:, hb * 512:(hb + 1) * 512]), inc="vv")
            vwoe = em.c["vv"]
            em.wait("scalar", "vv", vwoe)
            em.op("scalar", lambda e: e.activation(
                tmpa[:], tmpb[:], AF.Square, scale=SQS,
                accum_out=sm3[:, 0:1]), inc="aa")
            em.op("scalar", lambda e: e.activation(sm3[:, 1:2], sm3[:, 0:1],
                                                   AF.Sqrt), inc="aa")
            aso = em.c["aa"]
            em.wait("vector", "aa", aso)
            em.op("vector", lambda e: e.tensor_scalar(
                sm3[:, 2:3], sm3[:, 1:2], SNB, EPS_RMS,
                op0=ALU.mult, op1=ALU.add), inc="vv")
            em.wait("vector", "vv")
            em.op("vector", lambda e: e.reciprocal(sm3[:, 3:4], sm3[:, 2:3]),
                  inc="vv")
            em.wait("vector", "vv")
            em.op("vector", lambda e: e.scalar_tensor_tensor(
                tmpa[:], tmpb[:], sm3[:, 3:4], lnw[:], op0=ALU.mult,
                op1=ALU.mult), inc="vv")
            em.op("vector", lambda e: e.tensor_tensor(
                xn_tok[:], tmpa[:], xn_tok[:], op=ALU.add), inc="vv")
            vs = em.c["vv"]
            em.wait("sync", "vv", vs)
            em.dma("sync", dscr_s[ci * TP:(ci + 1) * TP, :], xn_tok[:], "dx")

        em.wait("gpsimd", "tt", em.c["tt"])
        em.wait("gpsimd", "aa", em.c["aa"])
        em.dma("gpsimd", wpool[:, 0:16384], dwffn[:, 0:16384], "dg")
        em.dma("gpsimd", wpool[:, 16384:32768], dwffn[:, 16384:32768], "dg")
        em.dma("gpsimd", wpool[:, 32768:40960], dwffn[:, 32768:40960], "dg")
        em.dma("gpsimd", wpool[:, 40960:41024], dwffn[:, 40960:41024], "dg")
        ffnw = em.c["dg"]
        em.wait("tensor", "dg", ffnw)
        wgate_sl = wpool[:, 40960:41024]

        w2_slot_user = {0: em.c["tt"], 1: em.c["tt"]}
        lg = sm2[:, 0:8]
        ex = sm2[:, 8:16]
        mask1 = sctmp[:, 0:8]
        masked = sctmp[:, 8:16]
        keep = sctmp[:, 16:24]

        for grp in groups:
            g0 = grp[0]
            for ci in grp:
                lci = ci - g0
                em.wait("sync", "tt")
                em.dma("sync", xnfmb[:], dscr_xnb[ci], "dx")
                xbl = em.c["dx"]
                em.wait("tensor", "dx", xbl)
                xnv = xnfmb[:].rearrange("p (f t) -> p f t", f=FC)
                em.wait("tensor", "vv")
                for dc in range(FC):
                    em.op("tensor", lambda e, dc=dc, st=(dc == 0),
                          sp=(dc == FC - 1): e.matmul(
                              paux[:, 0:8], xnv[:, dc, :],
                              wgate_sl[:, dc * 8:(dc + 1) * 8], start=st, stop=sp),
                          inc=("tt" if dc == FC - 1 else None))
                tgt = em.c["tt"]
                em.wait("vector", "tt", tgt)
                em.op("vector", lambda e: e.tensor_tensor(
                    lg, paux[:, 0:8], gbr[:], op=ALU.add), inc="vv")
                em.wait("vector", "vv")
                em.op("vector", lambda e: e.tensor_reduce(
                    sm3[:, 0:1], lg, axis=AX.X, op=ALU.max, negate=True), inc="vv")
                vlg = em.c["vv"]
                em.wait("scalar", "vv", vlg)
                em.op("scalar", lambda e: e.activation(
                    ex, lg, AF.Exp, bias=sm3[:, 0:1]), inc="aa")
                age = em.c["aa"]
                em.wait("vector", "aa", age)
                em.op("vector", lambda e: e.tensor_reduce(
                    sm3[:, 1:2], ex, axis=AX.X, op=ALU.add), inc="vv")
                em.wait("vector", "vv")
                em.op("vector", lambda e: e.reciprocal(sm3[:, 2:3], sm3[:, 1:2]),
                      inc="vv")
                em.op("vector", lambda e: e.tensor_reduce(
                    sm3[:, 3:4], lg, axis=AX.X, op=ALU.max), inc="vv")
                em.wait("vector", "vv")
                em.op("vector", lambda e: e.tensor_single_scalar(
                    mask1, lg, sm3[:, 3:4], op=ALU.is_ge), inc="vv")
                em.wait("vector", "vv")
                em.op("vector", lambda e: e.scalar_tensor_tensor(
                    masked, mask1, -1e30, lg, op0=ALU.mult, op1=ALU.add), inc="vv")
                em.wait("vector", "vv")
                em.op("vector", lambda e: e.tensor_reduce(
                    sm3[:, 4:5], masked, axis=AX.X, op=ALU.max), inc="vv")
                em.wait("vector", "vv")
                em.op("vector", lambda e: e.tensor_single_scalar(
                    keep, lg, sm3[:, 4:5], op=ALU.is_ge), inc="vv")
                em.wait("vector", "vv")
                em.op("vector", lambda e: e.tensor_tensor(
                    keep, keep, ex, op=ALU.mult), inc="vv")
                em.wait("vector", "vv")
                em.op("vector", lambda e, ci=ci: e.tensor_scalar(
                    wdense[:, ci * 8:(ci + 1) * 8], keep, sm3[:, 2:3], None,
                    op0=ALU.mult), inc="vv")

                def mm_hidden(base_col, dst_bf):
                    for fc in range(FC):
                        for dc in range(FC):
                            em.op("tensor", lambda e, fc=fc, dc=dc,
                                  st=(dc == 0), sp=(dc == FC - 1),
                                  base_col=base_col: e.matmul(
                                      pbig[:, fc * 128:(fc + 1) * 128],
                                      wpool[:, base_col + dc * 2048 + fc * 128:
                                            base_col + dc * 2048 + (fc + 1) * 128],
                                      xnv[:, dc, :], start=st, stop=sp),
                                  inc=("tt" if dc == FC - 1 and fc == FC - 1
                                       else None))
                    t_a = em.c["tt"]
                    em.wait("scalar", "vv")
                    em.wait("scalar", "tt", t_a)
                    em.op("scalar", lambda e: e.activation(
                        satmp[:, 0:512], pbig[:, 0:512], AF.Silu), inc="aa")
                    em.op("scalar", lambda e: e.activation(
                        satmp[:, 512:1024], pbig[:, 512:1024], AF.Silu), inc="aa")
                    a_sa = em.c["aa"]
                    em.wait("tensor", "aa", a_sa)
                    for fc in range(FC):
                        for dc in range(FC):
                            em.op("tensor", lambda e, fc=fc, dc=dc,
                                  st=(dc == 0), sp=(dc == FC - 1),
                                  base_col=base_col: e.matmul(
                                      pbig[:, fc * 128:(fc + 1) * 128],
                                      wpool[:, base_col + dc * 2048 + 1024 + fc * 128:
                                            base_col + dc * 2048 + 1024 + (fc + 1) * 128],
                                      xnv[:, dc, :], start=st, stop=sp),
                                  inc=("tt" if dc == FC - 1 and fc == FC - 1
                                       else None))
                    t_b = em.c["tt"]
                    em.wait("vector", "tt", t_b)
                    for hb in range(2):
                        em.op("vector", lambda e, dst_bf=dst_bf, hb=hb:
                              e.tensor_tensor(
                                  dst_bf[:, hb * 512:(hb + 1) * 512],
                                  satmp[:, hb * 512:(hb + 1) * 512],
                                  pbig[:, hb * 512:(hb + 1) * 512],
                                  op=ALU.mult), inc="vv")
                    return em.c["vv"]

                v_sw = mm_hidden(0, swb)
                em.wait("tensor", "vv", v_sw)
                swv = swb.rearrange("p (f t) -> p f t", f=FC)
                for half in range(2):
                    for hc in range(FC):
                        em.op("tensor", lambda e, half=half, hc=hc,
                              st=(hc == 0), sp=(hc == FC - 1): e.matmul(
                                  pbig[:, half * 512:(half + 1) * 512],
                                  swv[:, hc, :],
                                  wpool[:, 32768 + hc * 1024 + half * 512:
                                        32768 + hc * 1024 + (half + 1) * 512],
                                  start=st, stop=sp),
                              inc=("tt" if half == 1 and hc == FC - 1 else None))
                tso = em.c["tt"]
                em.wait("vector", "tt", tso)
                for hb in range(2):
                    em.op("vector", lambda e, lci=lci, hb=hb: e.tensor_copy(
                        accq[:, lci * D + hb * 512:lci * D + (hb + 1) * 512],
                        pbig[:, hb * 512:(hb + 1) * 512]), inc="vv")
                em.wait("tensor", "vv", em.c["vv"])
                mm_hidden(16384, hidq[:, lci * D:(lci + 1) * D])
                em.wait("tensor", "vv", em.c["vv"])

            for e_i in range(8):
                dbslot = 49152 if e_i % 2 == 0 else 57344
                em.wait("gpsimd", "tt", w2_slot_user[e_i % 2])
                em.dma("gpsimd", wpool[:, dbslot:dbslot + 8192], dw2[e_i], "dg")
                w2l = em.c["dg"]
                em.wait("tensor", "dg", w2l)
                for ci in grp:
                    lci = ci - g0
                    hv = hidq[:, lci * D:(lci + 1) * D].rearrange(
                        "p (f t) -> p f t", f=FC)
                    for half in range(2):
                        for hc in range(FC):
                            em.op("tensor", lambda e, half=half, hc=hc,
                                  st=(hc == 0), sp=(hc == FC - 1), hv=hv,
                                  dbslot=dbslot: e.matmul(
                                      pbig[:, half * 512:(half + 1) * 512],
                                      hv[:, hc, :],
                                      wpool[:, dbslot + hc * 1024 + half * 512:
                                            dbslot + hc * 1024 + (half + 1) * 512],
                                      start=st, stop=sp),
                                  inc=("tt" if half == 1 and hc == FC - 1
                                       else None))
                    tex = em.c["tt"]
                    em.wait("vector", "tt", tex)
                    for hb in range(2):
                        em.op("vector", lambda e, lci=lci, ci=ci, e_i=e_i, hb=hb:
                              e.scalar_tensor_tensor(
                                  accq[:, lci * D + hb * 512:lci * D + (hb + 1) * 512],
                                  pbig[:, hb * 512:(hb + 1) * 512],
                                  wdense[:, ci * 8 + e_i:ci * 8 + e_i + 1],
                                  accq[:, lci * D + hb * 512:lci * D + (hb + 1) * 512],
                                  op0=ALU.mult, op1=ALU.add), inc="vv")
                    em.wait("tensor", "vv", em.c["vv"])
                w2_slot_user[e_i % 2] = em.c["tt"]

            for ci in grp:
                lci = ci - g0
                em.wait("scalar", "vv")
                em.wait("scalar", "dx")
                em.op("scalar", lambda e, lci=lci: e.activation(
                    tmpa[:], accq[:, lci * D:(lci + 1) * D], AF.Silu), inc="aa")
                agf = em.c["aa"]
                em.wait("sync", "vv")
                em.dma("sync", tmpb[:], dscr_s[ci * TP:(ci + 1) * TP, :], "dx")
                sxl = em.c["dx"]
                em.wait("vector", "aa", agf)
                em.wait("vector", "dx", sxl)
                em.op("vector", lambda e: e.tensor_tensor(
                    tmpa[:], tmpa[:], tmpb[:], op=ALU.mult), inc="vv")
                vsg = em.c["vv"]
                em.wait("sync", "vv", vsg)
                em.dma("sync", tmpb[:], dx[ci * TP:(ci + 1) * TP, :], "dx")
                xxl = em.c["dx"]
                em.wait("vector", "dx", xxl)
                em.op("vector", lambda e: e.tensor_tensor(
                    tmpa[:], tmpa[:], tmpb[:], op=ALU.add), inc="vv")
                vout = em.c["vv"]
                em.wait("sync", "vv", vout)
                em.dma("sync", dout[ci * TP:(ci + 1) * TP, :], tmpa[:], "dx")

        em.wait("sync", "dx", em.c["dx"])
        em.wait("sync", "dg", em.c["dg"])

        def replay(eng_name):
            def run(handle):
                for item in em.q[eng_name]:
                    if item[0] == "w":
                        handle.wait_ge(sems[item[1]], item[2])
                    else:
                        instr = item[1](handle)
                        for s, n in item[2]:
                            instr.then_inc(sems[s], n)
            return run

        block.sync(replay("sync"))
        block.tensor(replay("tensor"))
        block.vector(replay("vector"))
        block.scalar(replay("scalar"))
        block.gpsimd(replay("gpsimd"))

    return nc


def _prep(inputs):
    f32 = np.float32
    fmap = np.array([(fp % 16) * 64 + fp // 16 for fp in range(D)])

    def lhsT_tiles(Wt):
        return np.ascontiguousarray(
            Wt.reshape(8, 128, Wt.shape[1]).transpose(1, 0, 2).reshape(128, -1))

    g = lambda k: np.asarray(inputs[k], f32)
    mats = []
    for nm, perm in [("Wr", True), ("Wk", True), ("Wg", True), ("Wt", True),
                     ("Wv", False), ("Wa", False), ("Wb", False)]:
        W = g("L0_attn_" + nm)
        if perm:
            W = W[fmap]
        mats.append(lhsT_tiles(np.ascontiguousarray(W.T)))
    wproj = np.concatenate(mats, axis=1)
    wo = lhsT_tiles(np.ascontiguousarray(g("L0_attn_Wo").T[fmap]))
    w_in = lhsT_tiles(g("L0_ffn_w_shared_in"))
    w1 = lhsT_tiles(g("L0_ffn_w1_shared"))
    w_out = lhsT_tiles(g("L0_ffn_w_shared_out"))
    wgate = lhsT_tiles(g("L0_ffn_w_gate"))
    wffn = np.concatenate([w_in, w1, w_out, wgate], axis=1)
    w2 = np.stack([lhsT_tiles(g("L0_ffn_w2")[e]) for e in range(8)])
    lnw_rep = np.ascontiguousarray(
        np.broadcast_to(g("L0_attn_ln_weight")[None, :], (128, D)))
    gb_rep = np.ascontiguousarray(
        np.broadcast_to(g("L0_ffn_gate_bias")[None, :], (128, 8)))
    cw = g("L0_attn_conv_w")[:, 0, :]
    cw_f = np.ascontiguousarray(
        cw.reshape(8, 128, 4).transpose(1, 0, 2).reshape(128, 32))
    bdp = np.ascontiguousarray(g("L0_attn_base_decay")[fmap].reshape(8, 128).T)
    sel = np.zeros((16, 128), f32)
    for p in range(128):
        sel[p % 16, p] = 1.0
    ident = np.eye(128, dtype=f32)
    half = np.zeros((128, 2), f32)
    half[0:64, 0] = 1.0
    half[64:128, 1] = 1.0
    return dict(wproj=wproj, wo=wo, wffn=wffn, w2=w2, lnw_rep=lnw_rep,
                gb_rep=gb_rep, cw_f=cw_f, bdp=bdp, sel=sel, ident=ident,
                halfones=half)


def kernel(**inputs):
    x = np.asarray(inputs["L0_attn_x"], np.float32)
    B, L, _ = x.shape
    shared = _prep(inputs)
    nc = build_nc(L)
    in_maps = []
    for b in range(B):
        m = dict(shared)
        m["x"] = np.ascontiguousarray(x[b])
        in_maps.append(m)
    res = run_bass_kernel_spmd(nc, in_maps, core_ids=list(range(B)))
    return np.stack([res.results[b]["out"] for b in range(B)], axis=0)


# version 07 (final)
# speedup vs baseline: 1.3406x; 1.0025x over previous
import numpy as np
import ml_dtypes
import concourse.bass as bass
import concourse.mybir as mybir
from concourse.bass_utils import run_bass_kernel_spmd

F32 = mybir.dt.float32
BF16 = mybir.dt.bfloat16
AF = mybir.ActivationFunctionType
ALU = mybir.AluOpType
AX = mybir.AxisListType

D = 1024
H = 16
FC = 8
TP = 128
THJ = 16
TRS = 8
EPS_ST = 1e-5
EPS_RMS = 1e-6
SQS = float(2.0 ** -20)
SQS2 = float(2.0 ** -44)
SQB = float(2.0 ** 22 / 8.0)
SNB = float(2.0 ** 20 / 32.0)

ENGINES = ("sync", "tensor", "vector", "scalar", "gpsimd")


class Em:
    def __init__(self, sem_names):
        self.q = {k: [] for k in ENGINES}
        self.c = {k: 0 for k in sem_names}
        self.serialize = False

    def wait(self, eng, sem, val=None):
        if val is None:
            val = self.c[sem]
        if val > 0:
            self.q[eng].append(("w", sem, val))

    def op(self, eng, fn, inc=None):
        if self.serialize:
            for sname in self.c:
                self.wait(eng, sname)
        incs = ()
        if inc is not None:
            incs = ((inc, 1),)
            self.c[inc] += 1
        self.q[eng].append(("op", fn, incs))

    def dma(self, eng, out, in_, sem):
        if eng == "sync":
            self.wait("sync", sem)
        def fn(e, out=out, in_=in_):
            return e.dma_start(out=out, in_=in_)
        self.q[eng].append(("op", fn, ((sem, 16),)))
        self.c[sem] += 16


def build_nc(L=2048):
    NCH = L // TP
    NSUB = TP // THJ
    groups = [list(range(g, min(g + 2, NCH))) for g in range(0, NCH, 2)]
    nc = bass.Bass("TRN2", target_bir_lowering=False,
                   detect_race_conditions=False)

    dx = nc.dram_tensor("x", [L, D], F32, kind="ExternalInput").ap()
    dwproj = nc.dram_tensor("wproj", [128, 7 * 8192], F32, kind="ExternalInput").ap()
    dwo = nc.dram_tensor("wo", [128, 8192], F32, kind="ExternalInput").ap()
    dwffn = nc.dram_tensor("wffn", [128, 41024], F32, kind="ExternalInput").ap()
    dw2 = nc.dram_tensor("w2", [8, 128, 8192], F32, kind="ExternalInput").ap()
    dlnw = nc.dram_tensor("lnw_rep", [128, D], F32, kind="ExternalInput").ap()
    dgb = nc.dram_tensor("gb_rep", [128, 8], F32, kind="ExternalInput").ap()
    dcw = nc.dram_tensor("cw_f", [128, 32], F32, kind="ExternalInput").ap()
    dbdp = nc.dram_tensor("bdp", [128, 8], F32, kind="ExternalInput").ap()
    dsel = nc.dram_tensor("sel", [16, 128], F32, kind="ExternalInput").ap()
    dident = nc.dram_tensor("ident", [128, 128], F32, kind="ExternalInput").ap()
    dhalf = nc.dram_tensor("halfones", [128, 2], F32, kind="ExternalInput").ap()
    dout = nc.dram_tensor("out", [L, D], F32, kind="ExternalOutput").ap()
    dscr_s = nc.dram_tensor("scr_s", [L, D], F32).ap()
    dscr_xnb = nc.dram_tensor("scr_xnb", [NCH, 128, D], BF16).ap()
    dscr_cv = nc.dram_tensor("scr_cv", [3, D, TP], F32).ap()

    sem_names = ["dx", "dg", "tt", "vv", "aa"]

    from contextlib import ExitStack
    with ExitStack() as _st:
        wpool = _st.enter_context(nc.sbuf_tensor("wpool", [128, 65536], BF16))
        xn_tok = _st.enter_context(nc.sbuf_tensor("xn_tok", [128, D], F32))
        xn_fm = _st.enter_context(nc.sbuf_tensor("xn_fm", [128, FC * (TP + 3)], F32))
        kkfm = _st.enter_context(nc.sbuf_tensor("kkfm", [128, D], F32))
        cfm = _st.enter_context(nc.sbuf_tensor("cfm", [128, D], F32))
        vfm = _st.enter_context(nc.sbuf_tensor("vfm", [128, D], F32))
        winv = _st.enter_context(nc.sbuf_tensor("winv", [128, D], F32))
        kwb = _st.enter_context(nc.sbuf_tensor("kwb", [128, D], F32))
        tmpa = _st.enter_context(nc.sbuf_tensor("tmpa", [128, D], F32))
        rsbuf = _st.enter_context(nc.sbuf_tensor("rsbuf", [128, FC * TP], F32))
        lambuf = _st.enter_context(nc.sbuf_tensor("lambuf", [128, FC * TP], F32))
        zst = _st.enter_context(nc.sbuf_tensor("zst", [128, 512], F32))
        sm = _st.enter_context(nc.sbuf_tensor("sm", [128, 16], F32))
        sm2 = _st.enter_context(nc.sbuf_tensor("sm2", [128, 16], F32))
        sm3 = _st.enter_context(nc.sbuf_tensor("sm3", [128, 32], F32))
        sm4 = _st.enter_context(nc.sbuf_tensor("sm4", [128, 16], F32))
        sctmp = _st.enter_context(nc.sbuf_tensor("sctmp", [128, 64], F32))
        cwf = _st.enter_context(nc.sbuf_tensor("cwf", [128, 32], F32))
        bdps = _st.enter_context(nc.sbuf_tensor("bdps", [128, 8], F32))
        lnw = _st.enter_context(nc.sbuf_tensor("lnw", [128, D], BF16))
        gbr = _st.enter_context(nc.sbuf_tensor("gbr", [128, 8], F32))
        self32 = _st.enter_context(nc.sbuf_tensor("self32", [16, 128], F32))
        ident32 = _st.enter_context(nc.sbuf_tensor("ident32", [128, 128], F32))
        half2 = _st.enter_context(nc.sbuf_tensor("half2", [128, 2], F32))
        upool = _st.enter_context(nc.sbuf_tensor("upool", [128, 2 * 194 * THJ], F32))
        xnfmb = _st.enter_context(nc.sbuf_tensor("xnfmb", [128, D], BF16))
        up = upool[:]
        HJC = 194 * THJ
        hjbufs_ap = (up[0:16, 0:HJC], up[0:16, HJC:2 * HJC])
        hidq = up[:, 0:1024].bitcast(BF16)
        accq = up[:, 1024:3072]
        swb = up[:, 3072:3584].bitcast(BF16)
        satmp = up[:, 3584:4608]
        wdense = up[:, 4608:4608 + NCH * 8]
        tmpb = vfm
        gob = cfm[:, 0:512].bitcast(BF16)
        ebd = bdps
        pm = _st.enter_context(nc.psum_tensor("pm", [128, 512], F32))
        pm2 = _st.enter_context(nc.psum_tensor("pm2", [128, 512], F32))
        t1b = _st.enter_context(nc.psum_tensor("t1b", [128, 512], F32))
        prep0 = _st.enter_context(nc.psum_tensor("prep0", [128, 512], F32))
        prep1 = _st.enter_context(nc.psum_tensor("prep1", [128, 512], F32))
        pbig = _st.enter_context(nc.psum_tensor("pbig", [128, 1024], F32))
        paux = _st.enter_context(nc.psum_tensor("paux", [128, 512], F32))
        s_dx = _st.enter_context(nc.semaphore("s_dx"))
        s_dg = _st.enter_context(nc.semaphore("s_dg"))
        s_tt = _st.enter_context(nc.semaphore("s_tt"))
        s_vv = _st.enter_context(nc.semaphore("s_vv"))
        s_aa = _st.enter_context(nc.semaphore("s_aa"))
        block = _st.enter_context(nc.Block())
        sems = {"dx": s_dx, "dg": s_dg, "tt": s_tt, "vv": s_vv, "aa": s_aa}
        em = Em(sem_names)

        zv = zst[:].rearrange("p (f j) -> p f j", f=FC)
        rsb = rsbuf[:].rearrange("p (t f) -> p t f", f=FC)
        lmb = lambuf[:].rearrange("p (t f) -> p t f", f=FC)
        xnfm = xn_fm[:].rearrange("p (f t) -> p f t", f=FC)
        winvv = winv[:].rearrange("p (f t) -> p f t", f=FC)
        kwv = kwb[:].rearrange("p (f t) -> p f t", f=FC)

        rs_trk = sm[:, 0:8]
        dnegs = (sm[:, 8:16], sm4[:, 0:8])
        sqvs = (sm3[:, 8:16], sm4[:, 8:16])
        p_sc = sm2[:, 0:8]
        q_sc = sm2[:, 8:16]
        mue = sm3[:, 0:8]
        sqv = sm3[:, 8:16]
        linv = sm3[:, 16:24]
        rs_tmp = sm3[:, 24:32]

        def bc8(ap2d, n=64):
            return ap2d.unsqueeze(2).broadcast_to([128, 8, n])

        def bc64(ap2d):
            return ap2d.unsqueeze(1).broadcast_to([128, 8, 64])

        for dst, src in [(cwf[:], dcw), (bdps[:], dbdp),
                         (gbr[:], dgb), (self32[:], dsel), (ident32[:], dident),
                         (half2[:], dhalf)]:
            em.dma("sync", dst, src, "dx")
        const_ld = em.c["dx"]
        em.dma("gpsimd", lnw[:], dlnw, "dg")
        for i in range(7):
            em.dma("gpsimd", wpool[:, i * 8192:(i + 1) * 8192],
                   dwproj[:, i * 8192:(i + 1) * 8192], "dg")
        em.dma("gpsimd", wpool[:, 7 * 8192:8 * 8192], dwo, "dg")
        attn_w_ld = em.c["dg"]

        em.wait("scalar", "dx", const_ld)
        em.op("scalar", lambda e: e.activation(bdps[:], bdps[:], AF.Exp,
                                               scale=-1.0), inc="aa")
        a_ebd = em.c["aa"]
        em.op("vector", lambda e: e.memset(zst[:], 0.0), inc="vv")
        em.op("vector", lambda e: e.memset(rs_trk, 0.0), inc="vv")
        em.wait("vector", "dx", const_ld)
        em.wait("vector", "dg", attn_w_ld)
        em.wait("vector", "aa", a_ebd)
        em.wait("tensor", "dx", const_ld)
        em.wait("tensor", "dg", attn_w_ld)

        wslice = {}
        for i, nm in enumerate(["r", "k", "g", "wt", "v", "a", "kk"]):
            wslice[nm] = wpool[:, i * 8192:(i + 1) * 8192]
        wslice["wo"] = wpool[:, 7 * 8192:8 * 8192]

        rep_free = {0: 0, 1: 0}
        pend_s6 = [None]

        for ci in range(NCH):
            em.wait("sync", "vv")
            em.dma("sync", tmpb[:], dx[ci * TP:(ci + 1) * TP, :], "dx")
            xld = em.c["dx"]
            em.wait("scalar", "vv")
            em.wait("scalar", "dx", xld)
            em.op("scalar", lambda e: e.activation(tmpa[:], tmpb[:], AF.Square,
                                                   accum_out=sm3[:, 0:1]), inc="aa")
            em.op("scalar", lambda e: e.activation(xn_tok[:], tmpb[:], AF.Identity,
                                                   accum_out=sm3[:, 1:2]), inc="aa")
            aln = em.c["aa"]
            em.wait("vector", "aa", aln)
            em.op("vector", lambda e: e.tensor_scalar(
                sm3[:, 2:3], sm3[:, 1:2], float(1.0 / D), None, op0=ALU.mult), inc="vv")
            em.op("vector", lambda e: e.tensor_scalar(
                sm3[:, 3:4], sm3[:, 0:1], float(1.0 / D), None, op0=ALU.mult), inc="vv")
            em.op("vector", lambda e: e.tensor_tensor(
                sm3[:, 4:5], sm3[:, 2:3], sm3[:, 2:3], op=ALU.mult), inc="vv")
            em.wait("vector", "vv")
            em.op("vector", lambda e: e.tensor_tensor(
                sm3[:, 5:6], sm3[:, 3:4], sm3[:, 4:5], op=ALU.subtract), inc="vv")
            em.wait("vector", "vv")
            em.op("vector", lambda e: e.tensor_scalar(
                sm3[:, 5:6], sm3[:, 5:6], 1e-5, None, op0=ALU.add), inc="vv")
            vvar = em.c["vv"]
            em.wait("scalar", "vv", vvar)
            em.op("scalar", lambda e: e.activation(sm3[:, 6:7], sm3[:, 5:6],
                                                   AF.Sqrt), inc="aa")
            asd = em.c["aa"]
            em.wait("vector", "aa", asd)
            em.op("vector", lambda e: e.reciprocal(sm3[:, 7:8], sm3[:, 6:7]), inc="vv")
            em.wait("vector", "vv")
            em.op("vector", lambda e: e.tensor_scalar(
                xn_tok[:], tmpb[:], sm3[:, 2:3], sm3[:, 7:8],
                op0=ALU.subtract, op1=ALU.mult), inc="vv")
            vxn = em.c["vv"]

            em.wait("tensor", "vv", vxn)
            for fc in range(FC):
                em.op("tensor", lambda e, fc=fc: e.transpose(
                    pbig[:, fc * 128:(fc + 1) * 128],
                    xn_tok[:, fc * 128:(fc + 1) * 128], ident32[:]),
                    inc=("tt" if fc == FC - 1 else None))
            ttr = em.c["tt"]
            em.wait("vector", "tt", ttr)
            if ci == 0:
                em.op("vector", lambda e: e.memset(xn_fm[:], 0.0), inc="vv")
            else:
                em.op("vector", lambda e: e.tensor_copy(
                    xnfm[:, :, 0:3], xnfm[:, :, TP:TP + 3]), inc="vv")
            for hb in range(2):
                em.op("vector", lambda e, hb=hb: e.tensor_copy(
                    xnfm[:, hb * 4:hb * 4 + 4, 3:TP + 3],
                    pbig[:, hb * 512:(hb + 1) * 512].rearrange(
                        "p (f t) -> p f t", f=4)), inc="vv")
            vxnfm = em.c["vv"]
            em.wait("scalar", "tt", ttr)
            em.wait("scalar", "dx")
            em.op("vector", lambda e: e.tensor_copy(
                xnfmb[:, 0:512], pbig[:, 0:512]), inc="aa")
            em.op("vector", lambda e: e.tensor_copy(
                xnfmb[:, 512:1024], pbig[:, 512:1024]), inc="aa")
            axnb = em.c["aa"]
            em.wait("sync", "aa", axnb)
            em.dma("sync", dscr_xnb[ci], xnfmb[:], "dx")

            cw3 = cwf[:].rearrange("p (f k) -> p f k", f=FC)
            em.op("vector", lambda e: e.tensor_tensor(
                cfm[:].rearrange("p (f t) -> p f t", f=FC),
                xnfm[:, :, 0:TP],
                cw3[:, :, 0:1].broadcast_to([128, FC, TP]), op=ALU.mult), inc="vv")
            for j in range(1, 4):
                em.op("vector", lambda e, j=j: e.tensor_tensor(
                    tmpa[:].rearrange("p (f t) -> p f t", f=FC),
                    xnfm[:, :, j:TP + j],
                    cw3[:, :, j:j + 1].broadcast_to([128, FC, TP]),
                    op=ALU.mult), inc="vv")
                em.op("vector", lambda e: e.tensor_tensor(
                    cfm[:], cfm[:], tmpa[:], op=ALU.add), inc="vv")
            vxc = em.c["vv"]
            em.wait("tensor", "vv", vxc)
            for fc in range(FC):
                em.op("tensor", lambda e, fc=fc: e.transpose(
                    pbig[:, fc * 128:(fc + 1) * 128],
                    cfm[:, fc * 128:(fc + 1) * 128], ident32[:]),
                    inc=("tt" if fc == FC - 1 else None))
            ttr2 = em.c["tt"]
            em.wait("vector", "tt", ttr2)
            for hb in range(2):
                em.op("vector", lambda e, hb=hb: e.scalar_tensor_tensor(
                    xn_tok[:, hb * 512:(hb + 1) * 512],
                    pbig[:, hb * 512:(hb + 1) * 512], 0.1,
                    xn_tok[:, hb * 512:(hb + 1) * 512],
                    op0=ALU.mult, op1=ALU.add), inc="vv")
            vxg = em.c["vv"]

            def proj(nm):
                ws = wslice[nm]
                for fc in range(FC):
                    for dc in range(FC):
                        lhsT = ws[:, dc * 1024 + fc * 128:dc * 1024 + (fc + 1) * 128]
                        rhs = xnfmb[:].rearrange("p (f t) -> p f t", f=FC)[:, dc, :]
                        em.op("tensor", lambda e, lhsT=lhsT, rhs=rhs, fc=fc,
                              st=(dc == 0), sp=(dc == FC - 1):
                              e.matmul(pbig[:, fc * 128:(fc + 1) * 128], lhsT,
                                       rhs, start=st, stop=sp),
                              inc=("tt" if dc == FC - 1 and fc == FC - 1 else None))
                return em.c["tt"]

            em.wait("tensor", "aa", axnb)
            em.wait("tensor", "vv", vxg)
            twt = proj("wt")
            em.wait("scalar", "vv")
            em.wait("scalar", "dx")
            em.wait("scalar", "tt", twt)
            em.op("scalar", lambda e: e.activation(
                tmpa[:, 0:512], pbig[:, 0:512], AF.Exp), inc="aa")
            em.op("scalar", lambda e: e.activation(
                tmpa[:, 512:1024], pbig[:, 512:1024], AF.Exp), inc="aa")
            em.op("scalar", lambda e: e.activation(tmpb[:], tmpa[:], AF.Exp,
                                                   scale=-1.0), inc="aa")
            awt = em.c["aa"]
            em.wait("vector", "aa", awt)
            em.op("vector", lambda e: e.tensor_tensor(
                tmpa[:].rearrange("p (f t) -> p f t", f=FC),
                tmpb[:].rearrange("p (f t) -> p f t", f=FC),
                bc8(ebd[:], TP), op=ALU.mult), inc="vv")
            em.op("vector", lambda e: e.tensor_scalar(
                tmpa[:], tmpa[:], 0.01, 0.99, op0=ALU.max, op1=ALU.min), inc="vv")
            em.op("vector", lambda e: e.reciprocal(winv[:], tmpa[:]), inc="vv")
            vwt = em.c["vv"]

            em.wait("tensor", "vv", vwt)
            tk = proj("k")
            em.wait("vector", "tt", tk)
            for hb in range(2):
                em.op("vector", lambda e, hb=hb: e.tensor_tensor(
                    kwb[:, hb * 512:(hb + 1) * 512],
                    pbig[:, hb * 512:(hb + 1) * 512],
                    winv[:, hb * 512:(hb + 1) * 512], op=ALU.mult), inc="vv")
            vk = em.c["vv"]

            em.wait("tensor", "vv", vk)
            tkk = proj("kk")
            em.wait("vector", "tt", tkk)
            for hb in range(2):
                em.op("vector", lambda e, hb=hb: e.tensor_copy(
                    kkfm[:, hb * 512:(hb + 1) * 512],
                    pbig[:, hb * 512:(hb + 1) * 512]), inc="vv")
            vkk = em.c["vv"]

            em.wait("tensor", "vv", vkk)
            ta = proj("a")
            em.wait("vector", "tt", ta)
            for hb in range(2):
                em.op("vector", lambda e, hb=hb: e.tensor_tensor(
                    cfm[:, hb * 512:(hb + 1) * 512],
                    kkfm[:, hb * 512:(hb + 1) * 512],
                    pbig[:, hb * 512:(hb + 1) * 512], op=ALU.mult), inc="vv")
            vc = em.c["vv"]

            em.wait("tensor", "vv", vc)
            tv = proj("v")
            em.wait("vector", "tt", tv)
            for hb in range(2):
                em.op("vector", lambda e, hb=hb: e.tensor_copy(
                    vfm[:, hb * 512:(hb + 1) * 512],
                    pbig[:, hb * 512:(hb + 1) * 512]), inc="vv")
            vv_ = em.c["vv"]

            em.wait("tensor", "vv", vv_)
            for t_i, src in enumerate((cfm, vfm)):
                for fc in range(FC):
                    em.op("tensor", lambda e, src=src, fc=fc, t_i=t_i: e.matmul(
                        paux[:, t_i * 16 + fc * 2:t_i * 16 + fc * 2 + 2],
                        src[:, fc * 128:(fc + 1) * 128], half2[:],
                        start=True, stop=True),
                        inc=("tt" if t_i == 1 and fc == FC - 1 else None))
            tsc = em.c["tt"]
            em.wait("vector", "tt", tsc)
            if ci == 0:
                em.op("vector", lambda e: e.memset(sctmp[:], 0.0), inc="vv")
            em.op("vector", lambda e: e.tensor_copy(sctmp[:, 0:16], paux[:, 0:16]),
                  inc="vv")
            em.op("vector", lambda e: e.tensor_copy(sctmp[:, 32:48], paux[:, 16:32]),
                  inc="vv")
            vsc = em.c["vv"]
            em.wait("tensor", "vv", vsc)
            em.op("tensor", lambda e: e.transpose(
                paux[0:64, 0:128], sctmp[:], ident32[:]), inc="tt")
            tsc2 = em.c["tt"]

            em.wait("sync", "vv", vv_)
            em.dma("sync", dscr_cv[0].rearrange("(fc p) t -> p fc t", p=128),
                   kkfm[:].rearrange("p (fc t) -> p fc t", fc=FC), "dx")
            em.dma("sync", dscr_cv[1].rearrange("(fc p) t -> p fc t", p=128),
                   cfm[:].rearrange("p (fc t) -> p fc t", fc=FC), "dx")
            em.dma("sync", dscr_cv[2].rearrange("(fc p) t -> p fc t", p=128),
                   vfm[:].rearrange("p (fc t) -> p fc t", fc=FC), "dx")
            cvout = em.c["dx"]
            rg = kkfm

            em.wait("tensor", "vv", vv_)
            tr = proj("r")
            em.wait("scalar", "tt", tr)
            em.wait("scalar", "dx", cvout)
            em.op("scalar", lambda e, rg=rg: e.activation(
                rg[:, 0:512], pbig[:, 0:512], AF.Sigmoid), inc="aa")
            em.op("scalar", lambda e, rg=rg: e.activation(
                rg[:, 512:1024], pbig[:, 512:1024], AF.Sigmoid), inc="aa")
            ar = em.c["aa"]

            em.wait("tensor", "aa", ar)
            tg = proj("g")
            em.wait("scalar", "tt", tg)
            em.wait("scalar", "vv")
            em.op("scalar", lambda e: e.activation(
                tmpb[:, 0:512], pbig[:, 0:512], AF.Sigmoid), inc="aa")
            em.op("scalar", lambda e: e.activation(
                tmpb[:, 512:1024], pbig[:, 512:1024], AF.Sigmoid), inc="aa")
            ag = em.c["aa"]
            em.wait("vector", "aa", ag)
            em.op("vector", lambda e, rg=rg: e.tensor_tensor(
                rg[:], rg[:], tmpb[:], op=ALU.mult), inc="vv")
            vrg = em.c["vv"]

            hjbufs = hjbufs_ap
            hj_dx = {}
            hj_vv = {}
            sel_done = {}

            def emit_hj_load(sub):
                hj = hjbufs[sub % 2]
                hjv = hj.rearrange("h (c t) -> h c t", c=194)
                em.wait("sync", "dx", cvout)
                if sub >= 2:
                    em.wait("sync", "tt", sel_done[sub - 2])
                for vi in range(3):
                    src = dscr_cv[vi].rearrange("(h j) t -> h j t", h=H)[
                        :, :, sub * THJ:(sub + 1) * THJ]
                    em.dma("sync", hjv[:, vi * 64:(vi + 1) * 64, :], src, "dx")
                hj_dx[sub] = em.c["dx"]
                em.wait("vector", "tt", tsc2)
                em.op("vector", lambda e, hjv=hjv, sub=sub: e.tensor_copy(
                    hjv[:, 192:193, :],
                    paux[0:16, sub * THJ:(sub + 1) * THJ].unsqueeze(1)), inc="vv")
                em.op("vector", lambda e, hjv=hjv, sub=sub: e.tensor_copy(
                    hjv[:, 193:194, :],
                    paux[32:48, sub * THJ:(sub + 1) * THJ].unsqueeze(1)), inc="vv")
                hj_vv[sub] = em.c["vv"]

            emit_hj_load(0)
            if NSUB > 1:
                emit_hj_load(1)

            for sub in range(NSUB):
                hj = hjbufs[sub % 2]
                hjv = hj.rearrange("h (c t) -> h c t", c=194)
                for ts in range(THJ):
                    tt_ = sub * THJ + ts
                    gt = ci * TP + tt_
                    par = gt % 2
                    prep = (prep0, prep1)[par]
                    dneg = dnegs[par]
                    sqv = sqvs[par]
                    after_rs = (gt % TRS == 0)

                    em.wait("tensor", "dx", hj_dx[sub])
                    em.wait("tensor", "vv", hj_vv[sub])
                    if rep_free[par] > 0:
                        em.wait("tensor", "vv", rep_free[par])
                    em.op("tensor", lambda e, prep=prep, hjv=hjv, ts=ts: e.matmul(
                        prep[:, 0:194], self32[:], hjv[:, :, ts],
                        start=True, stop=True), inc="tt")
                    trep = em.c["tt"]

                    wsl = winvv[:, :, tt_]
                    kwsl = kwv[:, :, tt_]

                    em.wait("vector", "tt", trep)
                    em.op("vector", lambda e, prep=prep: e.tensor_tensor(
                        pm[:].rearrange("p (f j) -> p f j", f=FC), zv,
                        bc64(prep[:, 0:64]), op=ALU.mult), inc="vv")
                    em.op("vector", lambda e, dneg=dneg: e.tensor_reduce(
                        dneg, pm[:].rearrange("p (f j) -> p f j", f=FC),
                        axis=AX.X, op=ALU.add, negate=True), inc="vv")
                    if pend_s6[0] is not None:
                        pend_s6[0]()
                        pend_s6[0] = None
                    else:
                        em.wait("vector", "vv")
                    em.op("vector", lambda e, wsl=wsl, dneg=dneg: e.tensor_tensor(
                        p_sc, dneg, wsl, op=ALU.mult), inc="vv")
                    if after_rs:
                        qop = kwsl
                        em.wait("vector", "vv")
                    else:
                        lprev = lmb[:, tt_ - 1, :]
                        em.op("vector", lambda e, lprev=lprev, kwsl=kwsl:
                              e.tensor_tensor(q_sc, lprev, kwsl, op=ALU.mult),
                              inc="vv")
                        qop = q_sc
                    em.op("vector", lambda e, prep=prep: e.tensor_tensor(
                        t1b[:].rearrange("p (f j) -> p f j", f=FC),
                        bc64(prep[:, 64:128]), bc8(p_sc), op=ALU.mult), inc="vv")
                    em.op("vector", lambda e: e.tensor_tensor(
                        zst[:], zst[:], t1b[:], op=ALU.add), inc="vv")
                    em.op("vector", lambda e, prep=prep, qop=qop: e.tensor_tensor(
                        t1b[:].rearrange("p (f j) -> p f j", f=FC),
                        bc64(prep[:, 128:192]), bc8(qop), op=ALU.mult), inc="vv")
                    rs_prev = rs_trk if after_rs else rsb[:, tt_ - 1, :]
                    em.op("vector", lambda e, prep=prep, rs_prev=rs_prev:
                          e.scalar_tensor_tensor(rs_tmp, p_sc, prep[:, 192:193],
                                                 rs_prev, op0=ALU.mult,
                                                 op1=ALU.add), inc="vv")
                    em.op("vector", lambda e: e.tensor_tensor(
                        zst[:], zst[:], t1b[:], op=ALU.add), inc="vv")
                    em.op("vector", lambda e, prep=prep, qop=qop, tt_=tt_:
                          e.scalar_tensor_tensor(rsb[:, tt_, :], qop,
                                                 prep[:, 193:194], rs_tmp,
                                                 op0=ALU.mult, op1=ALU.add),
                          inc="vv")
                    rep_free[par] = em.c["vv"]
                    em.op("vector", lambda e: e.scalar_tensor_tensor(
                        pm2[:], zst[:], SQS2, zst[:], op0=ALU.mult, op1=ALU.mult),
                        inc="vv")
                    em.op("vector", lambda e, dneg=dneg: e.tensor_reduce(
                        dneg, pm2[:].rearrange("p (f j) -> p f j", f=FC),
                        axis=AX.X, op=ALU.add), inc="vv")
                    vnsq = em.c["vv"]
                    if after_rs:
                        em.op("vector", lambda e, wsl=wsl: e.tensor_scalar(
                            mue, wsl, EPS_ST, None, op0=ALU.mult), inc="vv")
                    else:
                        lprev = lmb[:, tt_ - 1, :]
                        em.op("vector", lambda e, lprev=lprev, wsl=wsl:
                              e.scalar_tensor_tensor(mue, lprev, EPS_ST, wsl,
                                                     op0=ALU.mult, op1=ALU.mult),
                              inc="vv")
                    em.wait("scalar", "vv", vnsq)
                    em.op("scalar", lambda e, sqv=sqv, dneg=dneg: e.activation(
                        sqv, dneg, AF.Sqrt), inc="aa")
                    asv = em.c["aa"]

                    def make_s6(tt_=tt_, asv=asv, sqv=sqv):
                        def emit():
                            em.wait("vector", "aa", asv)
                            em.op("vector", lambda e, tt_=tt_, sqv=sqv:
                                  e.scalar_tensor_tensor(
                                      lmb[:, tt_, :], sqv, SQB, mue,
                                      op0=ALU.mult, op1=ALU.add), inc="vv")
                        return emit

                    if gt % TRS == TRS - 1:
                        make_s6()()
                        em.wait("vector", "vv")
                        em.op("vector", lambda e, tt_=tt_: e.reciprocal(
                            linv, lmb[:, tt_, :]), inc="vv")
                        em.wait("vector", "vv")
                        em.op("vector", lambda e: e.tensor_tensor(
                            zv, zv, bc8(linv), op=ALU.mult), inc="vv")
                        em.op("vector", lambda e, tt_=tt_: e.tensor_tensor(
                            rs_trk, rsb[:, tt_, :], linv, op=ALU.mult), inc="vv")
                    else:
                        pend_s6[0] = make_s6()
                sel_done[sub] = em.c["tt"]
                if sub + 2 < NSUB:
                    emit_hj_load(sub + 2)

            if pend_s6[0] is not None:
                pend_s6[0]()
                pend_s6[0] = None
            em.wait("vector", "vv")
            em.wait("vector", "dx", cvout)
            em.op("vector", lambda e: e.reciprocal(tmpa[:], lambuf[:]), inc="vv")
            em.op("vector", lambda e: e.tensor_tensor(
                tmpb[:], rsbuf[:], tmpa[:], op=ALU.mult), inc="vv")
            em.op("vector", lambda e, rg=rg: e.tensor_tensor(
                gob[:, 0:FC * TP].rearrange("p (f t) -> p f t", f=FC),
                tmpb[:].rearrange("p (t f) -> p f t", f=FC),
                rg[:].rearrange("p (f t) -> p f t", f=FC), op=ALU.mult), inc="vv")
            vgo = em.c["vv"]

            em.wait("tensor", "vv", vgo)
            gov = gob[:, 0:FC * TP].rearrange("p (f t) -> p f t", f=FC)
            for half in range(2):
                for fcp in range(FC):
                    em.op("tensor", lambda e, half=half, fcp=fcp,
                          st=(fcp == 0), sp=(fcp == FC - 1): e.matmul(
                              pbig[:, half * 512:(half + 1) * 512],
                              gov[:, fcp, :],
                              wslice["wo"][:, fcp * 1024 + half * 512:
                                           fcp * 1024 + (half + 1) * 512],
                              start=st, stop=sp),
                          inc=("tt" if half == 1 and fcp == FC - 1 else None))
            two = em.c["tt"]
            em.wait("vector", "tt", two)
            for hb in range(2):
                em.op("vector", lambda e, hb=hb: e.tensor_copy(
                    tmpb[:, hb * 512:(hb + 1) * 512],
                    pbig[:, hb * 512:(hb + 1) * 512]), inc="vv")
            vwoe = em.c["vv"]
            em.wait("scalar", "vv", vwoe)
            em.op("scalar", lambda e: e.activation(
                tmpa[:], tmpb[:], AF.Square, scale=SQS,
                accum_out=sm3[:, 0:1]), inc="aa")
            em.op("scalar", lambda e: e.activation(sm3[:, 1:2], sm3[:, 0:1],
                                                   AF.Sqrt), inc="aa")
            aso = em.c["aa"]
            em.wait("vector", "aa", aso)
            em.op("vector", lambda e: e.tensor_scalar(
                sm3[:, 2:3], sm3[:, 1:2], SNB, EPS_RMS,
                op0=ALU.mult, op1=ALU.add), inc="vv")
            em.wait("vector", "vv")
            em.op("vector", lambda e: e.reciprocal(sm3[:, 3:4], sm3[:, 2:3]),
                  inc="vv")
            em.wait("vector", "vv")
            em.op("vector", lambda e: e.scalar_tensor_tensor(
                tmpa[:], tmpb[:], sm3[:, 3:4], lnw[:], op0=ALU.mult,
                op1=ALU.mult), inc="vv")
            em.op("vector", lambda e: e.tensor_tensor(
                xn_tok[:], tmpa[:], xn_tok[:], op=ALU.add), inc="vv")
            vs = em.c["vv"]
            em.wait("sync", "vv", vs)
            em.dma("sync", dscr_s[ci * TP:(ci + 1) * TP, :], xn_tok[:], "dx")

        em.wait("gpsimd", "tt", em.c["tt"])
        em.wait("gpsimd", "aa", em.c["aa"])
        em.dma("gpsimd", wpool[:, 0:16384], dwffn[:, 0:16384], "dg")
        em.dma("gpsimd", wpool[:, 16384:32768], dwffn[:, 16384:32768], "dg")
        em.dma("gpsimd", wpool[:, 32768:40960], dwffn[:, 32768:40960], "dg")
        em.dma("gpsimd", wpool[:, 40960:41024], dwffn[:, 40960:41024], "dg")
        ffnw = em.c["dg"]
        em.wait("tensor", "dg", ffnw)
        wgate_sl = wpool[:, 40960:41024]

        w2_slot_user = {0: em.c["tt"], 1: em.c["tt"]}
        evac_done = {0: em.c["vv"], 1: em.c["vv"]}
        thalf = {0: 0, 1: 0}
        lg = sm2[:, 0:8]
        ex = sm2[:, 8:16]
        mask1 = sctmp[:, 0:8]
        masked = sctmp[:, 8:16]
        keep = sctmp[:, 16:24]

        for grp in groups:
            g0 = grp[0]
            for ci in grp:
                lci = ci - g0
                em.wait("sync", "tt")
                em.dma("sync", xnfmb[:], dscr_xnb[ci], "dx")
                xbl = em.c["dx"]
                em.wait("tensor", "dx", xbl)
                xnv = xnfmb[:].rearrange("p (f t) -> p f t", f=FC)
                em.wait("tensor", "vv")
                for dc in range(FC):
                    em.op("tensor", lambda e, dc=dc, st=(dc == 0),
                          sp=(dc == FC - 1): e.matmul(
                              paux[:, 0:8], xnv[:, dc, :],
                              wgate_sl[:, dc * 8:(dc + 1) * 8], start=st, stop=sp),
                          inc=("tt" if dc == FC - 1 else None))
                tgt = em.c["tt"]
                em.wait("vector", "tt", tgt)
                em.op("vector", lambda e: e.tensor_tensor(
                    lg, paux[:, 0:8], gbr[:], op=ALU.add), inc="vv")
                em.wait("vector", "vv")
                em.op("vector", lambda e: e.tensor_reduce(
                    sm3[:, 0:1], lg, axis=AX.X, op=ALU.max, negate=True), inc="vv")
                vlg = em.c["vv"]
                em.wait("scalar", "vv", vlg)
                em.op("scalar", lambda e: e.activation(
                    ex, lg, AF.Exp, bias=sm3[:, 0:1]), inc="aa")
                age = em.c["aa"]
                em.wait("vector", "aa", age)
                em.op("vector", lambda e: e.tensor_reduce(
                    sm3[:, 1:2], ex, axis=AX.X, op=ALU.add), inc="vv")
                em.wait("vector", "vv")
                em.op("vector", lambda e: e.reciprocal(sm3[:, 2:3], sm3[:, 1:2]),
                      inc="vv")
                em.op("vector", lambda e: e.tensor_reduce(
                    sm3[:, 3:4], lg, axis=AX.X, op=ALU.max), inc="vv")
                em.wait("vector", "vv")
                em.op("vector", lambda e: e.tensor_single_scalar(
                    mask1, lg, sm3[:, 3:4], op=ALU.is_ge), inc="vv")
                em.wait("vector", "vv")
                em.op("vector", lambda e: e.scalar_tensor_tensor(
                    masked, mask1, -1e30, lg, op0=ALU.mult, op1=ALU.add), inc="vv")
                em.wait("vector", "vv")
                em.op("vector", lambda e: e.tensor_reduce(
                    sm3[:, 4:5], masked, axis=AX.X, op=ALU.max), inc="vv")
                em.wait("vector", "vv")
                em.op("vector", lambda e: e.tensor_single_scalar(
                    keep, lg, sm3[:, 4:5], op=ALU.is_ge), inc="vv")
                em.wait("vector", "vv")
                em.op("vector", lambda e: e.tensor_tensor(
                    keep, keep, ex, op=ALU.mult), inc="vv")
                em.wait("vector", "vv")
                em.op("vector", lambda e, ci=ci: e.tensor_scalar(
                    wdense[:, ci * 8:(ci + 1) * 8], keep, sm3[:, 2:3], None,
                    op0=ALU.mult), inc="vv")

                def mm_hidden(base_col, dst_bf):
                    for fc in range(FC):
                        for dc in range(FC):
                            em.op("tensor", lambda e, fc=fc, dc=dc,
                                  st=(dc == 0), sp=(dc == FC - 1),
                                  base_col=base_col: e.matmul(
                                      pbig[:, fc * 128:(fc + 1) * 128],
                                      wpool[:, base_col + dc * 2048 + fc * 128:
                                            base_col + dc * 2048 + (fc + 1) * 128],
                                      xnv[:, dc, :], start=st, stop=sp),
                                  inc=("tt" if dc == FC - 1 and fc == FC - 1
                                       else None))
                    t_a = em.c["tt"]
                    em.wait("scalar", "vv")
                    em.wait("scalar", "tt", t_a)
                    em.op("scalar", lambda e: e.activation(
                        satmp[:, 0:512], pbig[:, 0:512], AF.Silu), inc="aa")
                    em.op("scalar", lambda e: e.activation(
                        satmp[:, 512:1024], pbig[:, 512:1024], AF.Silu), inc="aa")
                    a_sa = em.c["aa"]
                    em.wait("tensor", "aa", a_sa)
                    for fc in range(FC):
                        for dc in range(FC):
                            em.op("tensor", lambda e, fc=fc, dc=dc,
                                  st=(dc == 0), sp=(dc == FC - 1),
                                  base_col=base_col: e.matmul(
                                      pbig[:, fc * 128:(fc + 1) * 128],
                                      wpool[:, base_col + dc * 2048 + 1024 + fc * 128:
                                            base_col + dc * 2048 + 1024 + (fc + 1) * 128],
                                      xnv[:, dc, :], start=st, stop=sp),
                                  inc=("tt" if dc == FC - 1 and fc == FC - 1
                                       else None))
                    t_b = em.c["tt"]
                    em.wait("vector", "tt", t_b)
                    for hb in range(2):
                        em.op("vector", lambda e, dst_bf=dst_bf, hb=hb:
                              e.tensor_tensor(
                                  dst_bf[:, hb * 512:(hb + 1) * 512],
                                  satmp[:, hb * 512:(hb + 1) * 512],
                                  pbig[:, hb * 512:(hb + 1) * 512],
                                  op=ALU.mult), inc="vv")
                    return em.c["vv"]

                v_sw = mm_hidden(0, swb)
                em.wait("tensor", "vv", v_sw)
                swv = swb.rearrange("p (f t) -> p f t", f=FC)
                for half in range(2):
                    for hc in range(FC):
                        em.op("tensor", lambda e, half=half, hc=hc,
                              st=(hc == 0), sp=(hc == FC - 1): e.matmul(
                                  pbig[:, half * 512:(half + 1) * 512],
                                  swv[:, hc, :],
                                  wpool[:, 32768 + hc * 1024 + half * 512:
                                        32768 + hc * 1024 + (half + 1) * 512],
                                  start=st, stop=sp),
                              inc=("tt" if half == 1 and hc == FC - 1 else None))
                tso = em.c["tt"]
                em.wait("vector", "tt", tso)
                for hb in range(2):
                    em.op("vector", lambda e, lci=lci, hb=hb: e.tensor_copy(
                        accq[:, lci * D + hb * 512:lci * D + (hb + 1) * 512],
                        pbig[:, hb * 512:(hb + 1) * 512]), inc="vv")
                em.wait("tensor", "vv", em.c["vv"])
                mm_hidden(16384, hidq[:, lci * D:(lci + 1) * D])
                em.wait("tensor", "vv", em.c["vv"])

            for e_i in range(8):
                dbslot = 49152 if e_i % 2 == 0 else 57344
                em.wait("gpsimd", "tt", w2_slot_user[e_i % 2])
                em.dma("gpsimd", wpool[:, dbslot:dbslot + 8192], dw2[e_i], "dg")
                w2l = em.c["dg"]
                em.wait("tensor", "dg", w2l)
                for ci in grp:
                    lci = ci - g0
                    hv = hidq[:, lci * D:(lci + 1) * D].rearrange(
                        "p (f t) -> p f t", f=FC)
                    for half in range(2):
                        em.wait("tensor", "vv", evac_done[half])
                        for hc in range(FC):
                            em.op("tensor", lambda e, half=half, hc=hc,
                                  st=(hc == 0), sp=(hc == FC - 1), hv=hv,
                                  dbslot=dbslot: e.matmul(
                                      pbig[:, half * 512:(half + 1) * 512],
                                      hv[:, hc, :],
                                      wpool[:, dbslot + hc * 1024 + half * 512:
                                            dbslot + hc * 1024 + (half + 1) * 512],
                                      start=st, stop=sp),
                                  inc=("tt" if hc == FC - 1 else None))
                        thalf[half] = em.c["tt"]
                    for hb in range(2):
                        em.wait("vector", "tt", thalf[hb])
                        em.op("vector", lambda e, lci=lci, ci=ci, e_i=e_i, hb=hb:
                              e.scalar_tensor_tensor(
                                  accq[:, lci * D + hb * 512:lci * D + (hb + 1) * 512],
                                  pbig[:, hb * 512:(hb + 1) * 512],
                                  wdense[:, ci * 8 + e_i:ci * 8 + e_i + 1],
                                  accq[:, lci * D + hb * 512:lci * D + (hb + 1) * 512],
                                  op0=ALU.mult, op1=ALU.add), inc="vv")
                        evac_done[hb] = em.c["vv"]
                w2_slot_user[e_i % 2] = em.c["tt"]

            for ci in grp:
                lci = ci - g0
                em.wait("scalar", "vv")
                em.wait("scalar", "dx")
                em.op("scalar", lambda e, lci=lci: e.activation(
                    tmpa[:], accq[:, lci * D:(lci + 1) * D], AF.Silu), inc="aa")
                agf = em.c["aa"]
                em.wait("sync", "vv")
                em.dma("sync", tmpb[:], dscr_s[ci * TP:(ci + 1) * TP, :], "dx")
                sxl = em.c["dx"]
                em.wait("vector", "aa", agf)
                em.wait("vector", "dx", sxl)
                em.op("vector", lambda e: e.tensor_tensor(
                    tmpa[:], tmpa[:], tmpb[:], op=ALU.mult), inc="vv")
                vsg = em.c["vv"]
                em.wait("sync", "vv", vsg)
                em.dma("sync", tmpb[:], dx[ci * TP:(ci + 1) * TP, :], "dx")
                xxl = em.c["dx"]
                em.wait("vector", "dx", xxl)
                em.op("vector", lambda e: e.tensor_tensor(
                    tmpa[:], tmpa[:], tmpb[:], op=ALU.add), inc="vv")
                vout = em.c["vv"]
                em.wait("sync", "vv", vout)
                em.dma("sync", dout[ci * TP:(ci + 1) * TP, :], tmpa[:], "dx")

        em.wait("sync", "dx", em.c["dx"])
        em.wait("sync", "dg", em.c["dg"])

        def replay(eng_name):
            def run(handle):
                for item in em.q[eng_name]:
                    if item[0] == "w":
                        handle.wait_ge(sems[item[1]], item[2])
                    else:
                        instr = item[1](handle)
                        for s, n in item[2]:
                            instr.then_inc(sems[s], n)
            return run

        block.sync(replay("sync"))
        block.tensor(replay("tensor"))
        block.vector(replay("vector"))
        block.scalar(replay("scalar"))
        block.gpsimd(replay("gpsimd"))

    return nc


def _prep(inputs):
    f32 = np.float32
    fmap = np.array([(fp % 16) * 64 + fp // 16 for fp in range(D)])

    def lhsT_tiles(Wt):
        return np.ascontiguousarray(
            Wt.reshape(8, 128, Wt.shape[1]).transpose(1, 0, 2).reshape(128, -1))

    g = lambda k: np.asarray(inputs[k], f32)
    mats = []
    for nm, perm in [("Wr", True), ("Wk", True), ("Wg", True), ("Wt", True),
                     ("Wv", False), ("Wa", False), ("Wb", False)]:
        W = g("L0_attn_" + nm)
        if perm:
            W = W[fmap]
        mats.append(lhsT_tiles(np.ascontiguousarray(W.T)))
    wproj = np.concatenate(mats, axis=1)
    wo = lhsT_tiles(np.ascontiguousarray(g("L0_attn_Wo").T[fmap]))
    w_in = lhsT_tiles(g("L0_ffn_w_shared_in"))
    w1 = lhsT_tiles(g("L0_ffn_w1_shared"))
    w_out = lhsT_tiles(g("L0_ffn_w_shared_out"))
    wgate = lhsT_tiles(g("L0_ffn_w_gate"))
    wffn = np.concatenate([w_in, w1, w_out, wgate], axis=1)
    w2 = np.stack([lhsT_tiles(g("L0_ffn_w2")[e]) for e in range(8)])
    lnw_rep = np.ascontiguousarray(
        np.broadcast_to(g("L0_attn_ln_weight")[None, :], (128, D)))
    gb_rep = np.ascontiguousarray(
        np.broadcast_to(g("L0_ffn_gate_bias")[None, :], (128, 8)))
    cw = g("L0_attn_conv_w")[:, 0, :]
    cw_f = np.ascontiguousarray(
        cw.reshape(8, 128, 4).transpose(1, 0, 2).reshape(128, 32))
    bdp = np.ascontiguousarray(g("L0_attn_base_decay")[fmap].reshape(8, 128).T)
    sel = np.zeros((16, 128), f32)
    for p in range(128):
        sel[p % 16, p] = 1.0
    ident = np.eye(128, dtype=f32)
    half = np.zeros((128, 2), f32)
    half[0:64, 0] = 1.0
    half[64:128, 1] = 1.0
    return dict(wproj=wproj, wo=wo, wffn=wffn, w2=w2, lnw_rep=lnw_rep,
                gb_rep=gb_rep, cw_f=cw_f, bdp=bdp, sel=sel, ident=ident,
                halfones=half)


def kernel(**inputs):
    x = np.asarray(inputs["L0_attn_x"], np.float32)
    B, L, _ = x.shape
    shared = _prep(inputs)
    nc = build_nc(L)
    in_maps = []
    for b in range(B):
        m = dict(shared)
        m["x"] = np.ascontiguousarray(x[b])
        in_maps.append(m)
    res = run_bass_kernel_spmd(nc, in_maps, core_ids=list(range(B)))
    return np.stack([res.results[b]["out"] for b in range(B)], axis=0)
